# version 25
# baseline (speedup 1.0000x reference)
import numpy as np
import ml_dtypes
import concourse.bass as bass
import concourse.mybir as mybir
from concourse.bass_utils import run_bass_kernel_spmd
from concourse.alu_op_type import AluOpType
from concourse.aot_env import allow_ici_all_to_all
from contextlib import ExitStack

F32 = mybir.dt.float32
BF16 = mybir.dt.bfloat16
nbf = ml_dtypes.bfloat16
AF = mybir.ActivationFunctionType
Alu = AluOpType

NCORES = 8
DM = 1024
DFF = 4096
RANK = 256
MAGIC = 12582912.0
EXP_BIAS = -12.0
DEBUG_STAGE = 3
DEBUG_DUMP = False
DEBUG_DUMP2 = False


class Op:
    __slots__ = ("eng", "waits", "fn", "inc")

    def __init__(self, eng, waits, fn, inc):
        self.eng = eng
        self.waits = waits
        self.fn = fn
        self.inc = inc


SERIAL = False


class Builder:
    def __init__(self):
        self.ops = []
        self.cnt = {}

    def op(self, eng, waits, fn, inc=None, incn=1):
        if SERIAL:
            waits = list(waits) + [(s, v) for (s, v) in self.cnt.items()]
        w = [(s, v) for (s, v) in waits if v and v > 0]
        if inc is None:
            self.ops.append(Op(eng, w, fn, None))
            return None
        self.cnt[inc] = self.cnt.get(inc, 0) + incn
        self.ops.append(Op(eng, w, fn, (inc, incn)))
        return self.cnt[inc]

    def bump(self, name, n):
        self.cnt[name] = self.cnt.get(name, 0) + n
        return self.cnt[name]

    def get(self, s):
        return self.cnt.get(s, 0)


def _a2a(e, tin, tout):
    with allow_ici_all_to_all():
        return e.collective_compute(
            "AllToAll", mybir.AluOpType.bypass,
            replica_groups=[list(range(NCORES))],
            ins=[tin.ap().opt()], outs=[tout.ap().opt()])


def build_program(tpc):
    tb = 4 * tpc
    ntt = tpc // 128
    nqc = tb // 512
    nch1 = min(256, tpc)
    nch2 = min(256, tpc)
    nqk = min(512, tpc)
    assert tpc % 256 == 0 and tb % 512 == 0

    nc = bass.Bass("TRN2", target_bir_lowering=False)

    x_in = nc.dram_tensor("x_shard", [tpc, DM], F32, kind="ExternalInput").ap()
    w_attn = nc.dram_tensor("w_attn", [3 * DM, DM], F32, kind="ExternalInput").ap()
    w_proj = nc.dram_tensor("w_proj", [DM, DM], F32, kind="ExternalInput").ap()
    U1 = nc.dram_tensor("U1", [DM, RANK], F32, kind="ExternalInput").ap()
    V1 = nc.dram_tensor("V1", [RANK, DFF], F32, kind="ExternalInput").ap()
    b1 = nc.dram_tensor("b1", [DFF], F32, kind="ExternalInput").ap()
    U2 = nc.dram_tensor("U2", [DFF, RANK], F32, kind="ExternalInput").ap()
    V2 = nc.dram_tensor("V2", [RANK, DM], F32, kind="ExternalInput").ap()
    b2 = nc.dram_tensor("b2", [DM], F32, kind="ExternalInput").ap()
    tri_in = nc.dram_tensor("tri", [128, 128], BF16, kind="ExternalInput").ap()
    idf_in = nc.dram_tensor("identf", [128, 128], F32, kind="ExternalInput").ap()
    id_in = nc.dram_tensor("ident", [128, 128], BF16, kind="ExternalInput").ap()
    out = nc.dram_tensor("out_shard", [tpc, DM], F32, kind="ExternalOutput").ap()

    hp_ = tpc // 2
    a2a1_in = nc.dram_tensor("a2a1_in", [NCORES * 384, hp_], BF16)
    a2a1_out = nc.dram_tensor("a2a1_out", [NCORES * 384, hp_], BF16)
    a2a2_in = nc.dram_tensor("a2a2_in", [NCORES * 384, hp_], BF16)
    a2a2_out = nc.dram_tensor("a2a2_out", [NCORES * 384, hp_], BF16)
    a2ac_in = nc.dram_tensor("a2ac_in", [NCORES * 128, tpc], BF16)
    a2ac_out = nc.dram_tensor("a2ac_out", [NCORES * 128, tpc], BF16)

    B = Builder()
    SEMS = ["sX", "sW", "sScl", "sLnV", "sLnC", "sLnSt", "sLnAp", "sLnTr",
            "sLnEv", "sFp", "sFpe", "sFh", "sFg", "sFq2", "sFq2e", "sFo",
            "sFoV", "sQl", "sAbs", "sQs", "sQt", "sQv1", "sQg", "sQv3",
            "sQtr", "sQe", "sQK", "sQKe", "sQKd", "sQV", "sQVe", "sQVd",
            "sCC1", "sCC2", "sCCy", "sBl", "sBv", "sSc", "sEx", "sMk", "sAv",
            "sNmC", "sNmV", "sYn", "sYo", "sCy", "sFw2", "sPjT", "sPjV", "sOut",
            "sVa", "sVi", "sBc"]

    with ExitStack() as top:
        def sbuf(name, shape, dt, stack=None):
            return (stack or top).enter_context(nc.sbuf_tensor(name, shape, dt))

        X = sbuf("X", [128, ntt * DM], F32)
        xlntok = sbuf("xlntok", [128, 2 * DM], BF16)
        xdumpV = sbuf("xdumpV", [128, DM], BF16)
        xdumpC = sbuf("xdumpC", [128, DM], BF16)
        SUMX = sbuf("SUMX", [128, ntt], F32)
        SUMQ = sbuf("SUMQ", [128, ntt], F32)
        MEAN = sbuf("MEAN", [128, ntt], F32)
        MSQ = sbuf("MSQ", [128, ntt], F32)
        VAR = sbuf("VAR", [128, ntt], F32)
        SD = sbuf("SD", [128, ntt], F32)
        RST = sbuf("RST", [128, ntt], F32)
        tri = sbuf("tri_s", [128, 128], BF16)
        idb = sbuf("idb", [128, 128], BF16)
        onesb = sbuf("onesb", [128, 128], BF16)
        ones1 = sbuf("ones1", [1, 128], BF16)
        onesf = sbuf("onesf", [128, 128], F32)
        b1sb = sbuf("b1sb", [128, DFF // 128], F32)
        b1T = sbuf("b1T", [32, 128], F32)
        idf = sbuf("idf", [128, 128], F32)
        b2rowh = sbuf("b2rowh", [1, DM], BF16)
        scaleA = sbuf("scaleA", [128, 1], F32)
        iscaleA = sbuf("iscaleA", [128, 1], F32)
        scaleP = sbuf("scaleP", [128, 1], F32)
        iscaleP = sbuf("iscaleP", [128, 1], F32)
        PARTS = sbuf("PARTS", [128, 32], F32)
        PRED = sbuf("PRED", [128, 2], F32)
        epsb = sbuf("epsb", [128, 1], F32)
        wprojT = sbuf("wprojT", [128, 8192], BF16)
        nbias = sbuf("nbias", [128, 1], F32)
        RDBK = sbuf("RDBK", [128, 2], F32)

        pSS = top.enter_context(nc.psum_tensor("pSS", [128, 2048], F32))
        pb4 = top.enter_context(nc.psum_tensor("pb4", [128, 512], F32))
        pb5 = top.enter_context(nc.psum_tensor("pb5", [128, 512], F32))
        st_ptA = ExitStack()
        pt = [st_ptA.enter_context(nc.psum_tensor(f"ptA{i}", [128, 1024], BF16))
              for i in range(2)]
        PB = [pSS[:, 0:512], pSS[:, 512:1024], pSS[:, 1024:1536],
              pSS[:, 1536:2048], pb4[:, 0:512], pb5[:, 0:512]]

        S = {s: top.enter_context(nc.semaphore(s)) for s in SEMS}
        blk = top.enter_context(nc.Block())

        def x_sl(tt, c0=0, n=DM):
            return X[:, DM * tt + c0: DM * tt + c0 + n]

        def finish(stage_tag):
            B.op("g", [("sOut", B.get("sOut"))], lambda e: None)
            _emit(nc, blk, B, S)

        st_fma = ExitStack()
        FMA_A = sbuf("FMA_A", [128, 8 * tpc], BF16, st_fma)
        st_ffn1 = ExitStack()
        U1a = sbuf("U1a", [128, 8 * RANK], BF16, st_ffn1)
        V1a = sbuf("V1a", [128, 2 * DFF], BF16, st_ffn1)
        U2a = sbuf("U2a", [128, 32 * RANK], BF16, st_ffn1)
        V2a = sbuf("V2a", [128, 2 * DM], BF16, st_ffn1)
        sHa = sbuf("sHa", [128, 32 * nch1], BF16, st_ffn1)
        sPa = sbuf("sPa", [128, 2 * nch1], BF16, st_ffn1)
        sQ2a = sbuf("sQ2a", [128, 2 * nch1], BF16, st_ffn1)
        wv1 = sbuf("wv1", [128, DM], F32, st_ffn1)

        B.op("s", [], lambda e: e.dma_start(
            out=X[:, :].rearrange("p (tt c) -> p tt c", c=DM),
            in_=x_in.rearrange("(tt p) c -> p tt c", p=128)), "sX", 16)

        def wtile_src(j):
            return (w_attn[128 * j:128 * (j + 1), :] if j < 24
                    else w_proj[128 * (j - 24):128 * (j - 23), :])

        for j in range(32):
            w = [] if j < 1 else [("sAbs", j)]
            B.op("s", w, lambda e, j=j: e.dma_start(
                out=wv1[:, :], in_=wtile_src(j)), "sQl", 16)

        def ffnw_dmas(U1d, V1d, U2d, V2d):
            return [
                (U1d[:, :].rearrange("p (cc r) -> p cc r", r=RANK),
                 U1.rearrange("(cc p) r -> p cc r", p=128)),
                (V1d[:, :].rearrange("p (rt f) -> p rt f", f=DFF),
                 V1.rearrange("(rt p) f -> p rt f", p=128)),
                (U2d[:, :].rearrange("p (ft r) -> p ft r", r=RANK),
                 U2.rearrange("(ft p) r -> p ft r", p=128)),
                (V2d[:, :].rearrange("p (rt c) -> p rt c", c=DM),
                 V2.rearrange("(rt p) c -> p rt c", p=128)),
            ]

        for dst, src in [
            (b2rowh[:, :], b2.rearrange("(a c) -> a c", a=1)),
            (tri[:, :], tri_in),
            (idb[:, :], id_in),
            (idf[:, :], idf_in),
        ] + ffnw_dmas(U1a, V1a, U2a, V2a):
            B.op("g", [], lambda e, d=dst, s_=src: e.dma_start(out=d, in_=s_),
                 "sW", 16)
        B.op("g", [], lambda e: e.dma_start(
            out=b1T[:, :], in_=b1.rearrange("(ft p) -> ft p", p=128)), "sW", 16)
        B.op("g", [], lambda e: e.memset(onesb[:, :], 1.0), "sW", 1)
        B.op("g", [], lambda e: e.memset(ones1[:, :], 1.0), "sW", 1)
        B.op("g", [], lambda e: e.memset(onesf[:, :], 1.0), "sW", 1)
        B.op("g", [], lambda e: e.memset(epsb[:, :], 1e-5), "sW", 1)
        B.op("g", [], lambda e: e.memset(nbias[:, :], EXP_BIAS), "sW", 1)
        bt = B.op("t", [("sW", B.get("sW"))],
                  lambda e: e.matmul(pb4[:, 0:128], b1T[:, :], idf[0:32, :],
                                     start=True, stop=True), "sW", 1)
        B.op("v", [("sW", bt)],
             lambda e: e.tensor_copy(b1sb[:, :], pb4[:, 0:32]), "sW", 1)
        W_ALL = B.get("sW")

        B.op("v", [("sW", W_ALL)],
             lambda e: e.tensor_scalar_mul(V2a[:, :], V2a[:, :], 0.5))
        scl1 = B.op("v", [], lambda e: e.tensor_scalar_mul(
            b2rowh[:, :], b2rowh[:, :], 0.5), "sScl", 1)

        def emit_ln(gate_v, gate_c, fma_war, FMAd, tr_extra=()):
            r = {"ev": [], "ap": [], "tr": []}
            for tt in range(ntt):
                B.op("v", [("sX", 16)] + gate_v[tt],
                     lambda e, tt=tt: e.tensor_scalar(
                         xdumpV[:, :], x_sl(tt), 1.0, 0.0, Alu.mult, Alu.add,
                         accum_out=SUMX[:, tt:tt + 1]), "sLnV", 1)
            for tt in range(ntt):
                B.op("c", [("sX", 16)] + gate_c[tt],
                     lambda e, tt=tt: e.activation(
                         xdumpC[:, :], x_sl(tt), AF.Square,
                         accum_out=SUMQ[:, tt:tt + 1]), "sLnC", 1)
            vc = B.get("sLnC")
            va = B.get("sLnV")
            st0 = B.op("c", [("sLnV", va)],
                       lambda e: e.activation(MEAN[:, :], SUMX[:, :], AF.Copy,
                                              scale=1.0 / DM), "sLnSt", 1)
            B.op("v", [("sLnSt", st0)],
                 lambda e: e.tensor_tensor(MSQ[:, :], MEAN[:, :], MEAN[:, :], Alu.mult))
            B.op("v", [], lambda e: e.tensor_copy(xdumpV[:, :], xdumpV[:, :]))
            st1 = B.op("v", [("sLnC", vc)],
                       lambda e: e.scalar_tensor_tensor(
                           VAR[:, :], SUMQ[:, :], 1.0 / DM, MSQ[:, :],
                           Alu.mult, Alu.subtract), "sLnSt", 1)
            st2 = B.op("c", [("sLnSt", st1)],
                       lambda e: e.activation(SD[:, :], VAR[:, :], AF.Sqrt, bias=epsb[:, 0:1]),
                       "sLnSt", 1)
            B.op("v", [("sLnSt", st2)],
                 lambda e: e.reciprocal(RST[:, :], SD[:, :]))
            B.op("v", [], lambda e: e.tensor_copy(xdumpV[:, :], xdumpV[:, :]))
            for tt in range(ntt):
                par = tt % 2
                w = [("sLnTr", r["tr"][tt - 2])] if tt >= 2 else []
                r["ap"].append(B.op("v", w, lambda e, tt=tt, par=par: e.tensor_scalar(
                    xlntok[:, DM * par: DM * par + DM], x_sl(tt),
                    MEAN[:, tt:tt + 1], RST[:, tt:tt + 1],
                    Alu.subtract, Alu.mult), "sLnAp", 1))

                def tr_fn(e, tt=tt, par=par):
                    for cc in range(8):
                        ins = e.transpose(
                            pt[par][:, 128 * cc:128 * (cc + 1)],
                            xlntok[:, DM * par + 128 * cc: DM * par + 128 * (cc + 1)],
                            idb[:, :])
                    return ins
                w = [("sLnAp", r["ap"][tt]), ("sW", W_ALL)]
                if tt < 2:
                    w += list(tr_extra)
                else:
                    w.append(("sLnEv", r["ev"][tt - 2]))
                r["tr"].append(B.op("t", w, tr_fn, "sLnTr", 1))

                def ev_fn(e, tt=tt, par=par, FMAd=FMAd):
                    dst = FMAd[:, :].rearrange("p (cc t) -> p cc t", t=tpc)[
                        :, :, 128 * tt:128 * (tt + 1)]
                    src = pt[par][:, :].rearrange("p (cc r) -> p cc r", r=128)
                    return e.tensor_copy(dst, src)
                w = [("sLnTr", r["tr"][tt])] + (fma_war if tt == 0 else [])
                r["ev"].append(B.op("v", w, ev_fn, "sLnEv", 1))
            return r

        def emit_ffn(nch, ln, bufs, wwaits, is_last, extra_c=None):
            U1w, V1w, U2w, V2w, sPb, sHb, sQ2b, FMAd = bufs
            nchunks = tpc // nch
            ntl = nch // 128
            r = {"ov_tile": [None] * ntt}
            prev_q2e = [0, 0]
            prev_g = {}
            prev_ov = {}
            prev_o_last = 0
            prev_h_last = 0
            for n in range(nchunks):
                t0 = n * nch
                p_rec, h_rec, q2_rec, o_rec, g_list = [], [], [], [], []
                for rt in range(2):
                    w = [("sLnEv", ln["ev"][(t0 + nch) // 128 - 1]),
                         ("sFq2e", prev_q2e[rt])] + wwaits

                    def p_fn(e, rt=rt, t0=t0, U1w=U1w, FMAd=FMAd):
                        for cc in range(8):
                            ins = e.matmul(
                                PB[rt][:, 0:nch],
                                U1w[:, RANK * cc + 128 * rt: RANK * cc + 128 * (rt + 1)],
                                FMAd[:, tpc * cc + t0: tpc * cc + t0 + nch],
                                start=(cc == 0), stop=(cc == 7))
                        return ins
                    p_rec.append(B.op("t", w, p_fn, "sFp", 1))
                pe_rec = []
                for rt in range(2):
                    w = [("sFp", p_rec[rt]), ("sFh", prev_h_last)]
                    pe_rec.append(B.op("c", w, lambda e, rt=rt, sPb=sPb:
                                       e.activation(
                                           sPb[:, nch * rt: nch * rt + nch],
                                           PB[rt][:, 0:nch], AF.Copy), "sFpe", 1))
                for ft in range(32):
                    par = 2 + (ft % 2)
                    w = [("sFpe", pe_rec[1])]
                    if par in prev_g:
                        w.append(("sFg", prev_g[par]))

                    def h_fn(e, ft=ft, par=par, V1w=V1w, sPb=sPb):
                        for rt in range(2):
                            ins = e.matmul(
                                PB[par][:, 0:nch],
                                V1w[:, DFF * rt + 128 * ft: DFF * rt + 128 * (ft + 1)],
                                sPb[:, nch * rt: nch * rt + nch],
                                start=(rt == 0), stop=(rt == 1))
                        return ins
                    h_rec.append(B.op("t", w, h_fn, "sFh", 1))
                    g = B.op("c", [("sFh", h_rec[ft])],
                             lambda e, ft=ft, par=par, sHb=sHb: e.activation(
                                 sHb[:, nch * ft: nch * ft + nch], PB[par][:, 0:nch],
                                 AF.Gelu_apprx_tanh, bias=b1sb[:, ft:ft + 1]),
                             "sFg", 1)
                    prev_g[par] = g
                    g_list.append(g)
                for rt in range(2):
                    def q2_fn(e, rt=rt, g_list=tuple(g_list), U2w=U2w, sHb=sHb):
                        for ft in range(32):
                            e.wait_ge(S["sFg"], g_list[ft])
                            ins = e.matmul(
                                PB[rt][:, 0:nch],
                                U2w[:, RANK * ft + 128 * rt: RANK * ft + 128 * (rt + 1)],
                                sHb[:, nch * ft: nch * ft + nch],
                                start=(ft == 0), stop=(ft == 31))
                        return ins
                    w = [("sFpe", pe_rec[rt])]
                    q2_rec.append(B.op("t", w, q2_fn, "sFq2", 1))
                q2e_rec = []
                for rt in range(2):
                    w = [("sFq2", q2_rec[rt]), ("sFo", prev_o_last)]
                    q2e_rec.append(B.op("v", w, lambda e, rt=rt, sPb=sPb, sQ2b=sQ2b:
                                        e.tensor_copy(
                                            sQ2b[:, nch * rt: nch * rt + nch],
                                            PB[rt][:, 0:nch]), "sFq2e", 1))
                    prev_q2e[rt] = q2e_rec[rt]
                for tl in range(ntl):
                    for half in range(2):
                        idx = tl * 2 + half
                        par = 4 + (idx % 2)
                        gtt = t0 // 128 + tl
                        w = [("sFq2e", q2e_rec[1])] + wwaits
                        if par in prev_ov:
                            w.append(("sFoV", prev_ov[par]))

                        def o_fn(e, tl=tl, half=half, par=par, sQ2b=sQ2b, V2w=V2w):
                            for rt in range(2):
                                e.matmul(
                                    PB[par][:, 0:512],
                                    sQ2b[:, nch * rt + 128 * tl: nch * rt + 128 * (tl + 1)],
                                    V2w[:, DM * rt + 512 * half: DM * rt + 512 * (half + 1)],
                                    start=(rt == 0), stop=False)
                            return e.matmul(
                                PB[par][:, 0:512], ones1[0:1, 0:128],
                                b2rowh[0:1, 512 * half: 512 * (half + 1)],
                                start=False, stop=True)
                        o = B.op("t", w, o_fn, "sFo", 1)
                        o_rec.append(o)
                        prev_o_last = o
                        ov = B.op("v", [("sFo", o)],
                                  lambda e, gtt=gtt, half=half, par=par:
                                  e.scalar_tensor_tensor(
                                      x_sl(gtt, 512 * half, 512), PB[par][:, 0:512],
                                      1.0, x_sl(gtt, 512 * half, 512),
                                      Alu.mult, Alu.add), "sFoV", 1)
                        prev_ov[par] = ov
                        if half == 1:
                            r["ov_tile"][gtt] = ov
                            if is_last:
                                B.op("s", [("sFoV", ov)],
                                     lambda e, gtt=gtt: e.dma_start(
                                         out=out[128 * gtt:128 * (gtt + 1), :],
                                         in_=x_sl(gtt)), "sOut", 16)
                prev_h_last = h_rec[-1]
                if extra_c:
                    take = min(4, len(extra_c))
                    for fn_ in extra_c[:take]:
                        fn_()
                    del extra_c[:take]
            return r

        abs_rec = []

        def _mk_abs(j):
            def emit():
                abs_rec.append(B.op("c", [("sQl", 16 * (j + 1))],
                                    lambda e, j=j: e.activation(
                                        xdumpC[:, :], wv1[:, :], AF.Abs,
                                        accum_out=PARTS[:, j:j + 1]), "sAbs", 1))
            return emit
        abs_work = [_mk_abs(j) for j in range(32)]

        ln1 = emit_ln([[]] * ntt, [[]] * ntt, [], FMA_A)
        if DEBUG_DUMP:
            d_fma = nc.dram_tensor("d_fma", [128, 8 * tpc], BF16,
                                   kind="ExternalOutput").ap()
            d_st = nc.dram_tensor("d_st", [128, 8 * ntt], F32,
                                  kind="ExternalOutput").ap()
            B.op("s", [("sLnEv", B.get("sLnEv"))],
                 lambda e: e.dma_start(out=d_fma, in_=FMA_A[:, :]), "sOut", 16)

            def st_fn(e):
                for k, tns in enumerate([SUMX, SUMQ, MEAN, MSQ, VAR, SD, RST]):
                    e.dma_start(out=d_st[:, ntt * k: ntt * (k + 1)],
                                in_=tns[:, :]).then_inc(S["sOut"], 16)
                return None
            B.op("s", [("sLnAp", B.get("sLnAp"))], st_fn)
            B.bump("sOut", 16 * 7)
        bufs1 = (U1a, V1a, U2a, V2a, sPa, sHa, sQ2a, FMA_A)
        ffn1 = emit_ffn(nch1, ln1, bufs1, [("sW", W_ALL), ("sScl", scl1)],
                        is_last=(DEBUG_STAGE == 1), extra_c=abs_work)
        if DEBUG_DUMP:
            d_p = nc.dram_tensor("d_p", [128, 2 * nch1], BF16,
                                 kind="ExternalOutput").ap()
            d_h = nc.dram_tensor("d_h", [128, 32 * nch1], BF16,
                                 kind="ExternalOutput").ap()
            d_q2 = nc.dram_tensor("d_q2", [128, 2 * nch1], BF16,
                                  kind="ExternalOutput").ap()
            B.op("s", [("sFpe", B.get("sFpe"))],
                 lambda e: e.dma_start(out=d_p, in_=sPa[:, :]), "sOut", 16)
            B.op("s", [("sFg", B.get("sFg"))],
                 lambda e: e.dma_start(out=d_h, in_=sHa[:, :]), "sOut", 16)
            B.op("s", [("sFq2e", B.get("sFq2e"))],
                 lambda e: e.dma_start(out=d_q2, in_=sQ2a[:, :]), "sOut", 16)

        if DEBUG_STAGE == 1:
            finish("s1")
            st_ffn1.close()
            st_fma.close()
            return nc

        FFN1_OV = B.get("sFoV")
        FFN1_Q2E = B.get("sFq2e")
        FFN1_G = B.get("sFg")
        FFN1_FP = B.get("sFp")

        gate = [[("sFoV", ffn1["ov_tile"][tt])] for tt in range(ntt)]
        ln2 = emit_ln(gate, gate, [("sFp", FFN1_FP)], FMA_A)
        LN2_EV_ = B.get("sLnEv")

        _emit(nc, blk, B, S)
        st_ffn1.close()
        st_quant = ExitStack()
        wave = sbuf("wave", [128, 2 * DM], F32, st_quant)
        Q32 = sbuf("Q32", [128, 2 * DM], F32, st_quant)
        WQ = sbuf("WQ", [128, 2 * DM], BF16, st_quant)
        wattnT = sbuf("wattnT", [128, 24576], BF16, st_quant)

        for fn_ in abs_work:
            fn_()
        abs_work.clear()
        B.op("v", [("sAbs", abs_rec[-1])], lambda e: e.tensor_reduce(
            PRED[:, 0:1], PARTS[:, 0:24], mybir.AxisListType.X, Alu.add))
        qs1 = B.op("v", [], lambda e: e.tensor_reduce(
            PRED[:, 1:2], PARTS[:, 24:32], mybir.AxisListType.X, Alu.add), "sQs", 1)
        qt1 = B.op("t", [("sQs", qs1), ("sW", W_ALL), ("sFq2e", FFN1_Q2E),
                         ("sFoV", FFN1_OV)],
                   lambda e: (e.matmul(pSS[:, 0:1], onesf[:, :], PRED[:, 0:1],
                                       start=True, stop=True),
                              e.matmul(pSS[:, 512:513], onesf[:, :], PRED[:, 1:2],
                                       start=True, stop=True))[-1], "sQt", 1)
        B.op("v", [("sQt", qt1)], lambda e: e.tensor_scalar(
            scaleA[:, :], pSS[:, 0:1], 1.0 / (3072 * 1024), 1e-8, Alu.mult, Alu.max))
        B.op("v", [], lambda e: e.tensor_scalar(
            scaleP[:, :], pSS[:, 512:513], 1.0 / (1024 * 1024), 1e-8, Alu.mult, Alu.max))
        B.op("v", [], lambda e: e.tensor_copy(xdumpV[:, :], xdumpV[:, :]))
        B.op("v", [], lambda e: e.reciprocal(iscaleA[:, :], scaleA[:, :]))
        B.op("v", [], lambda e: e.tensor_copy(xdumpV[:, :], xdumpV[:, :]))
        qs2 = B.op("v", [], lambda e: e.reciprocal(iscaleP[:, :], scaleP[:, :]),
                   "sQs", 1)

        c1_rec, c3_rec, tr_rec, ev_rec = [], [], [], []

        def q_c1(j):
            par = j % 2
            isc = iscaleA if j < 24 else iscaleP
            w = [("sQl", 16 * (32 + j // 2 + 1)), ("sQs", qs2)]
            c1_rec.append(B.op("v", w, lambda e, par=par, isc=isc:
                               e.tensor_scalar(
                                   Q32[:, DM * par: DM * (par + 1)],
                                   wave[:, DM * par: DM * (par + 1)],
                                   isc[:, :], 2.25, Alu.mult, Alu.min), "sQv1", 1))

        def q_tail(j):
            par = j % 2
            sc_ = scaleA if j < 24 else scaleP
            B.op("v", [], lambda e, par=par: e.tensor_scalar(
                Q32[:, DM * par: DM * (par + 1)],
                Q32[:, DM * par: DM * (par + 1)],
                -2.25, MAGIC, Alu.max, Alu.add))
            B.op("v", [], lambda e: e.tensor_copy(xdumpV[:, :], xdumpV[:, :]))
            w = [("sQtr", tr_rec[j - 2])] if j >= 2 else []
            c3_rec.append(B.op("v", w, lambda e, par=par, sc_=sc_: e.tensor_scalar(
                WQ[:, DM * par: DM * (par + 1)],
                Q32[:, DM * par: DM * (par + 1)],
                -MAGIC, sc_[:, :], Alu.add, Alu.mult), "sQv3", 1))

            def tr_fn(e, par=par):
                for cc in range(8):
                    ins = e.transpose(
                        pt[par][:, 128 * cc:128 * (cc + 1)],
                        WQ[:, DM * par + 128 * cc: DM * par + 128 * (cc + 1)],
                        idb[:, :])
                return ins
            w = [("sQv3", c3_rec[j]), ("sLnEv", LN2_EV_)]
            if j >= 2:
                w.append(("sQe", ev_rec[j - 2]))
            tr_rec.append(B.op("t", w, tr_fn, "sQtr", 1))

            def ev_fn(e, j=j, par=par):
                if j < 24:
                    dst = wattnT[:, :].rearrange("p (cc o) -> p cc o", o=3072)[
                        :, :, 128 * j:128 * (j + 1)]
                else:
                    dst = wprojT[:, :].rearrange("p (cc o) -> p cc o", o=1024)[
                        :, :, 128 * (j - 24):128 * (j - 23)]
                src_ = pt[par][:, :].rearrange("p (cc r) -> p cc r", r=128)
                return e.activation(dst, src_, AF.Copy)
            ev_rec.append(B.op("c", [("sQtr", tr_rec[j])], ev_fn, "sQe", 1))

        def wpair_src(p):
            j = 2 * p
            if j < 24 and j + 1 < 24:
                return w_attn[128 * j:128 * (j + 2), :]
            return w_proj[128 * (j - 24):128 * (j - 22), :]

        for p in range(16):
            w = [("sQs", qs2)] if p == 0 else [("sQv1", c1_rec[2 * p - 1])]
            B.op("s" if p % 2 == 0 else "g", w, lambda e, p=p: e.dma_start(
                out=wave[:, :].rearrange("q (j c) -> q j c", c=DM),
                in_=wpair_src(p).rearrange("(j q) c -> q j c", q=128)),
                "sQl", 16)
            if p >= 1:
                q_tail(2 * p - 2)
            q_c1(2 * p)
            if p >= 1:
                q_tail(2 * p - 1)
            q_c1(2 * p + 1)
        q_tail(30)
        q_tail(31)
        QE_ALL = B.get("sQe")

        if DEBUG_DUMP2:
            d_wat = nc.dram_tensor("d_wat", [128, 24576], BF16,
                                   kind="ExternalOutput").ap()
            d_wpt = nc.dram_tensor("d_wpt", [128, 8192], BF16,
                                   kind="ExternalOutput").ap()
            B.op("s", [("sQe", B.get("sQe"))],
                 lambda e: e.dma_start(out=d_wat, in_=wattnT[:, :]), "sOut", 16)
            B.op("s", [("sQe", B.get("sQe"))],
                 lambda e: e.dma_start(out=d_wpt, in_=wprojT[:, :]), "sOut", 16)

        st_qkv = ExitStack()
        qkstage = sbuf("qkstage", [128, 2 * tpc], BF16, st_qkv)
        vstage = sbuf("vstage", [128, 2 * DM], BF16, st_qkv)
        hp = tpc // 2
        nqk_h = min(512, hp)
        nchq_h = hp // nqk_h
        war_ffn1 = [("sFq2e", FFN1_Q2E), ("sFg", FFN1_G), ("sFoV", FFN1_OV)]
        LN2_EV = LN2_EV_
        LN2EVL = ln2["ev"]

        def vflat_in(a2a):
            return (a2a.rearrange("(j s) c -> j s c", j=NCORES)[:, 256:384, :]
                    .rearrange("j s c -> j (s c)")
                    .rearrange("j (t d) -> t j d", d=128))

        qv_rec, qve_rec, qvd_rec = [], [], []
        qk_rec, qke_rec = [], []
        qkd_rec = {}
        iv = [0]
        iq = [0]

        def emit_v_tile(tl, half):
            for hf2 in range(2):
                i = iv[0]
                par = 2 + (i % 2)
                w = [("sQe", ev_rec[19 + 4 * hf2]), ("sLnEv", LN2EVL[tl])] \
                    + war_ffn1
                if i >= 2:
                    w.append(("sQVe", qve_rec[i - 2]))

                def qv_fn(e, tl=tl, hf2=hf2, par=par):
                    for cc in range(8):
                        ins = e.matmul(
                            PB[par][:, 0:512],
                            FMA_A[:, tpc * cc + 128 * tl: tpc * cc + 128 * (tl + 1)],
                            wattnT[:, 3072 * cc + 2048 + 512 * hf2:
                                   3072 * cc + 2048 + 512 * (hf2 + 1)],
                            start=(cc == 0), stop=(cc == 7))
                    return ins
                qv_rec.append(B.op("t", w, qv_fn, "sQV", 1))
                w = [("sQV", qv_rec[i])]
                if tl >= 2 and hf2 == 0:
                    w.append(("sQVd", 16 * (tl - 1)))
                qve_rec.append(B.op("c", w, lambda e, tl=tl, hf2=hf2, par=par:
                                    e.activation(
                                        vstage[:, DM * (tl % 2) + 512 * hf2:
                                               DM * (tl % 2) + 512 * (hf2 + 1)],
                                        PB[par][:, 0:512], AF.Copy), "sQVe", 1))
                iv[0] += 1
            rb = B.op("v", [("sQVe", qve_rec[-1])],
                      lambda e, tl=tl: e.tensor_reduce(
                          RDBK[:, 1:2], vstage[:, DM * (tl % 2): DM * (tl % 2 + 1)],
                          mybir.AxisListType.X, Alu.add), "sQVe", 1)
            w = [("sQVe", rb)]
            if half == 1:
                w.append(("sCC1", 1))
            a2a = a2a1_in if half == 0 else a2a2_in
            tl2 = tl % (ntt // 2)

            def vd_fn(e, a2a=a2a, tl=tl, tl2=tl2):
                dst = vflat_in(a2a)[128 * tl2:128 * (tl2 + 1), :, :]
                return e.dma_start(
                    out=dst,
                    in_=vstage[:, DM * (tl % 2): DM * (tl % 2 + 1)].rearrange(
                        "p (j d) -> p j d", d=128))
            qvd_rec.append(B.op("s", w, vd_fn, "sQVd", 16))

        def emit_qk_tile(ot, half):
            for nq in range(nchq_h):
                i = iq[0]
                par = i % 2
                c0 = hp * half + nqk_h * nq
                w = [("sQe", ev_rec[ot]),
                     ("sLnEv", LN2EVL[(c0 + nqk_h) // 128 - 1])] + war_ffn1
                if i >= 2:
                    w.append(("sQKe", qke_rec[i - 2]))

                def qk_fn(e, ot=ot, c0=c0, par=par):
                    for cc in range(8):
                        ins = e.matmul(
                            PB[par][:, 0:nqk_h],
                            wattnT[:, 3072 * cc + 128 * ot: 3072 * cc + 128 * (ot + 1)],
                            FMA_A[:, tpc * cc + c0: tpc * cc + c0 + nqk_h],
                            start=(cc == 0), stop=(cc == 7))
                    return ins
                qk_rec.append(B.op("t", w, qk_fn, "sQK", 1))
                w = [("sQK", qk_rec[i])]
                if (ot - 2, half) in qkd_rec and nq == 0:
                    w.append(("sQKd", qkd_rec[(ot - 2, half)]))
                elif half == 1 and ot < 2 and nq == 0 and (14 + ot, 0) in qkd_rec:
                    w.append(("sQKd", qkd_rec[(14 + ot, 0)]))

                def qke_fn(e, ot=ot, half=half, nq=nq, par=par):
                    sl = tpc * (ot % 2) + hp * half + nqk_h * nq
                    return e.activation(qkstage[:, sl: sl + nqk_h],
                                        PB[par][:, 0:nqk_h], AF.Copy)
                qke_rec.append(B.op("c", w, qke_fn, "sQKe", 1))
                iq[0] += 1
            rb = B.op("v", [("sQKe", qke_rec[-1])],
                      lambda e, ot=ot, half=half: e.tensor_reduce(
                          RDBK[:, 0:1],
                          qkstage[:, tpc * (ot % 2) + hp * half:
                                  tpc * (ot % 2) + hp * half + hp],
                          mybir.AxisListType.X, Alu.add), "sQKe", 1)
            w = [("sQKe", rb)]
            if half == 1:
                w.append(("sCC1", 1))
            a2a = a2a1_in if half == 0 else a2a2_in
            j = ot % 8
            r0 = 384 * j + 128 * (ot // 8)
            qkd_rec[(ot, half)] = B.op(
                "s", w, lambda e, a2a=a2a, r0=r0, ot=ot, half=half: e.dma_start(
                    out=a2a[r0:r0 + 128, :],
                    in_=qkstage[:, tpc * (ot % 2) + hp * half:
                                tpc * (ot % 2) + hp * half + hp]),
                "sQKd", 16)

        for ot in range(16):
            emit_qk_tile(ot, 0)
        for tl in range(ntt // 2):
            emit_v_tile(tl, 0)
        B.op("g", [("sQKd", B.get("sQKd")), ("sQVd", B.get("sQVd"))],
             lambda e: _a2a(e, a2a1_in, a2a1_out), "sCC1", 1)
        for ot in range(16):
            emit_qk_tile(ot, 1)
        for tl in range(ntt // 2, ntt):
            emit_v_tile(tl, 1)
        B.op("g", [("sQKd", B.get("sQKd")), ("sQVd", B.get("sQVd")),
                   ("sCC1", 1)],
             lambda e: _a2a(e, a2a2_in, a2a2_out), "sCC2", 1)
        QKE_ALL = B.get("sQKe")
        QVE_ALL = B.get("sQVe")

        _emit(nc, blk, B, S)
        st_ptA.close()
        st_qkv.close()
        st_quant.close()
        st_fma.close()
        st_att = ExitStack()
        nkbh = tb // 128
        sQKV = sbuf("sQKV", [128, 3 * tb], BF16, st_att)
        sVaug = sbuf("sVaug", [128, 130 * nkbh], BF16, st_att)
        sPP = sbuf("sPP", [128, 4096], BF16, st_att)
        rdsb = sbuf("rdsb", [128, 1024], BF16, st_att)
        rnsb = sbuf("rnsb", [128, 1024], F32, st_att)
        SEL = sbuf("SEL", [128, 128], BF16, st_att)
        yh0 = sbuf("yh0", [128, 1024], BF16, st_att)
        yh1 = sbuf("yh1", [128, 1024], BF16, st_att)
        pnorm = st_att.enter_context(nc.psum_tensor("pnorm", [128, 1024], F32))
        sQ = sQKV[:, 0:tb]
        sK = sQKV[:, tb:2 * tb]
        sV = sQKV[:, 2 * tb:3 * tb]

        B.op("g", [("sCC1", 1)], lambda e: e.memset(sVaug[:, :], 1.0), "sVi", 1)
        B.op("g", [], lambda e: e.memset(SEL[:, :], 0.0), "sVi", 1)
        B.op("g", [], lambda e: e.memset(SEL[64:65, :], 1.0), "sVi", 1)
        B.op("g", [], lambda e: e.memset(rdsb[:, :], 0.0), "sVi", 1)
        VI_ALL = B.get("sVi")

        def emit_qkload(u, half, waits):
            first = True
            for Bq in range(4):
                Bg = Bq + 4 * half
                gout = a2a1_out if half == 0 else a2a2_out
                r0 = 384 * (4 * u + Bq)
                w = waits if first else []
                first = False
                B.op("c", w, lambda e, gout=gout, r0=r0, Bg=Bg: e.dma_start(
                    out=sQ[:, hp * Bg: hp * (Bg + 1)],
                    in_=gout[r0:r0 + 128, :]), "sBl", 16)
                B.op("c", [], lambda e, gout=gout, r0=r0, Bg=Bg: e.dma_start(
                    out=sK[:, hp * Bg: hp * (Bg + 1)],
                    in_=gout[r0 + 128:r0 + 256, :]), "sBl", 16)
            return B.get("sBl")

        def emit_vload(u, half, waits):
            engs = ["g", "s", "c", "g"]
            for Bq in range(4):
                Bg = Bq + 4 * half
                gout = a2a1_out if half == 0 else a2a2_out
                r0 = 384 * (4 * u + Bq) + 256

                def fn(e, gout=gout, r0=r0, Bg=Bg):
                    src = (gout[r0:r0 + 128, :]
                           .rearrange("r c -> (r c)")
                           .rearrange("(kb q d) -> q kb d", q=128, d=128))
                    return e.dma_start(
                        out=sV[:, hp * Bg: hp * (Bg + 1)].rearrange(
                            "q (kb d) -> q kb d", d=128),
                        in_=src)
                B.op(engs[Bq], list(waits), fn, "sBv", 16)
            return B.get("sBv")

        nkb_half = nkbh // 2

        def emit_vaug(u, half, bv_snap, av_snap):
            kb0 = nkb_half * half
            c0 = 4 * hp * half

            def src3(e_cols):
                return sV[:, c0:c0 + 4 * hp].rearrange(
                    "p (kb c) -> p kb c", c=128)[:, :, e_cols[0]:e_cols[1]]
            w = [("sBv", bv_snap), ("sVi", VI_ALL)]
            if av_snap:
                w.append(("sAv", av_snap))
            B.op("v", w, lambda e, kb0=kb0: e.tensor_copy(
                sVaug[:, :].rearrange("p (kb c) -> p kb c", c=130)[
                    :, kb0:kb0 + nkb_half, 0:64], src3((0, 64))))
            return B.op("v", [], lambda e, kb0=kb0: e.tensor_copy(
                sVaug[:, :].rearrange("p (kb c) -> p kb c", c=130)[
                    :, kb0:kb0 + nkb_half, 65:129], src3((64, 128))), "sVa", 1)

        ex_rec = {}
        g = 0
        yn_last = 0
        yo_rec = {}
        pend_mm = []
        av_pend = []
        for u in range(2):
            u0_sc = B.get("sSc")
            u0_av = g
            u0_ex = B.get("sEx")
            u0_va = B.get("sVa")
            sbl = {0: emit_qkload(u, 0, [("sCC1", 1), ("sSc", u0_sc),
                                         ("sEx", u0_ex)])}
            sbv = {0: emit_vload(u, 0, [("sCC1", 1), ("sVa", u0_va)])}
            va = {0: emit_vaug(u, 0, sbv[0], u0_av)}
            for qc in range(nqc):
                if qc == nqc // 2:
                    sbl[1] = emit_qkload(u, 1, [("sCC2", 1)])
                    sbv[1] = emit_vload(u, 1, [("sCC2", 1)])
                    va[1] = emit_vaug(u, 1, sbv[1], u0_av)
                hh = 0 if qc < nqc // 2 else 1
                nkb = 4 * qc + 4
                for kb in range(nkb):
                    diag = kb >= 4 * qc
                    qlo = 128 * (kb - 4 * qc) if diag else 0
                    par = g % 2
                    slot = g % 4
                    q0 = 512 * qc + qlo
                    n = 512 - qlo
                    w = [("sBl", sbl[hh])]
                    if g >= 2:
                        w.append(("sEx", ex_rec[g - 2]))
                    if g == 0:
                        w += [("sQKe", QKE_ALL), ("sQVe", QVE_ALL)]

                    def sc_fn(e, kb=kb, par=par, qlo=qlo, q0=q0, n=n):
                        e.matmul(pSS[:, 1024 * par + qlo: 1024 * par + 512],
                                 sK[0:64, 128 * kb:128 * (kb + 1)],
                                 sQ[0:64, q0:q0 + n],
                                 start=True, stop=True, tile_position=(0, 0))
                        return e.matmul(pSS[:, 1024 * par + 512 + qlo:
                                            1024 * par + 1024],
                                        sK[64:128, 128 * kb:128 * (kb + 1)],
                                        sQ[64:128, q0:q0 + n],
                                        start=True, stop=True,
                                        tile_position=(64, 0))
                    sc = B.op("t", w, sc_fn, "sSc", 1)
                    uq = u * nqc + qc
                    if pend_mm and kb == 1:
                        while av_pend and av_pend[0][0] < uq:
                            av_pend.pop(0)[1]()
                        for fn_ in pend_mm:
                            fn_()
                        pend_mm = []
                    while len(av_pend) >= 2:
                        av_pend.pop(0)[1]()
                    w = [("sSc", sc)]
                    if g >= 4:
                        w.append(("sAv", g - 3))
                    ex_rec[g] = B.op("c", w, lambda e, par=par, slot=slot:
                                     e.activation(
                                         sPP[:, 1024 * slot:1024 * (slot + 1)],
                                         pSS[:, 1024 * par:1024 * (par + 1)],
                                         AF.Exp, scale=0.125,
                                         bias=nbias[:, 0:1]), "sEx", 1)
                    mk = None
                    if diag:
                        def mk_fn(e, slot=slot, qlo=qlo):
                            e.tensor_tensor(
                                sPP[:, 1024 * slot + qlo:
                                    1024 * slot + qlo + 128],
                                sPP[:, 1024 * slot + qlo:
                                    1024 * slot + qlo + 128],
                                tri[:, :], Alu.mult)
                            return e.tensor_tensor(
                                sPP[:, 1024 * slot + 512 + qlo:
                                    1024 * slot + 512 + qlo + 128],
                                sPP[:, 1024 * slot + 512 + qlo:
                                    1024 * slot + 512 + qlo + 128],
                                tri[:, :], Alu.mult)
                        mk = B.op("v", [("sEx", ex_rec[g])], mk_fn, "sMk", 1)

                    def emit_av(g=g, kb=kb, slot=slot, qlo=qlo, mk=mk,
                                diag=diag, nkb=nkb, va_snap=va[hh], u=u, qc=qc,
                                yn_snap=yn_last):
                        w = [("sMk", mk)] if diag else [("sEx", ex_rec[g])]
                        w.append(("sVa", va_snap))
                        if kb == 0:
                            w.append(("sYn", yn_snap))
                            if u == 0 and qc == 0:
                                w.append(("sFoV", FFN1_OV))

                        def fn(e, kb=kb, slot=slot, qlo=qlo,
                               st=(kb == 0), sp=(kb == nkb - 1)):
                            p1 = sPP[:, 1024 * slot + qlo: 1024 * slot + 512]
                            p2 = sPP[:, 1024 * slot + 512 + qlo:
                                     1024 * slot + 1024]
                            e.matmul(pb4[0:65, qlo:512],
                                     sVaug[:, 130 * kb:130 * kb + 65], p1,
                                     start=st, stop=sp, tile_position=(0, 0))
                            return e.matmul(pb5[0:65, qlo:512],
                                            sVaug[:, 130 * kb + 65:
                                                  130 * (kb + 1)], p2,
                                            start=st, stop=sp,
                                            tile_position=(0, 0))
                        B.op("t", w, fn, "sAv", 1)
                    av_pend.append((u * nqc + qc, emit_av))
                    g += 1
                qpar = qc % 2

                def rc_fn(e, src, dst_c0):
                    with nc.allow_low_precision(reason="recip as bf16 matmul operand"):
                        return e.reciprocal(rdsb[64:65, dst_c0:dst_c0 + 512], src)
                B.op("v", [("sAv", g), ("sVi", VI_ALL)],
                     lambda e: rc_fn(e, pb4[64:65, 0:512], 0))
                nmv = B.op("v", [], lambda e: rc_fn(
                    e, pb5[64:65, 0:512], 512), "sNmV", 1)
                bc_target = B.get("sBc") + 2

                def mk_mm(nmv=nmv):
                    B.op("t", [("sNmV", nmv), ("sVi", VI_ALL)],
                         lambda e: e.matmul(pnorm[0:64, 0:512],
                                            SEL[64:96, 0:64],
                                            rdsb[64:96, 0:512],
                                            start=True, stop=True,
                                            tile_position=(64, 0)), "sBc", 1)
                    B.op("t", [], lambda e: e.matmul(pnorm[0:64, 512:1024],
                                                     SEL[64:96, 0:64],
                                                     rdsb[64:96, 512:1024],
                                                     start=True, stop=True,
                                                     tile_position=(64, 0)),
                         "sBc", 1)
                pend_mm = [mk_mm]
                B.op("v", [("sBc", bc_target)], lambda e: e.tensor_copy(
                    rnsb[0:64, 0:1024], pnorm[0:64, 0:1024]), "sNmC", 1)
                w = [("sYo", yo_rec.get(qpar, 0))]
                B.op("v", w, lambda e, qpar=qpar: e.tensor_tensor(
                    yh0[0:64, 512 * qpar:512 * qpar + 512],
                    pb4[0:64, 0:512], rnsb[0:64, 0:512], Alu.mult), "sYn", 1)
                yn_last = B.op("v", [], lambda e, qpar=qpar: e.tensor_tensor(
                    yh1[0:64, 512 * qpar:512 * qpar + 512],
                    pb5[0:64, 0:512], rnsb[0:64, 512:1024], Alu.mult),
                    "sYn", 1)
                q0 = 512 * qc
                Bt = q0 // hp
                j = 4 * u + (Bt if Bt < 4 else Bt - 4)
                cdst = (0 if Bt < 4 else hp) + q0 - hp * Bt
                w = [("sYn", yn_last), ("sCC2", 1)]
                B.op("s", w, lambda e, j=j, cdst=cdst, qpar=qpar: e.dma_start(
                    out=a2ac_in[128 * j:128 * j + 64, cdst:cdst + 512],
                    in_=yh0[0:64, 512 * qpar:512 * qpar + 512]), "sYo", 16)
                yo_rec[qpar] = B.op(
                    "s", [], lambda e, j=j, cdst=cdst, qpar=qpar: e.dma_start(
                        out=a2ac_in[128 * j + 64:128 * (j + 1), cdst:cdst + 512],
                        in_=yh1[0:64, 512 * qpar:512 * qpar + 512]), "sYo", 16)
        while av_pend:
            av_pend.pop(0)[1]()
        for fn_ in pend_mm:
            fn_()
        pend_mm = []
        B.op("g", [("sYo", B.get("sYo")), ("sCC2", 1)],
             lambda e: _a2a(e, a2ac_in, a2ac_out), "sCCy", 1)

        ATT_EX = B.get("sEx")
        ATT_NMV = B.get("sNmV")
        ATT_YO = B.get("sYo")
        ATT_YN = B.get("sYn")

        _emit(nc, blk, B, S)
        st_att.close()
        st_c = ExitStack()
        pt[0] = st_c.enter_context(nc.psum_tensor("ptC0", [128, 1024], BF16))
        pt[1] = st_c.enter_context(nc.psum_tensor("ptC1", [128, 1024], BF16))
        FMA_C = sbuf("FMA_C", [128, 8 * tpc], BF16, st_c)
        U1c = sbuf("U1c", [128, 8 * RANK], BF16, st_c)
        V1c = sbuf("V1c", [128, 2 * DFF], BF16, st_c)
        U2c = sbuf("U2c", [128, 32 * RANK], BF16, st_c)
        V2c = sbuf("V2c", [128, 2 * DM], BF16, st_c)
        sHc = sbuf("sHc", [128, 32 * nch2], BF16, st_c)
        sPc = sbuf("sPc", [128, 2 * nch2], BF16, st_c)
        sQ2c = sbuf("sQ2c", [128, 2 * nch2], BF16, st_c)

        for cc in range(8):
            B.op("g", [("sCCy", 1)] if cc == 0 else [],
                 lambda e, cc=cc: e.dma_start(
                     out=FMA_C[:, tpc * cc: tpc * (cc + 1)],
                     in_=a2ac_out[128 * cc:128 * (cc + 1), :]), "sCy", 16)
        CY_ALL = B.get("sCy")

        if DEBUG_DUMP2:
            d_fmc = nc.dram_tensor("d_fmc", [128, 8 * tpc], BF16,
                                   kind="ExternalOutput").ap()
            B.op("s", [("sCy", CY_ALL)],
                 lambda e: e.dma_start(out=d_fmc, in_=FMA_C[:, :]), "sOut", 16)
        gate_w2 = [("sNmV", ATT_NMV), ("sYo", ATT_YO), ("sYn", ATT_YN)]
        for dst, src in ffnw_dmas(U1c, V1c, U2c, V2c):
            B.op("g", gate_w2, lambda e, d=dst, s_=src: e.dma_start(out=d, in_=s_),
                 "sFw2", 16)
            gate_w2 = []
        FW2_ALL = B.get("sFw2")
        scl2 = B.op("v", [("sFw2", FW2_ALL), ("sNmV", ATT_NMV)],
                    lambda e: e.tensor_scalar_mul(V2c[:, :], V2c[:, :], 0.5),
                    "sScl", 1)

        pj_rec, pjv_rec = [], []
        pjv_tile = [None] * ntt
        i = 0
        for tl in range(ntt):
            for half in range(2):
                par = i % 2
                w = [("sCy", CY_ALL), ("sQe", QE_ALL)]
                if i < 2:
                    w += [("sEx", ATT_EX), ("sNmV", ATT_NMV)]
                else:
                    w.append(("sPjV", pjv_rec[i - 2]))

                def fn(e, tl=tl, half=half, par=par):
                    for cc in range(8):
                        ins = e.matmul(
                            PB[par][:, 0:512],
                            FMA_C[:, tpc * cc + 128 * tl: tpc * cc + 128 * (tl + 1)],
                            wprojT[:, 1024 * cc + 512 * half:
                                   1024 * cc + 512 * (half + 1)],
                            start=(cc == 0), stop=(cc == 7))
                    return ins
                pj_rec.append(B.op("t", w, fn, "sPjT", 1))
                pjv_rec.append(B.op("v", [("sPjT", pj_rec[i])],
                                   lambda e, tl=tl, half=half, par=par:
                                   e.scalar_tensor_tensor(
                                       x_sl(tl, 512 * half, 512), PB[par][:, 0:512],
                                       1.0, x_sl(tl, 512 * half, 512),
                                       Alu.mult, Alu.add), "sPjV", 1))
                if half == 1:
                    pjv_tile[tl] = pjv_rec[i]
                i += 1
        PJT_ALL = B.get("sPjT")
        PJV_ALL = B.get("sPjV")

        if DEBUG_STAGE == 2:
            for tt in range(ntt):
                B.op("s", [("sPjV", pjv_tile[tt])],
                     lambda e, tt=tt: e.dma_start(
                         out=out[128 * tt:128 * (tt + 1), :], in_=x_sl(tt)),
                     "sOut", 16)
            finish("s2")
            st_c.close()
            return nc

        gate3 = [[("sPjV", pjv_tile[tt])] for tt in range(ntt)]
        ln3 = emit_ln(gate3, gate3, [("sPjT", PJT_ALL)], FMA_C)
        bufs2 = (U1c, V1c, U2c, V2c, sPc, sHc, sQ2c, FMA_C)
        wwaits2 = [("sFw2", FW2_ALL), ("sScl", scl2), ("sPjV", PJV_ALL),
                   ("sEx", ATT_EX), ("sNmV", ATT_NMV), ("sYn", ATT_YN)]
        emit_ffn(nch2, ln3, bufs2, wwaits2, is_last=True)

        finish("s3")
        st_c.close()
    return nc


def _emit(nc, blk, B, S):
    eng_map = {"s": blk.sync, "g": blk.gpsimd, "t": blk.tensor,
               "v": blk.vector, "c": blk.scalar}
    for ename, deco in eng_map.items():
        ops = [o for o in B.ops if o.eng == ename]
        if not ops:
            continue

        def body(e, ops=ops):
            last = {}
            for o in ops:
                for (s, v) in o.waits:
                    if v > last.get(s, 0):
                        e.wait_ge(S[s], v)
                        last[s] = v
                ins = o.fn(e)
                if o.inc is not None:
                    assert ins is not None, f"inc op returned None ({ename})"
                    ins.then_inc(S[o.inc[0]], o.inc[1])
        deco(body)
    B.ops = []


_CACHE = {}


def _get_nc(tpc):
    if tpc not in _CACHE:
        _CACHE[tpc] = build_program(tpc)
    return _CACHE[tpc]


def make_in_maps(inputs, tpc):
    x = np.ascontiguousarray(np.asarray(inputs["L0_attn_x"], dtype=np.float32))
    Bn, T, C = x.shape
    xf = x.reshape(Bn * T, C)
    tri = np.triu(np.ones((128, 128), dtype=np.float32)).astype(nbf)
    ident = np.eye(128, dtype=np.float32).astype(nbf)
    identf = np.eye(128, dtype=np.float32)
    shared = dict(
        w_attn=np.ascontiguousarray(np.asarray(inputs["L0_attn_w_attn"], np.float32)),
        w_proj=np.ascontiguousarray(np.asarray(inputs["L0_attn_w_proj"], np.float32)),
        U1=np.ascontiguousarray(np.asarray(inputs["L0_ffn_U1"], np.float32)),
        V1=np.ascontiguousarray(np.asarray(inputs["L0_ffn_V1"], np.float32)),
        b1=np.ascontiguousarray(np.asarray(inputs["L0_ffn_b1"], np.float32)),
        U2=np.ascontiguousarray(np.asarray(inputs["L0_ffn_U2"], np.float32)),
        V2=np.ascontiguousarray(np.asarray(inputs["L0_ffn_V2"], np.float32)),
        b2=np.ascontiguousarray(np.asarray(inputs["L0_ffn_b2"], np.float32)),
        tri=tri, ident=ident, identf=identf,
    )
    hp = tpc // 2
    in_maps = []
    for c in range(NCORES):
        b, j = c // 4, c % 4
        base = b * (4 * tpc)
        rows = np.concatenate([
            xf[base + hp * j: base + hp * (j + 1)],
            xf[base + hp * (j + 4): base + hp * (j + 5)],
        ], axis=0)
        m = dict(shared)
        m["x_shard"] = np.ascontiguousarray(rows)
        in_maps.append(m)
    return in_maps


def kernel(**inputs):
    x = np.asarray(inputs["L0_attn_x"])
    Bn, T, C = x.shape
    tpc = (Bn * T) // NCORES
    hp = tpc // 2
    in_maps = make_in_maps(inputs, tpc)
    nc = _get_nc(tpc)
    res = run_bass_kernel_spmd(nc, in_maps, core_ids=list(range(NCORES)))
    out = np.empty((Bn * T, C), dtype=np.float32)
    for c in range(NCORES):
        shard = np.asarray(res.results[c]["out_shard"])
        b, j = c // 4, c % 4
        base = b * (4 * tpc)
        out[base + hp * j: base + hp * (j + 1)] = shard[0:hp]
        out[base + hp * (j + 4): base + hp * (j + 5)] = shard[hp:tpc]
    return out.reshape(Bn, T, C)


# version 26
# speedup vs baseline: 1.0011x; 1.0011x over previous
import numpy as np
import ml_dtypes
import concourse.bass as bass
import concourse.mybir as mybir
from concourse.bass_utils import run_bass_kernel_spmd
from concourse.alu_op_type import AluOpType
from concourse.aot_env import allow_ici_all_to_all
from contextlib import ExitStack

F32 = mybir.dt.float32
BF16 = mybir.dt.bfloat16
nbf = ml_dtypes.bfloat16
AF = mybir.ActivationFunctionType
Alu = AluOpType

NCORES = 8
DM = 1024
DFF = 4096
RANK = 256
MAGIC = 12582912.0
EXP_BIAS = -12.0
DEBUG_STAGE = 3
DEBUG_DUMP = False
DEBUG_DUMP2 = False


class Op:
    __slots__ = ("eng", "waits", "fn", "inc")

    def __init__(self, eng, waits, fn, inc):
        self.eng = eng
        self.waits = waits
        self.fn = fn
        self.inc = inc


SERIAL = False


class Builder:
    def __init__(self):
        self.ops = []
        self.cnt = {}

    def op(self, eng, waits, fn, inc=None, incn=1):
        if SERIAL:
            waits = list(waits) + [(s, v) for (s, v) in self.cnt.items()]
        w = [(s, v) for (s, v) in waits if v and v > 0]
        if inc is None:
            self.ops.append(Op(eng, w, fn, None))
            return None
        self.cnt[inc] = self.cnt.get(inc, 0) + incn
        self.ops.append(Op(eng, w, fn, (inc, incn)))
        return self.cnt[inc]

    def bump(self, name, n):
        self.cnt[name] = self.cnt.get(name, 0) + n
        return self.cnt[name]

    def get(self, s):
        return self.cnt.get(s, 0)


def _a2a(e, tin, tout):
    with allow_ici_all_to_all():
        return e.collective_compute(
            "AllToAll", mybir.AluOpType.bypass,
            replica_groups=[list(range(NCORES))],
            ins=[tin.ap().opt()], outs=[tout.ap().opt()])


def build_program(tpc):
    tb = 4 * tpc
    ntt = tpc // 128
    nqc = tb // 512
    nch1 = min(256, tpc)
    nch2 = min(256, tpc)
    nqk = min(512, tpc)
    assert tpc % 256 == 0 and tb % 512 == 0

    nc = bass.Bass("TRN2", target_bir_lowering=False)

    x_in = nc.dram_tensor("x_shard", [tpc, DM], F32, kind="ExternalInput").ap()
    w_attn = nc.dram_tensor("w_attn", [3 * DM, DM], F32, kind="ExternalInput").ap()
    w_proj = nc.dram_tensor("w_proj", [DM, DM], F32, kind="ExternalInput").ap()
    U1 = nc.dram_tensor("U1", [DM, RANK], F32, kind="ExternalInput").ap()
    V1 = nc.dram_tensor("V1", [RANK, DFF], F32, kind="ExternalInput").ap()
    b1 = nc.dram_tensor("b1", [DFF], F32, kind="ExternalInput").ap()
    U2 = nc.dram_tensor("U2", [DFF, RANK], F32, kind="ExternalInput").ap()
    V2 = nc.dram_tensor("V2", [RANK, DM], F32, kind="ExternalInput").ap()
    b2 = nc.dram_tensor("b2", [DM], F32, kind="ExternalInput").ap()
    tri_in = nc.dram_tensor("tri", [128, 128], BF16, kind="ExternalInput").ap()
    idf_in = nc.dram_tensor("identf", [128, 128], F32, kind="ExternalInput").ap()
    id_in = nc.dram_tensor("ident", [128, 128], BF16, kind="ExternalInput").ap()
    out = nc.dram_tensor("out_shard", [tpc, DM], F32, kind="ExternalOutput").ap()

    hp_ = tpc // 2
    a2a1_in = nc.dram_tensor("a2a1_in", [NCORES * 384, hp_], BF16)
    a2a1_out = nc.dram_tensor("a2a1_out", [NCORES * 384, hp_], BF16)
    a2a2_in = nc.dram_tensor("a2a2_in", [NCORES * 384, hp_], BF16)
    a2a2_out = nc.dram_tensor("a2a2_out", [NCORES * 384, hp_], BF16)
    a2ac_in = nc.dram_tensor("a2ac_in", [NCORES * 128, tpc], BF16)
    a2ac_out = nc.dram_tensor("a2ac_out", [NCORES * 128, tpc], BF16)

    B = Builder()
    SEMS = ["sX", "sW", "sScl", "sLnV", "sLnC", "sLnSt", "sLnAp", "sLnTr",
            "sLnEv", "sFp", "sFpe", "sFh", "sFg", "sFq2", "sFq2e", "sFo",
            "sFoV", "sQl", "sAbs", "sQs", "sQt", "sQv1", "sQg", "sQv3",
            "sQtr", "sQe", "sQK", "sQKe", "sQKd", "sQV", "sQVe", "sQVd",
            "sCC1", "sCC2", "sCCy", "sBl", "sBv", "sSc", "sEx", "sMk", "sAv",
            "sNmC", "sNmV", "sYn", "sYo", "sCy", "sFw2", "sPjT", "sPjV", "sOut",
            "sVa", "sVi", "sBc"]

    with ExitStack() as top:
        def sbuf(name, shape, dt, stack=None):
            return (stack or top).enter_context(nc.sbuf_tensor(name, shape, dt))

        X = sbuf("X", [128, ntt * DM], F32)
        xlntok = sbuf("xlntok", [128, 2 * DM], BF16)
        xdumpV = sbuf("xdumpV", [128, DM], BF16)
        xdumpC = sbuf("xdumpC", [128, DM], BF16)
        SUMX = sbuf("SUMX", [128, ntt], F32)
        SUMQ = sbuf("SUMQ", [128, ntt], F32)
        MEAN = sbuf("MEAN", [128, ntt], F32)
        MSQ = sbuf("MSQ", [128, ntt], F32)
        VAR = sbuf("VAR", [128, ntt], F32)
        SD = sbuf("SD", [128, ntt], F32)
        RST = sbuf("RST", [128, ntt], F32)
        tri = sbuf("tri_s", [128, 128], BF16)
        idb = sbuf("idb", [128, 128], BF16)
        onesb = sbuf("onesb", [128, 128], BF16)
        ones1 = sbuf("ones1", [1, 128], BF16)
        onesf = sbuf("onesf", [128, 128], F32)
        b1sb = sbuf("b1sb", [128, DFF // 128], F32)
        b1T = sbuf("b1T", [32, 128], F32)
        idf = sbuf("idf", [128, 128], F32)
        b2rowh = sbuf("b2rowh", [1, DM], BF16)
        scaleA = sbuf("scaleA", [128, 1], F32)
        iscaleA = sbuf("iscaleA", [128, 1], F32)
        scaleP = sbuf("scaleP", [128, 1], F32)
        iscaleP = sbuf("iscaleP", [128, 1], F32)
        PARTS = sbuf("PARTS", [128, 32], F32)
        PRED = sbuf("PRED", [128, 2], F32)
        epsb = sbuf("epsb", [128, 1], F32)
        wprojT = sbuf("wprojT", [128, 8192], BF16)
        nbias = sbuf("nbias", [128, 1], F32)
        RDBK = sbuf("RDBK", [128, 2], F32)

        pSS = top.enter_context(nc.psum_tensor("pSS", [128, 2048], F32))
        pb4 = top.enter_context(nc.psum_tensor("pb4", [128, 512], F32))
        pb5 = top.enter_context(nc.psum_tensor("pb5", [128, 512], F32))
        st_ptA = ExitStack()
        pt = [st_ptA.enter_context(nc.psum_tensor(f"ptA{i}", [128, 1024], BF16))
              for i in range(2)]
        PB = [pSS[:, 0:512], pSS[:, 512:1024], pSS[:, 1024:1536],
              pSS[:, 1536:2048], pb4[:, 0:512], pb5[:, 0:512]]

        S = {s: top.enter_context(nc.semaphore(s)) for s in SEMS}
        blk = top.enter_context(nc.Block())

        def x_sl(tt, c0=0, n=DM):
            return X[:, DM * tt + c0: DM * tt + c0 + n]

        def finish(stage_tag):
            B.op("g", [("sOut", B.get("sOut"))], lambda e: None)
            _emit(nc, blk, B, S)

        st_fma = ExitStack()
        FMA_A = sbuf("FMA_A", [128, 8 * tpc], BF16, st_fma)
        st_ffn1 = ExitStack()
        U1a = sbuf("U1a", [128, 8 * RANK], BF16, st_ffn1)
        V1a = sbuf("V1a", [128, 2 * DFF], BF16, st_ffn1)
        U2a = sbuf("U2a", [128, 32 * RANK], BF16, st_ffn1)
        V2a = sbuf("V2a", [128, 2 * DM], BF16, st_ffn1)
        sHa = sbuf("sHa", [128, 32 * nch1], BF16, st_ffn1)
        sPa = sbuf("sPa", [128, 2 * nch1], BF16, st_ffn1)
        sQ2a = sbuf("sQ2a", [128, 2 * nch1], BF16, st_ffn1)
        wv1 = sbuf("wv1", [128, DM], F32, st_ffn1)

        B.op("s", [], lambda e: e.dma_start(
            out=X[:, :].rearrange("p (tt c) -> p tt c", c=DM),
            in_=x_in.rearrange("(tt p) c -> p tt c", p=128)), "sX", 16)

        def wtile_src(j):
            return (w_attn[128 * j:128 * (j + 1), :] if j < 24
                    else w_proj[128 * (j - 24):128 * (j - 23), :])

        for j in range(32):
            w = [] if j < 1 else [("sAbs", j)]
            B.op("s", w, lambda e, j=j: e.dma_start(
                out=wv1[:, :], in_=wtile_src(j)), "sQl", 16)

        def ffnw_dmas(U1d, V1d, U2d, V2d):
            return [
                (U1d[:, :].rearrange("p (cc r) -> p cc r", r=RANK),
                 U1.rearrange("(cc p) r -> p cc r", p=128)),
                (V1d[:, :].rearrange("p (rt f) -> p rt f", f=DFF),
                 V1.rearrange("(rt p) f -> p rt f", p=128)),
                (U2d[:, :].rearrange("p (ft r) -> p ft r", r=RANK),
                 U2.rearrange("(ft p) r -> p ft r", p=128)),
                (V2d[:, :].rearrange("p (rt c) -> p rt c", c=DM),
                 V2.rearrange("(rt p) c -> p rt c", p=128)),
            ]

        for dst, src in [
            (b2rowh[:, :], b2.rearrange("(a c) -> a c", a=1)),
            (tri[:, :], tri_in),
            (idb[:, :], id_in),
            (idf[:, :], idf_in),
        ] + ffnw_dmas(U1a, V1a, U2a, V2a):
            B.op("g", [], lambda e, d=dst, s_=src: e.dma_start(out=d, in_=s_),
                 "sW", 16)
        B.op("g", [], lambda e: e.dma_start(
            out=b1T[:, :], in_=b1.rearrange("(ft p) -> ft p", p=128)), "sW", 16)
        B.op("g", [], lambda e: e.memset(onesb[:, :], 1.0), "sW", 1)
        B.op("g", [], lambda e: e.memset(ones1[:, :], 1.0), "sW", 1)
        B.op("g", [], lambda e: e.memset(onesf[:, :], 1.0), "sW", 1)
        B.op("g", [], lambda e: e.memset(epsb[:, :], 1e-5), "sW", 1)
        B.op("g", [], lambda e: e.memset(nbias[:, :], EXP_BIAS), "sW", 1)
        bt = B.op("t", [("sW", B.get("sW"))],
                  lambda e: e.matmul(pb4[:, 0:128], b1T[:, :], idf[0:32, :],
                                     start=True, stop=True), "sW", 1)
        B.op("v", [("sW", bt)],
             lambda e: e.tensor_copy(b1sb[:, :], pb4[:, 0:32]), "sW", 1)
        W_ALL = B.get("sW")

        B.op("v", [("sW", W_ALL)],
             lambda e: e.tensor_scalar_mul(V2a[:, :], V2a[:, :], 0.5))
        scl1 = B.op("v", [], lambda e: e.tensor_scalar_mul(
            b2rowh[:, :], b2rowh[:, :], 0.5), "sScl", 1)

        def emit_ln(gate_v, gate_c, fma_war, FMAd, tr_extra=()):
            r = {"ev": [], "ap": [], "tr": []}
            for tt in range(ntt):
                B.op("v", [("sX", 16)] + gate_v[tt],
                     lambda e, tt=tt: e.tensor_scalar(
                         xdumpV[:, :], x_sl(tt), 1.0, 0.0, Alu.mult, Alu.add,
                         accum_out=SUMX[:, tt:tt + 1]), "sLnV", 1)
            for tt in range(ntt):
                B.op("c", [("sX", 16)] + gate_c[tt],
                     lambda e, tt=tt: e.activation(
                         xdumpC[:, :], x_sl(tt), AF.Square,
                         accum_out=SUMQ[:, tt:tt + 1]), "sLnC", 1)
            vc = B.get("sLnC")
            va = B.get("sLnV")
            st0 = B.op("c", [("sLnV", va)],
                       lambda e: e.activation(MEAN[:, :], SUMX[:, :], AF.Copy,
                                              scale=1.0 / DM), "sLnSt", 1)
            B.op("v", [("sLnSt", st0)],
                 lambda e: e.tensor_tensor(MSQ[:, :], MEAN[:, :], MEAN[:, :], Alu.mult))
            B.op("v", [], lambda e: e.tensor_copy(xdumpV[:, :], xdumpV[:, :]))
            st1 = B.op("v", [("sLnC", vc)],
                       lambda e: e.scalar_tensor_tensor(
                           VAR[:, :], SUMQ[:, :], 1.0 / DM, MSQ[:, :],
                           Alu.mult, Alu.subtract), "sLnSt", 1)
            st2 = B.op("c", [("sLnSt", st1)],
                       lambda e: e.activation(SD[:, :], VAR[:, :], AF.Sqrt, bias=epsb[:, 0:1]),
                       "sLnSt", 1)
            B.op("v", [("sLnSt", st2)],
                 lambda e: e.reciprocal(RST[:, :], SD[:, :]))
            B.op("v", [], lambda e: e.tensor_copy(xdumpV[:, :], xdumpV[:, :]))
            for tt in range(ntt):
                par = tt % 2
                w = [("sLnTr", r["tr"][tt - 2])] if tt >= 2 else []
                r["ap"].append(B.op("v", w, lambda e, tt=tt, par=par: e.tensor_scalar(
                    xlntok[:, DM * par: DM * par + DM], x_sl(tt),
                    MEAN[:, tt:tt + 1], RST[:, tt:tt + 1],
                    Alu.subtract, Alu.mult), "sLnAp", 1))

                def tr_fn(e, tt=tt, par=par):
                    for cc in range(8):
                        ins = e.transpose(
                            pt[par][:, 128 * cc:128 * (cc + 1)],
                            xlntok[:, DM * par + 128 * cc: DM * par + 128 * (cc + 1)],
                            idb[:, :])
                    return ins
                w = [("sLnAp", r["ap"][tt]), ("sW", W_ALL)]
                if tt < 2:
                    w += list(tr_extra)
                else:
                    w.append(("sLnEv", r["ev"][tt - 2]))
                r["tr"].append(B.op("t", w, tr_fn, "sLnTr", 1))

                def ev_fn(e, tt=tt, par=par, FMAd=FMAd):
                    dst = FMAd[:, :].rearrange("p (cc t) -> p cc t", t=tpc)[
                        :, :, 128 * tt:128 * (tt + 1)]
                    src = pt[par][:, :].rearrange("p (cc r) -> p cc r", r=128)
                    return e.tensor_copy(dst, src)
                w = [("sLnTr", r["tr"][tt])] + (fma_war if tt == 0 else [])
                r["ev"].append(B.op("v", w, ev_fn, "sLnEv", 1))
            return r

        def emit_ffn(nch, ln, bufs, wwaits, is_last, extra_c=None):
            U1w, V1w, U2w, V2w, sPb, sHb, sQ2b, FMAd = bufs
            nchunks = tpc // nch
            ntl = nch // 128
            r = {"ov_tile": [None] * ntt}
            prev_q2e = [0, 0]
            prev_g = {}
            prev_ov = {}
            prev_o_last = 0
            prev_h_last = 0
            for n in range(nchunks):
                t0 = n * nch
                p_rec, h_rec, q2_rec, o_rec, g_list = [], [], [], [], []
                for rt in range(2):
                    w = [("sLnEv", ln["ev"][(t0 + nch) // 128 - 1]),
                         ("sFq2e", prev_q2e[rt])] + wwaits

                    def p_fn(e, rt=rt, t0=t0, U1w=U1w, FMAd=FMAd):
                        for cc in range(8):
                            ins = e.matmul(
                                PB[rt][:, 0:nch],
                                U1w[:, RANK * cc + 128 * rt: RANK * cc + 128 * (rt + 1)],
                                FMAd[:, tpc * cc + t0: tpc * cc + t0 + nch],
                                start=(cc == 0), stop=(cc == 7))
                        return ins
                    p_rec.append(B.op("t", w, p_fn, "sFp", 1))
                pe_rec = []
                for rt in range(2):
                    w = [("sFp", p_rec[rt]), ("sFh", prev_h_last)]
                    pe_rec.append(B.op("c", w, lambda e, rt=rt, sPb=sPb:
                                       e.activation(
                                           sPb[:, nch * rt: nch * rt + nch],
                                           PB[rt][:, 0:nch], AF.Copy), "sFpe", 1))
                for ft in range(32):
                    par = 2 + (ft % 2)
                    w = [("sFpe", pe_rec[1])]
                    if par in prev_g:
                        w.append(("sFg", prev_g[par]))

                    def h_fn(e, ft=ft, par=par, V1w=V1w, sPb=sPb):
                        for rt in range(2):
                            ins = e.matmul(
                                PB[par][:, 0:nch],
                                V1w[:, DFF * rt + 128 * ft: DFF * rt + 128 * (ft + 1)],
                                sPb[:, nch * rt: nch * rt + nch],
                                start=(rt == 0), stop=(rt == 1))
                        return ins
                    h_rec.append(B.op("t", w, h_fn, "sFh", 1))
                    g = B.op("c", [("sFh", h_rec[ft])],
                             lambda e, ft=ft, par=par, sHb=sHb: e.activation(
                                 sHb[:, nch * ft: nch * ft + nch], PB[par][:, 0:nch],
                                 AF.Gelu_apprx_tanh, bias=b1sb[:, ft:ft + 1]),
                             "sFg", 1)
                    prev_g[par] = g
                    g_list.append(g)
                for rt in range(2):
                    def q2_fn(e, rt=rt, g_list=tuple(g_list), U2w=U2w, sHb=sHb):
                        for ft in range(32):
                            e.wait_ge(S["sFg"], g_list[ft])
                            ins = e.matmul(
                                PB[rt][:, 0:nch],
                                U2w[:, RANK * ft + 128 * rt: RANK * ft + 128 * (rt + 1)],
                                sHb[:, nch * ft: nch * ft + nch],
                                start=(ft == 0), stop=(ft == 31))
                        return ins
                    w = [("sFpe", pe_rec[rt])]
                    q2_rec.append(B.op("t", w, q2_fn, "sFq2", 1))
                q2e_rec = []
                for rt in range(2):
                    w = [("sFq2", q2_rec[rt]), ("sFo", prev_o_last)]
                    q2e_rec.append(B.op("v", w, lambda e, rt=rt, sPb=sPb, sQ2b=sQ2b:
                                        e.tensor_copy(
                                            sQ2b[:, nch * rt: nch * rt + nch],
                                            PB[rt][:, 0:nch]), "sFq2e", 1))
                    prev_q2e[rt] = q2e_rec[rt]
                for tl in range(ntl):
                    for half in range(2):
                        idx = tl * 2 + half
                        par = 4 + (idx % 2)
                        gtt = t0 // 128 + tl
                        w = [("sFq2e", q2e_rec[1])] + wwaits
                        if par in prev_ov:
                            w.append(("sFoV", prev_ov[par]))

                        def o_fn(e, tl=tl, half=half, par=par, sQ2b=sQ2b, V2w=V2w):
                            for rt in range(2):
                                e.matmul(
                                    PB[par][:, 0:512],
                                    sQ2b[:, nch * rt + 128 * tl: nch * rt + 128 * (tl + 1)],
                                    V2w[:, DM * rt + 512 * half: DM * rt + 512 * (half + 1)],
                                    start=(rt == 0), stop=False)
                            return e.matmul(
                                PB[par][:, 0:512], ones1[0:1, 0:128],
                                b2rowh[0:1, 512 * half: 512 * (half + 1)],
                                start=False, stop=True)
                        o = B.op("t", w, o_fn, "sFo", 1)
                        o_rec.append(o)
                        prev_o_last = o
                        ov = B.op("v", [("sFo", o)],
                                  lambda e, gtt=gtt, half=half, par=par:
                                  e.scalar_tensor_tensor(
                                      x_sl(gtt, 512 * half, 512), PB[par][:, 0:512],
                                      1.0, x_sl(gtt, 512 * half, 512),
                                      Alu.mult, Alu.add), "sFoV", 1)
                        prev_ov[par] = ov
                        if half == 1:
                            r["ov_tile"][gtt] = ov
                            if is_last:
                                B.op("s", [("sFoV", ov)],
                                     lambda e, gtt=gtt: e.dma_start(
                                         out=out[128 * gtt:128 * (gtt + 1), :],
                                         in_=x_sl(gtt)), "sOut", 16)
                prev_h_last = h_rec[-1]
                if extra_c:
                    take = min(4, len(extra_c))
                    for fn_ in extra_c[:take]:
                        fn_()
                    del extra_c[:take]
            return r

        abs_rec = []

        def _mk_abs(j):
            def emit():
                abs_rec.append(B.op("c", [("sQl", 16 * (j + 1))],
                                    lambda e, j=j: e.activation(
                                        xdumpC[:, :], wv1[:, :], AF.Abs,
                                        accum_out=PARTS[:, j:j + 1]), "sAbs", 1))
            return emit
        abs_work = [_mk_abs(j) for j in range(32)]

        ln1 = emit_ln([[]] * ntt, [[]] * ntt, [], FMA_A)
        if DEBUG_DUMP:
            d_fma = nc.dram_tensor("d_fma", [128, 8 * tpc], BF16,
                                   kind="ExternalOutput").ap()
            d_st = nc.dram_tensor("d_st", [128, 8 * ntt], F32,
                                  kind="ExternalOutput").ap()
            B.op("s", [("sLnEv", B.get("sLnEv"))],
                 lambda e: e.dma_start(out=d_fma, in_=FMA_A[:, :]), "sOut", 16)

            def st_fn(e):
                for k, tns in enumerate([SUMX, SUMQ, MEAN, MSQ, VAR, SD, RST]):
                    e.dma_start(out=d_st[:, ntt * k: ntt * (k + 1)],
                                in_=tns[:, :]).then_inc(S["sOut"], 16)
                return None
            B.op("s", [("sLnAp", B.get("sLnAp"))], st_fn)
            B.bump("sOut", 16 * 7)
        bufs1 = (U1a, V1a, U2a, V2a, sPa, sHa, sQ2a, FMA_A)
        ffn1 = emit_ffn(nch1, ln1, bufs1, [("sW", W_ALL), ("sScl", scl1)],
                        is_last=(DEBUG_STAGE == 1), extra_c=abs_work)
        if DEBUG_DUMP:
            d_p = nc.dram_tensor("d_p", [128, 2 * nch1], BF16,
                                 kind="ExternalOutput").ap()
            d_h = nc.dram_tensor("d_h", [128, 32 * nch1], BF16,
                                 kind="ExternalOutput").ap()
            d_q2 = nc.dram_tensor("d_q2", [128, 2 * nch1], BF16,
                                  kind="ExternalOutput").ap()
            B.op("s", [("sFpe", B.get("sFpe"))],
                 lambda e: e.dma_start(out=d_p, in_=sPa[:, :]), "sOut", 16)
            B.op("s", [("sFg", B.get("sFg"))],
                 lambda e: e.dma_start(out=d_h, in_=sHa[:, :]), "sOut", 16)
            B.op("s", [("sFq2e", B.get("sFq2e"))],
                 lambda e: e.dma_start(out=d_q2, in_=sQ2a[:, :]), "sOut", 16)

        if DEBUG_STAGE == 1:
            finish("s1")
            st_ffn1.close()
            st_fma.close()
            return nc

        FFN1_OV = B.get("sFoV")
        FFN1_Q2E = B.get("sFq2e")
        FFN1_G = B.get("sFg")
        FFN1_FP = B.get("sFp")

        gate = [[("sFoV", ffn1["ov_tile"][tt])] for tt in range(ntt)]
        ln2 = emit_ln(gate, gate, [("sFp", FFN1_FP)], FMA_A)
        LN2_EV_ = B.get("sLnEv")

        _emit(nc, blk, B, S)
        st_ffn1.close()
        st_quant = ExitStack()
        wave = sbuf("wave", [128, 2 * DM], F32, st_quant)
        Q32 = sbuf("Q32", [128, 2 * DM], F32, st_quant)
        WQ = sbuf("WQ", [128, 2 * DM], BF16, st_quant)
        wattnT = sbuf("wattnT", [128, 24576], BF16, st_quant)

        for fn_ in abs_work:
            fn_()
        abs_work.clear()
        B.op("v", [("sAbs", abs_rec[-1])], lambda e: e.tensor_reduce(
            PRED[:, 0:1], PARTS[:, 0:24], mybir.AxisListType.X, Alu.add))
        qs1 = B.op("v", [], lambda e: e.tensor_reduce(
            PRED[:, 1:2], PARTS[:, 24:32], mybir.AxisListType.X, Alu.add), "sQs", 1)
        qt1 = B.op("t", [("sQs", qs1), ("sW", W_ALL), ("sFq2e", FFN1_Q2E),
                         ("sFoV", FFN1_OV)],
                   lambda e: (e.matmul(pSS[:, 0:1], onesf[:, :], PRED[:, 0:1],
                                       start=True, stop=True),
                              e.matmul(pSS[:, 512:513], onesf[:, :], PRED[:, 1:2],
                                       start=True, stop=True))[-1], "sQt", 1)
        B.op("v", [("sQt", qt1)], lambda e: e.tensor_scalar(
            scaleA[:, :], pSS[:, 0:1], 1.0 / (3072 * 1024), 1e-8, Alu.mult, Alu.max))
        B.op("v", [], lambda e: e.tensor_scalar(
            scaleP[:, :], pSS[:, 512:513], 1.0 / (1024 * 1024), 1e-8, Alu.mult, Alu.max))
        B.op("v", [], lambda e: e.tensor_copy(xdumpV[:, :], xdumpV[:, :]))
        B.op("v", [], lambda e: e.reciprocal(iscaleA[:, :], scaleA[:, :]))
        B.op("v", [], lambda e: e.tensor_copy(xdumpV[:, :], xdumpV[:, :]))
        qs2 = B.op("v", [], lambda e: e.reciprocal(iscaleP[:, :], scaleP[:, :]),
                   "sQs", 1)

        c1_rec, c3_rec, tr_rec, ev_rec = [], [], [], []

        def q_c1(j):
            par = j % 2
            isc = iscaleA if j < 24 else iscaleP
            w = [("sQl", 16 * (32 + j // 2 + 1)), ("sQs", qs2)]
            c1_rec.append(B.op("v", w, lambda e, par=par, isc=isc:
                               e.tensor_scalar(
                                   Q32[:, DM * par: DM * (par + 1)],
                                   wave[:, DM * par: DM * (par + 1)],
                                   isc[:, :], 2.25, Alu.mult, Alu.min), "sQv1", 1))

        def q_tail(j):
            par = j % 2
            sc_ = scaleA if j < 24 else scaleP
            B.op("v", [], lambda e, par=par: e.tensor_scalar(
                Q32[:, DM * par: DM * (par + 1)],
                Q32[:, DM * par: DM * (par + 1)],
                -2.25, MAGIC, Alu.max, Alu.add))
            B.op("v", [], lambda e: e.tensor_copy(xdumpV[:, :], xdumpV[:, :]))
            w = [("sQtr", tr_rec[j - 2])] if j >= 2 else []
            c3_rec.append(B.op("v", w, lambda e, par=par, sc_=sc_: e.tensor_scalar(
                WQ[:, DM * par: DM * (par + 1)],
                Q32[:, DM * par: DM * (par + 1)],
                -MAGIC, sc_[:, :], Alu.add, Alu.mult), "sQv3", 1))

            def tr_fn(e, par=par):
                for cc in range(8):
                    ins = e.transpose(
                        pt[par][:, 128 * cc:128 * (cc + 1)],
                        WQ[:, DM * par + 128 * cc: DM * par + 128 * (cc + 1)],
                        idb[:, :])
                return ins
            w = [("sQv3", c3_rec[j]), ("sLnEv", LN2_EV_)]
            if j >= 2:
                w.append(("sQe", ev_rec[j - 2]))
            tr_rec.append(B.op("t", w, tr_fn, "sQtr", 1))

            def ev_fn(e, j=j, par=par):
                if j < 24:
                    dst = wattnT[:, :].rearrange("p (cc o) -> p cc o", o=3072)[
                        :, :, 128 * j:128 * (j + 1)]
                else:
                    dst = wprojT[:, :].rearrange("p (cc o) -> p cc o", o=1024)[
                        :, :, 128 * (j - 24):128 * (j - 23)]
                src_ = pt[par][:, :].rearrange("p (cc r) -> p cc r", r=128)
                return e.activation(dst, src_, AF.Copy)
            ev_rec.append(B.op("c", [("sQtr", tr_rec[j])], ev_fn, "sQe", 1))

        def wpair_src(p):
            j = 2 * p
            if j < 24 and j + 1 < 24:
                return w_attn[128 * j:128 * (j + 2), :]
            return w_proj[128 * (j - 24):128 * (j - 22), :]

        for p in range(16):
            w = [("sQs", qs2)] if p == 0 else [("sQv1", c1_rec[2 * p - 1])]
            B.op("s" if p % 2 == 0 else "g", w, lambda e, p=p: e.dma_start(
                out=wave[:, :].rearrange("q (j c) -> q j c", c=DM),
                in_=wpair_src(p).rearrange("(j q) c -> q j c", q=128)),
                "sQl", 16)
            if p >= 1:
                q_tail(2 * p - 2)
            q_c1(2 * p)
            if p >= 1:
                q_tail(2 * p - 1)
            q_c1(2 * p + 1)
        q_tail(30)
        q_tail(31)
        QE_ALL = B.get("sQe")

        if DEBUG_DUMP2:
            d_wat = nc.dram_tensor("d_wat", [128, 24576], BF16,
                                   kind="ExternalOutput").ap()
            d_wpt = nc.dram_tensor("d_wpt", [128, 8192], BF16,
                                   kind="ExternalOutput").ap()
            B.op("s", [("sQe", B.get("sQe"))],
                 lambda e: e.dma_start(out=d_wat, in_=wattnT[:, :]), "sOut", 16)
            B.op("s", [("sQe", B.get("sQe"))],
                 lambda e: e.dma_start(out=d_wpt, in_=wprojT[:, :]), "sOut", 16)

        st_qkv = ExitStack()
        qkstage = sbuf("qkstage", [128, 2 * tpc], BF16, st_qkv)
        vstage = sbuf("vstage", [128, 2 * DM], BF16, st_qkv)
        hp = tpc // 2
        nqk_h = min(512, hp)
        nchq_h = hp // nqk_h
        war_ffn1 = [("sFq2e", FFN1_Q2E), ("sFg", FFN1_G), ("sFoV", FFN1_OV)]
        LN2_EV = LN2_EV_
        LN2EVL = ln2["ev"]

        def vflat_in(a2a):
            return (a2a.rearrange("(j s) c -> j s c", j=NCORES)[:, 256:384, :]
                    .rearrange("j s c -> j (s c)")
                    .rearrange("j (t d) -> t j d", d=128))

        qv_rec, qve_rec, qvd_rec = [], [], []
        qk_rec, qke_rec = [], []
        qkd_rec = {}
        iv = [0]
        iq = [0]

        def emit_v_tile(tl, half):
            for hf2 in range(2):
                i = iv[0]
                par = 2 + (i % 2)
                w = [("sQe", ev_rec[19 + 4 * hf2]), ("sLnEv", LN2EVL[tl])] \
                    + war_ffn1
                if i >= 2:
                    w.append(("sQVe", qve_rec[i - 2]))

                def qv_fn(e, tl=tl, hf2=hf2, par=par):
                    for cc in range(8):
                        ins = e.matmul(
                            PB[par][:, 0:512],
                            FMA_A[:, tpc * cc + 128 * tl: tpc * cc + 128 * (tl + 1)],
                            wattnT[:, 3072 * cc + 2048 + 512 * hf2:
                                   3072 * cc + 2048 + 512 * (hf2 + 1)],
                            start=(cc == 0), stop=(cc == 7))
                    return ins
                qv_rec.append(B.op("t", w, qv_fn, "sQV", 1))
                w = [("sQV", qv_rec[i])]
                if tl >= 2 and hf2 == 0:
                    w.append(("sQVd", 16 * (tl - 1)))
                qve_rec.append(B.op("c", w, lambda e, tl=tl, hf2=hf2, par=par:
                                    e.activation(
                                        vstage[:, DM * (tl % 2) + 512 * hf2:
                                               DM * (tl % 2) + 512 * (hf2 + 1)],
                                        PB[par][:, 0:512], AF.Copy), "sQVe", 1))
                iv[0] += 1
            rb = B.op("v", [("sQVe", qve_rec[-1])],
                      lambda e, tl=tl: e.tensor_reduce(
                          RDBK[:, 1:2], vstage[:, DM * (tl % 2): DM * (tl % 2 + 1)],
                          mybir.AxisListType.X, Alu.add), "sQVe", 1)
            w = [("sQVe", rb)]
            if half == 1:
                w.append(("sCC1", 1))
            a2a = a2a1_in if half == 0 else a2a2_in
            tl2 = tl % (ntt // 2)

            def vd_fn(e, a2a=a2a, tl=tl, tl2=tl2):
                dst = vflat_in(a2a)[128 * tl2:128 * (tl2 + 1), :, :]
                return e.dma_start(
                    out=dst,
                    in_=vstage[:, DM * (tl % 2): DM * (tl % 2 + 1)].rearrange(
                        "p (j d) -> p j d", d=128))
            qvd_rec.append(B.op("s", w, vd_fn, "sQVd", 16))

        def emit_qk_tile(ot, half):
            for nq in range(nchq_h):
                i = iq[0]
                par = i % 2
                c0 = hp * half + nqk_h * nq
                w = [("sQe", ev_rec[ot]),
                     ("sLnEv", LN2EVL[(c0 + nqk_h) // 128 - 1])] + war_ffn1
                if i >= 2:
                    w.append(("sQKe", qke_rec[i - 2]))

                def qk_fn(e, ot=ot, c0=c0, par=par):
                    for cc in range(8):
                        ins = e.matmul(
                            PB[par][:, 0:nqk_h],
                            wattnT[:, 3072 * cc + 128 * ot: 3072 * cc + 128 * (ot + 1)],
                            FMA_A[:, tpc * cc + c0: tpc * cc + c0 + nqk_h],
                            start=(cc == 0), stop=(cc == 7))
                    return ins
                qk_rec.append(B.op("t", w, qk_fn, "sQK", 1))
                w = [("sQK", qk_rec[i])]
                if (ot - 2, half) in qkd_rec and nq == 0:
                    w.append(("sQKd", qkd_rec[(ot - 2, half)]))
                elif half == 1 and ot < 2 and nq == 0 and (14 + ot, 0) in qkd_rec:
                    w.append(("sQKd", qkd_rec[(14 + ot, 0)]))

                def qke_fn(e, ot=ot, half=half, nq=nq, par=par):
                    sl = tpc * (ot % 2) + hp * half + nqk_h * nq
                    return e.activation(qkstage[:, sl: sl + nqk_h],
                                        PB[par][:, 0:nqk_h], AF.Copy)
                qke_rec.append(B.op("c", w, qke_fn, "sQKe", 1))
                iq[0] += 1
            rb = B.op("v", [("sQKe", qke_rec[-1])],
                      lambda e, ot=ot, half=half: e.tensor_reduce(
                          RDBK[:, 0:1],
                          qkstage[:, tpc * (ot % 2) + hp * half:
                                  tpc * (ot % 2) + hp * half + hp],
                          mybir.AxisListType.X, Alu.add), "sQKe", 1)
            w = [("sQKe", rb)]
            if half == 1:
                w.append(("sCC1", 1))
            a2a = a2a1_in if half == 0 else a2a2_in
            j = ot % 8
            r0 = 384 * j + 128 * (ot // 8)
            qkd_rec[(ot, half)] = B.op(
                "s", w, lambda e, a2a=a2a, r0=r0, ot=ot, half=half: e.dma_start(
                    out=a2a[r0:r0 + 128, :],
                    in_=qkstage[:, tpc * (ot % 2) + hp * half:
                                tpc * (ot % 2) + hp * half + hp]),
                "sQKd", 16)

        for ot in range(16):
            emit_qk_tile(ot, 0)
        for tl in range(ntt // 2):
            emit_v_tile(tl, 0)
        B.op("g", [("sQKd", B.get("sQKd")), ("sQVd", B.get("sQVd"))],
             lambda e: _a2a(e, a2a1_in, a2a1_out), "sCC1", 1)
        for ot in range(16):
            emit_qk_tile(ot, 1)
        for tl in range(ntt // 2, ntt):
            emit_v_tile(tl, 1)
        B.op("g", [("sQKd", B.get("sQKd")), ("sQVd", B.get("sQVd")),
                   ("sCC1", 1)],
             lambda e: _a2a(e, a2a2_in, a2a2_out), "sCC2", 1)
        QKE_ALL = B.get("sQKe")
        QVE_ALL = B.get("sQVe")

        _emit(nc, blk, B, S)
        st_ptA.close()
        st_qkv.close()
        st_quant.close()
        st_fma.close()
        st_att = ExitStack()
        nkbh = tb // 128
        sQKV = sbuf("sQKV", [128, 3 * tb], BF16, st_att)
        sVaug = sbuf("sVaug", [128, 130 * nkbh], BF16, st_att)
        sPP = sbuf("sPP", [128, 4096], BF16, st_att)
        rdsb = sbuf("rdsb", [128, 1024], BF16, st_att)
        rnsb = sbuf("rnsb", [128, 1024], F32, st_att)
        SEL = sbuf("SEL", [128, 128], BF16, st_att)
        yh0 = sbuf("yh0", [128, 1024], BF16, st_att)
        yh1 = sbuf("yh1", [128, 1024], BF16, st_att)
        pnorm = st_att.enter_context(nc.psum_tensor("pnorm", [128, 1024], F32))
        sQ = sQKV[:, 0:tb]
        sK = sQKV[:, tb:2 * tb]
        sV = sQKV[:, 2 * tb:3 * tb]

        B.op("g", [("sCC1", 1)], lambda e: e.memset(sVaug[:, :], 1.0), "sVi", 1)
        B.op("g", [], lambda e: e.memset(SEL[:, :], 0.0), "sVi", 1)
        B.op("g", [], lambda e: e.memset(SEL[64:65, :], 1.0), "sVi", 1)
        B.op("g", [], lambda e: e.memset(rdsb[:, :], 0.0), "sVi", 1)
        VI_ALL = B.get("sVi")

        def emit_qkload(u, half, waits):
            first = True
            for Bq in range(4):
                Bg = Bq + 4 * half
                gout = a2a1_out if half == 0 else a2a2_out
                r0 = 384 * (4 * u + Bq)
                w = waits if first else []
                first = False
                B.op("c", w, lambda e, gout=gout, r0=r0, Bg=Bg: e.dma_start(
                    out=sQ[:, hp * Bg: hp * (Bg + 1)],
                    in_=gout[r0:r0 + 128, :]), "sBl", 16)
                B.op("c", [], lambda e, gout=gout, r0=r0, Bg=Bg: e.dma_start(
                    out=sK[:, hp * Bg: hp * (Bg + 1)],
                    in_=gout[r0 + 128:r0 + 256, :]), "sBl", 16)
            return B.get("sBl")

        def emit_vload(u, half, waits):
            engs = ["g", "c", "g", "c"]
            for Bq in range(4):
                Bg = Bq + 4 * half
                gout = a2a1_out if half == 0 else a2a2_out
                r0 = 384 * (4 * u + Bq) + 256

                def fn(e, gout=gout, r0=r0, Bg=Bg):
                    src = (gout[r0:r0 + 128, :]
                           .rearrange("r c -> (r c)")
                           .rearrange("(kb q d) -> q kb d", q=128, d=128))
                    return e.dma_start(
                        out=sV[:, hp * Bg: hp * (Bg + 1)].rearrange(
                            "q (kb d) -> q kb d", d=128),
                        in_=src)
                B.op(engs[Bq], list(waits), fn, "sBv", 16)
            return B.get("sBv")

        nkb_half = nkbh // 2

        def emit_vaug(u, half, bv_snap, av_snap):
            kb0 = nkb_half * half
            c0 = 4 * hp * half

            def src3(e_cols):
                return sV[:, c0:c0 + 4 * hp].rearrange(
                    "p (kb c) -> p kb c", c=128)[:, :, e_cols[0]:e_cols[1]]
            w = [("sBv", bv_snap), ("sVi", VI_ALL)]
            if av_snap:
                w.append(("sAv", av_snap))
            B.op("v", w, lambda e, kb0=kb0: e.tensor_copy(
                sVaug[:, :].rearrange("p (kb c) -> p kb c", c=130)[
                    :, kb0:kb0 + nkb_half, 0:64], src3((0, 64))))
            return B.op("v", [], lambda e, kb0=kb0: e.tensor_copy(
                sVaug[:, :].rearrange("p (kb c) -> p kb c", c=130)[
                    :, kb0:kb0 + nkb_half, 65:129], src3((64, 128))), "sVa", 1)

        ex_rec = {}
        g = 0
        yn_last = 0
        yo_rec = {}
        pend_mm = []
        av_pend = []
        for u in range(2):
            u0_sc = B.get("sSc")
            u0_av = g
            u0_ex = B.get("sEx")
            u0_va = B.get("sVa")
            sbl = {0: emit_qkload(u, 0, [("sCC1", 1), ("sSc", u0_sc),
                                         ("sEx", u0_ex)])}
            sbv = {0: emit_vload(u, 0, [("sCC1", 1), ("sVa", u0_va)])}
            va = {0: emit_vaug(u, 0, sbv[0], u0_av)}
            for qc in range(nqc):
                if qc == nqc // 2:
                    sbl[1] = emit_qkload(u, 1, [("sCC2", 1)])
                    sbv[1] = emit_vload(u, 1, [("sCC2", 1)])
                    va[1] = emit_vaug(u, 1, sbv[1], u0_av)
                hh = 0 if qc < nqc // 2 else 1
                nkb = 4 * qc + 4
                for kb in range(nkb):
                    diag = kb >= 4 * qc
                    qlo = 128 * (kb - 4 * qc) if diag else 0
                    par = g % 2
                    slot = g % 4
                    q0 = 512 * qc + qlo
                    n = 512 - qlo
                    w = [("sBl", sbl[hh])]
                    if g >= 2:
                        w.append(("sEx", ex_rec[g - 2]))
                    if g == 0:
                        w += [("sQKe", QKE_ALL), ("sQVe", QVE_ALL)]

                    def sc_fn(e, kb=kb, par=par, qlo=qlo, q0=q0, n=n):
                        e.matmul(pSS[:, 1024 * par + qlo: 1024 * par + 512],
                                 sK[0:64, 128 * kb:128 * (kb + 1)],
                                 sQ[0:64, q0:q0 + n],
                                 start=True, stop=True, tile_position=(0, 0))
                        return e.matmul(pSS[:, 1024 * par + 512 + qlo:
                                            1024 * par + 1024],
                                        sK[64:128, 128 * kb:128 * (kb + 1)],
                                        sQ[64:128, q0:q0 + n],
                                        start=True, stop=True,
                                        tile_position=(64, 0))
                    sc = B.op("t", w, sc_fn, "sSc", 1)
                    uq = u * nqc + qc
                    if pend_mm and kb == 1:
                        while av_pend and av_pend[0][0] < uq:
                            av_pend.pop(0)[1]()
                        for fn_ in pend_mm:
                            fn_()
                        pend_mm = []
                    while len(av_pend) >= 2:
                        av_pend.pop(0)[1]()
                    w = [("sSc", sc)]
                    if g >= 4:
                        w.append(("sAv", g - 3))
                    ex_rec[g] = B.op("c", w, lambda e, par=par, slot=slot:
                                     e.activation(
                                         sPP[:, 1024 * slot:1024 * (slot + 1)],
                                         pSS[:, 1024 * par:1024 * (par + 1)],
                                         AF.Exp, scale=0.125,
                                         bias=nbias[:, 0:1]), "sEx", 1)
                    mk = None
                    if diag:
                        def mk_fn(e, slot=slot, qlo=qlo):
                            e.tensor_tensor(
                                sPP[:, 1024 * slot + qlo:
                                    1024 * slot + qlo + 128],
                                sPP[:, 1024 * slot + qlo:
                                    1024 * slot + qlo + 128],
                                tri[:, :], Alu.mult)
                            return e.tensor_tensor(
                                sPP[:, 1024 * slot + 512 + qlo:
                                    1024 * slot + 512 + qlo + 128],
                                sPP[:, 1024 * slot + 512 + qlo:
                                    1024 * slot + 512 + qlo + 128],
                                tri[:, :], Alu.mult)
                        mk = B.op("v", [("sEx", ex_rec[g])], mk_fn, "sMk", 1)

                    def emit_av(g=g, kb=kb, slot=slot, qlo=qlo, mk=mk,
                                diag=diag, nkb=nkb, va_snap=va[hh], u=u, qc=qc,
                                yn_snap=yn_last):
                        w = [("sMk", mk)] if diag else [("sEx", ex_rec[g])]
                        w.append(("sVa", va_snap))
                        if kb == 0:
                            w.append(("sYn", yn_snap))
                            if u == 0 and qc == 0:
                                w.append(("sFoV", FFN1_OV))

                        def fn(e, kb=kb, slot=slot, qlo=qlo,
                               st=(kb == 0), sp=(kb == nkb - 1)):
                            p1 = sPP[:, 1024 * slot + qlo: 1024 * slot + 512]
                            p2 = sPP[:, 1024 * slot + 512 + qlo:
                                     1024 * slot + 1024]
                            e.matmul(pb4[0:65, qlo:512],
                                     sVaug[:, 130 * kb:130 * kb + 65], p1,
                                     start=st, stop=sp, tile_position=(0, 0))
                            return e.matmul(pb5[0:65, qlo:512],
                                            sVaug[:, 130 * kb + 65:
                                                  130 * (kb + 1)], p2,
                                            start=st, stop=sp,
                                            tile_position=(0, 0))
                        B.op("t", w, fn, "sAv", 1)
                    av_pend.append((u * nqc + qc, emit_av))
                    g += 1
                qpar = qc % 2

                def rc_fn(e, src, dst_c0):
                    with nc.allow_low_precision(reason="recip as bf16 matmul operand"):
                        return e.reciprocal(rdsb[64:65, dst_c0:dst_c0 + 512], src)
                B.op("v", [("sAv", g), ("sVi", VI_ALL)],
                     lambda e: rc_fn(e, pb4[64:65, 0:512], 0))
                nmv = B.op("v", [], lambda e: rc_fn(
                    e, pb5[64:65, 0:512], 512), "sNmV", 1)
                bc_target = B.get("sBc") + 2

                def mk_mm(nmv=nmv):
                    B.op("t", [("sNmV", nmv), ("sVi", VI_ALL)],
                         lambda e: e.matmul(pnorm[0:64, 0:512],
                                            SEL[64:96, 0:64],
                                            rdsb[64:96, 0:512],
                                            start=True, stop=True,
                                            tile_position=(64, 0)), "sBc", 1)
                    B.op("t", [], lambda e: e.matmul(pnorm[0:64, 512:1024],
                                                     SEL[64:96, 0:64],
                                                     rdsb[64:96, 512:1024],
                                                     start=True, stop=True,
                                                     tile_position=(64, 0)),
                         "sBc", 1)
                pend_mm = [mk_mm]
                B.op("v", [("sBc", bc_target)], lambda e: e.tensor_copy(
                    rnsb[0:64, 0:1024], pnorm[0:64, 0:1024]), "sNmC", 1)
                w = [("sYo", yo_rec.get(qpar, 0))]
                B.op("v", w, lambda e, qpar=qpar: e.tensor_tensor(
                    yh0[0:64, 512 * qpar:512 * qpar + 512],
                    pb4[0:64, 0:512], rnsb[0:64, 0:512], Alu.mult), "sYn", 1)
                yn_last = B.op("v", [], lambda e, qpar=qpar: e.tensor_tensor(
                    yh1[0:64, 512 * qpar:512 * qpar + 512],
                    pb5[0:64, 0:512], rnsb[0:64, 512:1024], Alu.mult),
                    "sYn", 1)
                q0 = 512 * qc
                Bt = q0 // hp
                j = 4 * u + (Bt if Bt < 4 else Bt - 4)
                cdst = (0 if Bt < 4 else hp) + q0 - hp * Bt
                w = [("sYn", yn_last), ("sCC2", 1)]
                B.op("s", w, lambda e, j=j, cdst=cdst, qpar=qpar: e.dma_start(
                    out=a2ac_in[128 * j:128 * j + 64, cdst:cdst + 512],
                    in_=yh0[0:64, 512 * qpar:512 * qpar + 512]), "sYo", 16)
                yo_rec[qpar] = B.op(
                    "s", [], lambda e, j=j, cdst=cdst, qpar=qpar: e.dma_start(
                        out=a2ac_in[128 * j + 64:128 * (j + 1), cdst:cdst + 512],
                        in_=yh1[0:64, 512 * qpar:512 * qpar + 512]), "sYo", 16)
        while av_pend:
            av_pend.pop(0)[1]()
        for fn_ in pend_mm:
            fn_()
        pend_mm = []
        B.op("g", [("sYo", B.get("sYo")), ("sCC2", 1)],
             lambda e: _a2a(e, a2ac_in, a2ac_out), "sCCy", 1)

        ATT_EX = B.get("sEx")
        ATT_NMV = B.get("sNmV")
        ATT_YO = B.get("sYo")
        ATT_YN = B.get("sYn")

        _emit(nc, blk, B, S)
        st_att.close()
        st_c = ExitStack()
        pt[0] = st_c.enter_context(nc.psum_tensor("ptC0", [128, 1024], BF16))
        pt[1] = st_c.enter_context(nc.psum_tensor("ptC1", [128, 1024], BF16))
        FMA_C = sbuf("FMA_C", [128, 8 * tpc], BF16, st_c)
        U1c = sbuf("U1c", [128, 8 * RANK], BF16, st_c)
        V1c = sbuf("V1c", [128, 2 * DFF], BF16, st_c)
        U2c = sbuf("U2c", [128, 32 * RANK], BF16, st_c)
        V2c = sbuf("V2c", [128, 2 * DM], BF16, st_c)
        sHc = sbuf("sHc", [128, 32 * nch2], BF16, st_c)
        sPc = sbuf("sPc", [128, 2 * nch2], BF16, st_c)
        sQ2c = sbuf("sQ2c", [128, 2 * nch2], BF16, st_c)

        for cc in range(8):
            B.op("g", [("sCCy", 1)] if cc == 0 else [],
                 lambda e, cc=cc: e.dma_start(
                     out=FMA_C[:, tpc * cc: tpc * (cc + 1)],
                     in_=a2ac_out[128 * cc:128 * (cc + 1), :]), "sCy", 16)
        CY_ALL = B.get("sCy")

        if DEBUG_DUMP2:
            d_fmc = nc.dram_tensor("d_fmc", [128, 8 * tpc], BF16,
                                   kind="ExternalOutput").ap()
            B.op("s", [("sCy", CY_ALL)],
                 lambda e: e.dma_start(out=d_fmc, in_=FMA_C[:, :]), "sOut", 16)
        gate_w2 = [("sNmV", ATT_NMV), ("sYo", ATT_YO), ("sYn", ATT_YN)]
        for dst, src in ffnw_dmas(U1c, V1c, U2c, V2c):
            B.op("g", gate_w2, lambda e, d=dst, s_=src: e.dma_start(out=d, in_=s_),
                 "sFw2", 16)
            gate_w2 = []
        FW2_ALL = B.get("sFw2")
        scl2 = B.op("v", [("sFw2", FW2_ALL), ("sNmV", ATT_NMV)],
                    lambda e: e.tensor_scalar_mul(V2c[:, :], V2c[:, :], 0.5),
                    "sScl", 1)

        pj_rec, pjv_rec = [], []
        pjv_tile = [None] * ntt
        i = 0
        for tl in range(ntt):
            for half in range(2):
                par = i % 2
                w = [("sCy", CY_ALL), ("sQe", QE_ALL)]
                if i < 2:
                    w += [("sEx", ATT_EX), ("sNmV", ATT_NMV)]
                else:
                    w.append(("sPjV", pjv_rec[i - 2]))

                def fn(e, tl=tl, half=half, par=par):
                    for cc in range(8):
                        ins = e.matmul(
                            PB[par][:, 0:512],
                            FMA_C[:, tpc * cc + 128 * tl: tpc * cc + 128 * (tl + 1)],
                            wprojT[:, 1024 * cc + 512 * half:
                                   1024 * cc + 512 * (half + 1)],
                            start=(cc == 0), stop=(cc == 7))
                    return ins
                pj_rec.append(B.op("t", w, fn, "sPjT", 1))
                pjv_rec.append(B.op("v", [("sPjT", pj_rec[i])],
                                   lambda e, tl=tl, half=half, par=par:
                                   e.scalar_tensor_tensor(
                                       x_sl(tl, 512 * half, 512), PB[par][:, 0:512],
                                       1.0, x_sl(tl, 512 * half, 512),
                                       Alu.mult, Alu.add), "sPjV", 1))
                if half == 1:
                    pjv_tile[tl] = pjv_rec[i]
                i += 1
        PJT_ALL = B.get("sPjT")
        PJV_ALL = B.get("sPjV")

        if DEBUG_STAGE == 2:
            for tt in range(ntt):
                B.op("s", [("sPjV", pjv_tile[tt])],
                     lambda e, tt=tt: e.dma_start(
                         out=out[128 * tt:128 * (tt + 1), :], in_=x_sl(tt)),
                     "sOut", 16)
            finish("s2")
            st_c.close()
            return nc

        gate3 = [[("sPjV", pjv_tile[tt])] for tt in range(ntt)]
        ln3 = emit_ln(gate3, gate3, [("sPjT", PJT_ALL)], FMA_C)
        bufs2 = (U1c, V1c, U2c, V2c, sPc, sHc, sQ2c, FMA_C)
        wwaits2 = [("sFw2", FW2_ALL), ("sScl", scl2), ("sPjV", PJV_ALL),
                   ("sEx", ATT_EX), ("sNmV", ATT_NMV), ("sYn", ATT_YN)]
        emit_ffn(nch2, ln3, bufs2, wwaits2, is_last=True)

        finish("s3")
        st_c.close()
    return nc


def _emit(nc, blk, B, S):
    eng_map = {"s": blk.sync, "g": blk.gpsimd, "t": blk.tensor,
               "v": blk.vector, "c": blk.scalar}
    for ename, deco in eng_map.items():
        ops = [o for o in B.ops if o.eng == ename]
        if not ops:
            continue

        def body(e, ops=ops):
            last = {}
            for o in ops:
                for (s, v) in o.waits:
                    if v > last.get(s, 0):
                        e.wait_ge(S[s], v)
                        last[s] = v
                ins = o.fn(e)
                if o.inc is not None:
                    assert ins is not None, f"inc op returned None ({ename})"
                    ins.then_inc(S[o.inc[0]], o.inc[1])
        deco(body)
    B.ops = []


_CACHE = {}


def _get_nc(tpc):
    if tpc not in _CACHE:
        _CACHE[tpc] = build_program(tpc)
    return _CACHE[tpc]


def make_in_maps(inputs, tpc):
    x = np.ascontiguousarray(np.asarray(inputs["L0_attn_x"], dtype=np.float32))
    Bn, T, C = x.shape
    xf = x.reshape(Bn * T, C)
    tri = np.triu(np.ones((128, 128), dtype=np.float32)).astype(nbf)
    ident = np.eye(128, dtype=np.float32).astype(nbf)
    identf = np.eye(128, dtype=np.float32)
    shared = dict(
        w_attn=np.ascontiguousarray(np.asarray(inputs["L0_attn_w_attn"], np.float32)),
        w_proj=np.ascontiguousarray(np.asarray(inputs["L0_attn_w_proj"], np.float32)),
        U1=np.ascontiguousarray(np.asarray(inputs["L0_ffn_U1"], np.float32)),
        V1=np.ascontiguousarray(np.asarray(inputs["L0_ffn_V1"], np.float32)),
        b1=np.ascontiguousarray(np.asarray(inputs["L0_ffn_b1"], np.float32)),
        U2=np.ascontiguousarray(np.asarray(inputs["L0_ffn_U2"], np.float32)),
        V2=np.ascontiguousarray(np.asarray(inputs["L0_ffn_V2"], np.float32)),
        b2=np.ascontiguousarray(np.asarray(inputs["L0_ffn_b2"], np.float32)),
        tri=tri, ident=ident, identf=identf,
    )
    hp = tpc // 2
    in_maps = []
    for c in range(NCORES):
        b, j = c // 4, c % 4
        base = b * (4 * tpc)
        rows = np.concatenate([
            xf[base + hp * j: base + hp * (j + 1)],
            xf[base + hp * (j + 4): base + hp * (j + 5)],
        ], axis=0)
        m = dict(shared)
        m["x_shard"] = np.ascontiguousarray(rows)
        in_maps.append(m)
    return in_maps


def kernel(**inputs):
    x = np.asarray(inputs["L0_attn_x"])
    Bn, T, C = x.shape
    tpc = (Bn * T) // NCORES
    hp = tpc // 2
    in_maps = make_in_maps(inputs, tpc)
    nc = _get_nc(tpc)
    res = run_bass_kernel_spmd(nc, in_maps, core_ids=list(range(NCORES)))
    out = np.empty((Bn * T, C), dtype=np.float32)
    for c in range(NCORES):
        shard = np.asarray(res.results[c]["out_shard"])
        b, j = c // 4, c % 4
        base = b * (4 * tpc)
        out[base + hp * j: base + hp * (j + 1)] = shard[0:hp]
        out[base + hp * (j + 4): base + hp * (j + 5)] = shard[hp:tpc]
    return out.reshape(Bn, T, C)


# version 27
# speedup vs baseline: 1.0058x; 1.0047x over previous
import numpy as np
import ml_dtypes
import concourse.bass as bass
import concourse.mybir as mybir
from concourse.bass_utils import run_bass_kernel_spmd
from concourse.alu_op_type import AluOpType
from concourse.aot_env import allow_ici_all_to_all
from contextlib import ExitStack

F32 = mybir.dt.float32
BF16 = mybir.dt.bfloat16
nbf = ml_dtypes.bfloat16
AF = mybir.ActivationFunctionType
Alu = AluOpType

NCORES = 8
DM = 1024
DFF = 4096
RANK = 256
MAGIC = 12582912.0
EXP_BIAS = -12.0
DEBUG_STAGE = 3
DEBUG_DUMP = False
DEBUG_DUMP2 = False


class Op:
    __slots__ = ("eng", "waits", "fn", "inc")

    def __init__(self, eng, waits, fn, inc):
        self.eng = eng
        self.waits = waits
        self.fn = fn
        self.inc = inc


SERIAL = False


class Builder:
    def __init__(self):
        self.ops = []
        self.cnt = {}

    def op(self, eng, waits, fn, inc=None, incn=1):
        if SERIAL:
            waits = list(waits) + [(s, v) for (s, v) in self.cnt.items()]
        w = [(s, v) for (s, v) in waits if v and v > 0]
        if inc is None:
            self.ops.append(Op(eng, w, fn, None))
            return None
        self.cnt[inc] = self.cnt.get(inc, 0) + incn
        self.ops.append(Op(eng, w, fn, (inc, incn)))
        return self.cnt[inc]

    def bump(self, name, n):
        self.cnt[name] = self.cnt.get(name, 0) + n
        return self.cnt[name]

    def get(self, s):
        return self.cnt.get(s, 0)


def _a2a(e, tin, tout):
    with allow_ici_all_to_all():
        return e.collective_compute(
            "AllToAll", mybir.AluOpType.bypass,
            replica_groups=[list(range(NCORES))],
            ins=[tin.ap().opt()], outs=[tout.ap().opt()])


def build_program(tpc):
    tb = 4 * tpc
    ntt = tpc // 128
    nqc = tb // 512
    nch1 = min(256, tpc)
    nch2 = min(256, tpc)
    nqk = min(512, tpc)
    assert tpc % 256 == 0 and tb % 512 == 0

    nc = bass.Bass("TRN2", target_bir_lowering=False)

    x_in = nc.dram_tensor("x_shard", [tpc, DM], F32, kind="ExternalInput").ap()
    w_attn = nc.dram_tensor("w_attn", [3 * DM, DM], F32, kind="ExternalInput").ap()
    w_proj = nc.dram_tensor("w_proj", [DM, DM], F32, kind="ExternalInput").ap()
    U1 = nc.dram_tensor("U1", [DM, RANK], F32, kind="ExternalInput").ap()
    V1 = nc.dram_tensor("V1", [RANK, DFF], F32, kind="ExternalInput").ap()
    b1 = nc.dram_tensor("b1", [DFF], F32, kind="ExternalInput").ap()
    U2 = nc.dram_tensor("U2", [DFF, RANK], F32, kind="ExternalInput").ap()
    V2 = nc.dram_tensor("V2", [RANK, DM], F32, kind="ExternalInput").ap()
    b2 = nc.dram_tensor("b2", [DM], F32, kind="ExternalInput").ap()
    tri_in = nc.dram_tensor("tri", [128, 128], BF16, kind="ExternalInput").ap()
    idf_in = nc.dram_tensor("identf", [128, 128], F32, kind="ExternalInput").ap()
    id_in = nc.dram_tensor("ident", [128, 128], BF16, kind="ExternalInput").ap()
    out = nc.dram_tensor("out_shard", [tpc, DM], F32, kind="ExternalOutput").ap()

    hp_ = tpc // 2
    a2a1_in = nc.dram_tensor("a2a1_in", [NCORES * 384, hp_], BF16)
    a2a1_out = nc.dram_tensor("a2a1_out", [NCORES * 384, hp_], BF16)
    a2a2_in = nc.dram_tensor("a2a2_in", [NCORES * 384, hp_], BF16)
    a2a2_out = nc.dram_tensor("a2a2_out", [NCORES * 384, hp_], BF16)
    a2ac_in = nc.dram_tensor("a2ac_in", [NCORES * 128, tpc], BF16)
    a2ac_out = nc.dram_tensor("a2ac_out", [NCORES * 128, tpc], BF16)

    B = Builder()
    SEMS = ["sX", "sW", "sScl", "sLnV", "sLnC", "sLnSt", "sLnAp", "sLnTr",
            "sLnEv", "sFp", "sFpe", "sFh", "sFg", "sFq2", "sFq2e", "sFo",
            "sFoV", "sQl", "sAbs", "sQs", "sQt", "sQv1", "sQg", "sQv3",
            "sQtr", "sQe", "sQK", "sQKe", "sQKd", "sQV", "sQVe", "sQVd",
            "sCC1", "sCC2", "sCCy", "sBl", "sBv", "sSc", "sEx", "sMk", "sAv",
            "sNmC", "sNmV", "sYn", "sYo", "sCy", "sFw2", "sPjT", "sPjV", "sOut",
            "sVa", "sVi", "sBc"]

    with ExitStack() as top:
        def sbuf(name, shape, dt, stack=None):
            return (stack or top).enter_context(nc.sbuf_tensor(name, shape, dt))

        X = sbuf("X", [128, ntt * DM], F32)
        xlntok = sbuf("xlntok", [128, 2 * DM], BF16)
        xdumpV = sbuf("xdumpV", [128, DM], BF16)
        xdumpC = sbuf("xdumpC", [128, DM], BF16)
        SUMX = sbuf("SUMX", [128, ntt], F32)
        SUMQ = sbuf("SUMQ", [128, ntt], F32)
        MEAN = sbuf("MEAN", [128, ntt], F32)
        MSQ = sbuf("MSQ", [128, ntt], F32)
        VAR = sbuf("VAR", [128, ntt], F32)
        SD = sbuf("SD", [128, ntt], F32)
        RST = sbuf("RST", [128, ntt], F32)
        tri = sbuf("tri_s", [128, 128], BF16)
        idb = sbuf("idb", [128, 128], BF16)
        onesb = sbuf("onesb", [128, 128], BF16)
        ones1 = sbuf("ones1", [1, 128], BF16)
        onesf = sbuf("onesf", [128, 128], F32)
        b1sb = sbuf("b1sb", [128, DFF // 128], F32)
        b1T = sbuf("b1T", [32, 128], F32)
        idf = sbuf("idf", [128, 128], F32)
        b2rowh = sbuf("b2rowh", [1, DM], BF16)
        scaleA = sbuf("scaleA", [128, 1], F32)
        iscaleA = sbuf("iscaleA", [128, 1], F32)
        scaleP = sbuf("scaleP", [128, 1], F32)
        iscaleP = sbuf("iscaleP", [128, 1], F32)
        PARTS = sbuf("PARTS", [128, 32], F32)
        PRED = sbuf("PRED", [128, 2], F32)
        epsb = sbuf("epsb", [128, 1], F32)
        wprojT = sbuf("wprojT", [128, 8192], BF16)
        nbias = sbuf("nbias", [128, 1], F32)
        RDBK = sbuf("RDBK", [128, 2], F32)

        pSS = top.enter_context(nc.psum_tensor("pSS", [128, 2048], F32))
        pb4 = top.enter_context(nc.psum_tensor("pb4", [128, 512], F32))
        pb5 = top.enter_context(nc.psum_tensor("pb5", [128, 512], F32))
        st_ptA = ExitStack()
        pt = [st_ptA.enter_context(nc.psum_tensor(f"ptA{i}", [128, 1024], BF16))
              for i in range(2)]
        PB = [pSS[:, 0:512], pSS[:, 512:1024], pSS[:, 1024:1536],
              pSS[:, 1536:2048], pb4[:, 0:512], pb5[:, 0:512]]

        S = {s: top.enter_context(nc.semaphore(s)) for s in SEMS}
        blk = top.enter_context(nc.Block())

        def x_sl(tt, c0=0, n=DM):
            return X[:, DM * tt + c0: DM * tt + c0 + n]

        def finish(stage_tag):
            B.op("g", [("sOut", B.get("sOut"))], lambda e: None)
            _emit(nc, blk, B, S)

        st_fma = ExitStack()
        FMA_A = sbuf("FMA_A", [128, 8 * tpc], BF16, st_fma)
        st_ffn1 = ExitStack()
        U1a = sbuf("U1a", [128, 8 * RANK], BF16, st_ffn1)
        V1a = sbuf("V1a", [128, 2 * DFF], BF16, st_ffn1)
        U2a = sbuf("U2a", [128, 32 * RANK], BF16, st_ffn1)
        V2a = sbuf("V2a", [128, 2 * DM], BF16, st_ffn1)
        sHa = sbuf("sHa", [128, 32 * nch1], BF16, st_ffn1)
        sPa = sbuf("sPa", [128, 2 * nch1], BF16, st_ffn1)
        sQ2a = sbuf("sQ2a", [128, 2 * nch1], BF16, st_ffn1)
        wv1 = sbuf("wv1", [128, DM], F32, st_ffn1)

        B.op("s", [], lambda e: e.dma_start(
            out=X[:, :].rearrange("p (tt c) -> p tt c", c=DM),
            in_=x_in.rearrange("(tt p) c -> p tt c", p=128)), "sX", 16)

        def wtile_src(j):
            return (w_attn[128 * j:128 * (j + 1), :] if j < 24
                    else w_proj[128 * (j - 24):128 * (j - 23), :])

        for j in range(32):
            w = [] if j < 1 else [("sAbs", j)]
            B.op("s", w, lambda e, j=j: e.dma_start(
                out=wv1[:, :], in_=wtile_src(j)), "sQl", 16)

        def ffnw_dmas(U1d, V1d, U2d, V2d):
            return [
                (U1d[:, :].rearrange("p (cc r) -> p cc r", r=RANK),
                 U1.rearrange("(cc p) r -> p cc r", p=128)),
                (V1d[:, :].rearrange("p (rt f) -> p rt f", f=DFF),
                 V1.rearrange("(rt p) f -> p rt f", p=128)),
                (U2d[:, :].rearrange("p (ft r) -> p ft r", r=RANK),
                 U2.rearrange("(ft p) r -> p ft r", p=128)),
                (V2d[:, :].rearrange("p (rt c) -> p rt c", c=DM),
                 V2.rearrange("(rt p) c -> p rt c", p=128)),
            ]

        for dst, src in [
            (b2rowh[:, :], b2.rearrange("(a c) -> a c", a=1)),
            (tri[:, :], tri_in),
            (idb[:, :], id_in),
            (idf[:, :], idf_in),
        ] + ffnw_dmas(U1a, V1a, U2a, V2a):
            B.op("g", [], lambda e, d=dst, s_=src: e.dma_start(out=d, in_=s_),
                 "sW", 16)
        B.op("g", [], lambda e: e.dma_start(
            out=b1T[:, :], in_=b1.rearrange("(ft p) -> ft p", p=128)), "sW", 16)
        B.op("g", [], lambda e: e.memset(onesb[:, :], 1.0), "sW", 1)
        B.op("g", [], lambda e: e.memset(ones1[:, :], 1.0), "sW", 1)
        B.op("g", [], lambda e: e.memset(onesf[:, :], 1.0), "sW", 1)
        B.op("g", [], lambda e: e.memset(epsb[:, :], 1e-5), "sW", 1)
        B.op("g", [], lambda e: e.memset(nbias[:, :], EXP_BIAS), "sW", 1)
        bt = B.op("t", [("sW", B.get("sW"))],
                  lambda e: e.matmul(pb4[:, 0:128], b1T[:, :], idf[0:32, :],
                                     start=True, stop=True), "sW", 1)
        B.op("v", [("sW", bt)],
             lambda e: e.tensor_copy(b1sb[:, :], pb4[:, 0:32]), "sW", 1)
        W_ALL = B.get("sW")

        B.op("v", [("sW", W_ALL)],
             lambda e: e.tensor_scalar_mul(V2a[:, :], V2a[:, :], 0.5))
        scl1 = B.op("v", [], lambda e: e.tensor_scalar_mul(
            b2rowh[:, :], b2rowh[:, :], 0.5), "sScl", 1)

        def emit_ln(gate_v, gate_c, fma_war, FMAd, tr_extra=()):
            r = {"ev": [], "ap": [], "tr": []}
            for tt in range(ntt):
                B.op("v", [("sX", 16)] + gate_v[tt],
                     lambda e, tt=tt: e.tensor_scalar(
                         xdumpV[:, :], x_sl(tt), 1.0, 0.0, Alu.mult, Alu.add,
                         accum_out=SUMX[:, tt:tt + 1]), "sLnV", 1)
            for tt in range(ntt):
                B.op("c", [("sX", 16)] + gate_c[tt],
                     lambda e, tt=tt: e.activation(
                         xdumpC[:, :], x_sl(tt), AF.Square,
                         accum_out=SUMQ[:, tt:tt + 1]), "sLnC", 1)
            vc = B.get("sLnC")
            va = B.get("sLnV")
            st0 = B.op("c", [("sLnV", va)],
                       lambda e: e.activation(MEAN[:, :], SUMX[:, :], AF.Copy,
                                              scale=1.0 / DM), "sLnSt", 1)
            B.op("v", [("sLnSt", st0)],
                 lambda e: e.tensor_tensor(MSQ[:, :], MEAN[:, :], MEAN[:, :], Alu.mult))
            B.op("v", [], lambda e: e.tensor_copy(xdumpV[:, :], xdumpV[:, :]))
            st1 = B.op("v", [("sLnC", vc)],
                       lambda e: e.scalar_tensor_tensor(
                           VAR[:, :], SUMQ[:, :], 1.0 / DM, MSQ[:, :],
                           Alu.mult, Alu.subtract), "sLnSt", 1)
            st2 = B.op("c", [("sLnSt", st1)],
                       lambda e: e.activation(SD[:, :], VAR[:, :], AF.Sqrt, bias=epsb[:, 0:1]),
                       "sLnSt", 1)
            B.op("v", [("sLnSt", st2)],
                 lambda e: e.reciprocal(RST[:, :], SD[:, :]))
            B.op("v", [], lambda e: e.tensor_copy(xdumpV[:, :], xdumpV[:, :]))
            for tt in range(ntt):
                par = tt % 2
                w = [("sLnTr", r["tr"][tt - 2])] if tt >= 2 else []
                r["ap"].append(B.op("v", w, lambda e, tt=tt, par=par: e.tensor_scalar(
                    xlntok[:, DM * par: DM * par + DM], x_sl(tt),
                    MEAN[:, tt:tt + 1], RST[:, tt:tt + 1],
                    Alu.subtract, Alu.mult), "sLnAp", 1))

                def tr_fn(e, tt=tt, par=par):
                    for cc in range(8):
                        ins = e.transpose(
                            pt[par][:, 128 * cc:128 * (cc + 1)],
                            xlntok[:, DM * par + 128 * cc: DM * par + 128 * (cc + 1)],
                            idb[:, :])
                    return ins
                w = [("sLnAp", r["ap"][tt]), ("sW", W_ALL)]
                if tt < 2:
                    w += list(tr_extra)
                else:
                    w.append(("sLnEv", r["ev"][tt - 2]))
                r["tr"].append(B.op("t", w, tr_fn, "sLnTr", 1))

                def ev_fn(e, tt=tt, par=par, FMAd=FMAd):
                    dst = FMAd[:, :].rearrange("p (cc t) -> p cc t", t=tpc)[
                        :, :, 128 * tt:128 * (tt + 1)]
                    src = pt[par][:, :].rearrange("p (cc r) -> p cc r", r=128)
                    return e.tensor_copy(dst, src)
                w = [("sLnTr", r["tr"][tt])] + (fma_war if tt == 0 else [])
                r["ev"].append(B.op("v", w, ev_fn, "sLnEv", 1))
            return r

        def emit_ffn(nch, ln, bufs, wwaits, is_last, extra_c=None):
            U1w, V1w, U2w, V2w, sPb, sHb, sQ2b, FMAd = bufs
            nchunks = tpc // nch
            ntl = nch // 128
            r = {"ov_tile": [None] * ntt}
            prev_q2e = [0, 0]
            prev_g = {}
            prev_ov = {}
            prev_o_last = 0
            prev_h_last = 0
            for n in range(nchunks):
                t0 = n * nch
                p_rec, h_rec, q2_rec, o_rec, g_list = [], [], [], [], []
                for rt in range(2):
                    w = [("sLnEv", ln["ev"][(t0 + nch) // 128 - 1]),
                         ("sFq2e", prev_q2e[rt])] + wwaits

                    def p_fn(e, rt=rt, t0=t0, U1w=U1w, FMAd=FMAd):
                        for cc in range(8):
                            ins = e.matmul(
                                PB[rt][:, 0:nch],
                                U1w[:, RANK * cc + 128 * rt: RANK * cc + 128 * (rt + 1)],
                                FMAd[:, tpc * cc + t0: tpc * cc + t0 + nch],
                                start=(cc == 0), stop=(cc == 7))
                        return ins
                    p_rec.append(B.op("t", w, p_fn, "sFp", 1))
                pe_rec = []
                for rt in range(2):
                    w = [("sFp", p_rec[rt]), ("sFh", prev_h_last)]
                    pe_rec.append(B.op("c", w, lambda e, rt=rt, sPb=sPb:
                                       e.activation(
                                           sPb[:, nch * rt: nch * rt + nch],
                                           PB[rt][:, 0:nch], AF.Copy), "sFpe", 1))
                for ft in range(32):
                    par = 2 + (ft % 2)
                    w = [("sFpe", pe_rec[1])]
                    if par in prev_g:
                        w.append(("sFg", prev_g[par]))

                    def h_fn(e, ft=ft, par=par, V1w=V1w, sPb=sPb):
                        for rt in range(2):
                            ins = e.matmul(
                                PB[par][:, 0:nch],
                                V1w[:, DFF * rt + 128 * ft: DFF * rt + 128 * (ft + 1)],
                                sPb[:, nch * rt: nch * rt + nch],
                                start=(rt == 0), stop=(rt == 1))
                        return ins
                    h_rec.append(B.op("t", w, h_fn, "sFh", 1))
                    g = B.op("c", [("sFh", h_rec[ft])],
                             lambda e, ft=ft, par=par, sHb=sHb: e.activation(
                                 sHb[:, nch * ft: nch * ft + nch], PB[par][:, 0:nch],
                                 AF.Gelu_apprx_tanh, bias=b1sb[:, ft:ft + 1]),
                             "sFg", 1)
                    prev_g[par] = g
                    g_list.append(g)
                for rt in range(2):
                    def q2_fn(e, rt=rt, g_list=tuple(g_list), U2w=U2w, sHb=sHb):
                        for ft in range(32):
                            e.wait_ge(S["sFg"], g_list[ft])
                            ins = e.matmul(
                                PB[rt][:, 0:nch],
                                U2w[:, RANK * ft + 128 * rt: RANK * ft + 128 * (rt + 1)],
                                sHb[:, nch * ft: nch * ft + nch],
                                start=(ft == 0), stop=(ft == 31))
                        return ins
                    w = [("sFpe", pe_rec[rt])]
                    q2_rec.append(B.op("t", w, q2_fn, "sFq2", 1))
                q2e_rec = []
                for rt in range(2):
                    w = [("sFq2", q2_rec[rt]), ("sFo", prev_o_last)]
                    q2e_rec.append(B.op("v", w, lambda e, rt=rt, sPb=sPb, sQ2b=sQ2b:
                                        e.tensor_copy(
                                            sQ2b[:, nch * rt: nch * rt + nch],
                                            PB[rt][:, 0:nch]), "sFq2e", 1))
                    prev_q2e[rt] = q2e_rec[rt]
                for tl in range(ntl):
                    for half in range(2):
                        idx = tl * 2 + half
                        par = 4 + (idx % 2)
                        gtt = t0 // 128 + tl
                        w = [("sFq2e", q2e_rec[1])] + wwaits
                        if par in prev_ov:
                            w.append(("sFoV", prev_ov[par]))

                        def o_fn(e, tl=tl, half=half, par=par, sQ2b=sQ2b, V2w=V2w):
                            for rt in range(2):
                                e.matmul(
                                    PB[par][:, 0:512],
                                    sQ2b[:, nch * rt + 128 * tl: nch * rt + 128 * (tl + 1)],
                                    V2w[:, DM * rt + 512 * half: DM * rt + 512 * (half + 1)],
                                    start=(rt == 0), stop=False)
                            return e.matmul(
                                PB[par][:, 0:512], ones1[0:1, 0:128],
                                b2rowh[0:1, 512 * half: 512 * (half + 1)],
                                start=False, stop=True)
                        o = B.op("t", w, o_fn, "sFo", 1)
                        o_rec.append(o)
                        prev_o_last = o
                        ov = B.op("v", [("sFo", o)],
                                  lambda e, gtt=gtt, half=half, par=par:
                                  e.scalar_tensor_tensor(
                                      x_sl(gtt, 512 * half, 512), PB[par][:, 0:512],
                                      1.0, x_sl(gtt, 512 * half, 512),
                                      Alu.mult, Alu.add), "sFoV", 1)
                        prev_ov[par] = ov
                        if half == 1:
                            r["ov_tile"][gtt] = ov
                            if is_last:
                                B.op("s", [("sFoV", ov)],
                                     lambda e, gtt=gtt: e.dma_start(
                                         out=out[128 * gtt:128 * (gtt + 1), :],
                                         in_=x_sl(gtt)), "sOut", 16)
                prev_h_last = h_rec[-1]
                if extra_c:
                    take = min(4, len(extra_c))
                    for fn_ in extra_c[:take]:
                        fn_()
                    del extra_c[:take]
            return r

        abs_rec = []

        def _mk_abs(j):
            def emit():
                abs_rec.append(B.op("c", [("sQl", 16 * (j + 1))],
                                    lambda e, j=j: e.activation(
                                        xdumpC[:, :], wv1[:, :], AF.Abs,
                                        accum_out=PARTS[:, j:j + 1]), "sAbs", 1))
            return emit
        abs_work = [_mk_abs(j) for j in range(32)]

        ln1 = emit_ln([[]] * ntt, [[]] * ntt, [], FMA_A)
        if DEBUG_DUMP:
            d_fma = nc.dram_tensor("d_fma", [128, 8 * tpc], BF16,
                                   kind="ExternalOutput").ap()
            d_st = nc.dram_tensor("d_st", [128, 8 * ntt], F32,
                                  kind="ExternalOutput").ap()
            B.op("s", [("sLnEv", B.get("sLnEv"))],
                 lambda e: e.dma_start(out=d_fma, in_=FMA_A[:, :]), "sOut", 16)

            def st_fn(e):
                for k, tns in enumerate([SUMX, SUMQ, MEAN, MSQ, VAR, SD, RST]):
                    e.dma_start(out=d_st[:, ntt * k: ntt * (k + 1)],
                                in_=tns[:, :]).then_inc(S["sOut"], 16)
                return None
            B.op("s", [("sLnAp", B.get("sLnAp"))], st_fn)
            B.bump("sOut", 16 * 7)
        bufs1 = (U1a, V1a, U2a, V2a, sPa, sHa, sQ2a, FMA_A)
        ffn1 = emit_ffn(nch1, ln1, bufs1, [("sW", W_ALL), ("sScl", scl1)],
                        is_last=(DEBUG_STAGE == 1), extra_c=abs_work)
        if DEBUG_DUMP:
            d_p = nc.dram_tensor("d_p", [128, 2 * nch1], BF16,
                                 kind="ExternalOutput").ap()
            d_h = nc.dram_tensor("d_h", [128, 32 * nch1], BF16,
                                 kind="ExternalOutput").ap()
            d_q2 = nc.dram_tensor("d_q2", [128, 2 * nch1], BF16,
                                  kind="ExternalOutput").ap()
            B.op("s", [("sFpe", B.get("sFpe"))],
                 lambda e: e.dma_start(out=d_p, in_=sPa[:, :]), "sOut", 16)
            B.op("s", [("sFg", B.get("sFg"))],
                 lambda e: e.dma_start(out=d_h, in_=sHa[:, :]), "sOut", 16)
            B.op("s", [("sFq2e", B.get("sFq2e"))],
                 lambda e: e.dma_start(out=d_q2, in_=sQ2a[:, :]), "sOut", 16)

        if DEBUG_STAGE == 1:
            finish("s1")
            st_ffn1.close()
            st_fma.close()
            return nc

        FFN1_OV = B.get("sFoV")
        FFN1_Q2E = B.get("sFq2e")
        FFN1_G = B.get("sFg")
        FFN1_FP = B.get("sFp")

        gate = [[("sFoV", ffn1["ov_tile"][tt])] for tt in range(ntt)]
        ln2 = emit_ln(gate, gate, [("sFp", FFN1_FP)], FMA_A)
        LN2_EV_ = B.get("sLnEv")

        _emit(nc, blk, B, S)
        st_ffn1.close()
        st_quant = ExitStack()
        wave = sbuf("wave", [128, 2 * DM], F32, st_quant)
        Q32 = sbuf("Q32", [128, 2 * DM], F32, st_quant)
        WQ = sbuf("WQ", [128, 2 * DM], BF16, st_quant)
        wattnT = sbuf("wattnT", [128, 24576], BF16, st_quant)

        for fn_ in abs_work:
            fn_()
        abs_work.clear()
        B.op("v", [("sAbs", abs_rec[-1])], lambda e: e.tensor_reduce(
            PRED[:, 0:1], PARTS[:, 0:24], mybir.AxisListType.X, Alu.add))
        qs1 = B.op("v", [], lambda e: e.tensor_reduce(
            PRED[:, 1:2], PARTS[:, 24:32], mybir.AxisListType.X, Alu.add), "sQs", 1)
        qt1 = B.op("t", [("sQs", qs1), ("sW", W_ALL), ("sFq2e", FFN1_Q2E),
                         ("sFoV", FFN1_OV)],
                   lambda e: (e.matmul(pSS[:, 0:1], onesf[:, :], PRED[:, 0:1],
                                       start=True, stop=True),
                              e.matmul(pSS[:, 512:513], onesf[:, :], PRED[:, 1:2],
                                       start=True, stop=True))[-1], "sQt", 1)
        B.op("v", [("sQt", qt1)], lambda e: e.tensor_scalar(
            scaleA[:, :], pSS[:, 0:1], 1.0 / (3072 * 1024), 1e-8, Alu.mult, Alu.max))
        B.op("v", [], lambda e: e.tensor_scalar(
            scaleP[:, :], pSS[:, 512:513], 1.0 / (1024 * 1024), 1e-8, Alu.mult, Alu.max))
        B.op("v", [], lambda e: e.tensor_copy(xdumpV[:, :], xdumpV[:, :]))
        B.op("v", [], lambda e: e.reciprocal(iscaleA[:, :], scaleA[:, :]))
        B.op("v", [], lambda e: e.tensor_copy(xdumpV[:, :], xdumpV[:, :]))
        qs2 = B.op("v", [], lambda e: e.reciprocal(iscaleP[:, :], scaleP[:, :]),
                   "sQs", 1)

        c1_rec, c3_rec, tr_rec, ev_rec = [], [], [], []

        def q_c1(j):
            par = j % 2
            isc = iscaleA if j < 24 else iscaleP
            w = [("sQl", 16 * (32 + j // 2 + 1)), ("sQs", qs2)]
            c1_rec.append(B.op("v", w, lambda e, par=par, isc=isc:
                               e.tensor_scalar(
                                   Q32[:, DM * par: DM * (par + 1)],
                                   wave[:, DM * par: DM * (par + 1)],
                                   isc[:, :], 2.25, Alu.mult, Alu.min), "sQv1", 1))

        def q_tail(j):
            par = j % 2
            sc_ = scaleA if j < 24 else scaleP
            B.op("v", [], lambda e, par=par: e.tensor_scalar(
                Q32[:, DM * par: DM * (par + 1)],
                Q32[:, DM * par: DM * (par + 1)],
                -2.25, MAGIC, Alu.max, Alu.add))
            B.op("v", [], lambda e: e.tensor_copy(xdumpV[:, :], xdumpV[:, :]))
            w = [("sQtr", tr_rec[j - 2])] if j >= 2 else []
            c3_rec.append(B.op("v", w, lambda e, par=par, sc_=sc_: e.tensor_scalar(
                WQ[:, DM * par: DM * (par + 1)],
                Q32[:, DM * par: DM * (par + 1)],
                -MAGIC, sc_[:, :], Alu.add, Alu.mult), "sQv3", 1))

            def tr_fn(e, par=par):
                for cc in range(8):
                    ins = e.transpose(
                        pt[par][:, 128 * cc:128 * (cc + 1)],
                        WQ[:, DM * par + 128 * cc: DM * par + 128 * (cc + 1)],
                        idb[:, :])
                return ins
            w = [("sQv3", c3_rec[j]), ("sLnEv", LN2_EV_)]
            if j >= 2:
                w.append(("sQe", ev_rec[j - 2]))
            tr_rec.append(B.op("t", w, tr_fn, "sQtr", 1))

            def ev_fn(e, j=j, par=par):
                if j < 24:
                    dst = wattnT[:, :].rearrange("p (cc o) -> p cc o", o=3072)[
                        :, :, 128 * j:128 * (j + 1)]
                else:
                    dst = wprojT[:, :].rearrange("p (cc o) -> p cc o", o=1024)[
                        :, :, 128 * (j - 24):128 * (j - 23)]
                src_ = pt[par][:, :].rearrange("p (cc r) -> p cc r", r=128)
                return e.activation(dst, src_, AF.Copy)
            ev_rec.append(B.op("c", [("sQtr", tr_rec[j])], ev_fn, "sQe", 1))

        def wpair_src(p):
            j = 2 * p
            if j < 24 and j + 1 < 24:
                return w_attn[128 * j:128 * (j + 2), :]
            return w_proj[128 * (j - 24):128 * (j - 22), :]

        for p in range(16):
            w = [("sQs", qs2)] if p == 0 else [("sQv1", c1_rec[2 * p - 1])]
            B.op("s" if p % 2 == 0 else "g", w, lambda e, p=p: e.dma_start(
                out=wave[:, :].rearrange("q (j c) -> q j c", c=DM),
                in_=wpair_src(p).rearrange("(j q) c -> q j c", q=128)),
                "sQl", 16)
            if p >= 1:
                q_tail(2 * p - 2)
            q_c1(2 * p)
            if p >= 1:
                q_tail(2 * p - 1)
            q_c1(2 * p + 1)
        q_tail(30)
        q_tail(31)
        QE_ALL = B.get("sQe")

        if DEBUG_DUMP2:
            d_wat = nc.dram_tensor("d_wat", [128, 24576], BF16,
                                   kind="ExternalOutput").ap()
            d_wpt = nc.dram_tensor("d_wpt", [128, 8192], BF16,
                                   kind="ExternalOutput").ap()
            B.op("s", [("sQe", B.get("sQe"))],
                 lambda e: e.dma_start(out=d_wat, in_=wattnT[:, :]), "sOut", 16)
            B.op("s", [("sQe", B.get("sQe"))],
                 lambda e: e.dma_start(out=d_wpt, in_=wprojT[:, :]), "sOut", 16)

        st_qkv = ExitStack()
        qkstage = sbuf("qkstage", [128, 2 * tpc], BF16, st_qkv)
        vstage = sbuf("vstage", [128, 2 * DM], BF16, st_qkv)
        hp = tpc // 2
        nqk_h = min(512, hp)
        nchq_h = hp // nqk_h
        war_ffn1 = [("sFq2e", FFN1_Q2E), ("sFg", FFN1_G), ("sFoV", FFN1_OV)]
        LN2_EV = LN2_EV_
        LN2EVL = ln2["ev"]

        def vflat_in(a2a):
            return (a2a.rearrange("(j s) c -> j s c", j=NCORES)[:, 256:384, :]
                    .rearrange("j s c -> j (s c)")
                    .rearrange("j (t d) -> t j d", d=128))

        qv_rec, qve_rec, qvd_rec = [], [], []
        qk_rec, qke_rec = [], []
        qkd_rec = {}
        iv = [0]
        iq = [0]

        def emit_v_tile(tl, half):
            for hf2 in range(2):
                i = iv[0]
                par = 2 + (i % 2)
                w = [("sQe", ev_rec[19 + 4 * hf2]), ("sLnEv", LN2EVL[tl])] \
                    + war_ffn1
                if i >= 2:
                    w.append(("sQVe", qve_rec[i - 2]))

                def qv_fn(e, tl=tl, hf2=hf2, par=par):
                    for cc in range(8):
                        ins = e.matmul(
                            PB[par][:, 0:512],
                            FMA_A[:, tpc * cc + 128 * tl: tpc * cc + 128 * (tl + 1)],
                            wattnT[:, 3072 * cc + 2048 + 512 * hf2:
                                   3072 * cc + 2048 + 512 * (hf2 + 1)],
                            start=(cc == 0), stop=(cc == 7))
                    return ins
                qv_rec.append(B.op("t", w, qv_fn, "sQV", 1))
                w = [("sQV", qv_rec[i])]
                if tl >= 2 and hf2 == 0:
                    w.append(("sQVd", 16 * (tl - 1)))
                qve_rec.append(B.op("c", w, lambda e, tl=tl, hf2=hf2, par=par:
                                    e.activation(
                                        vstage[:, DM * (tl % 2) + 512 * hf2:
                                               DM * (tl % 2) + 512 * (hf2 + 1)],
                                        PB[par][:, 0:512], AF.Copy), "sQVe", 1))
                iv[0] += 1
            rb = B.op("v", [("sQVe", qve_rec[-1])],
                      lambda e, tl=tl: e.tensor_reduce(
                          RDBK[:, 1:2], vstage[:, DM * (tl % 2): DM * (tl % 2 + 1)],
                          mybir.AxisListType.X, Alu.add), "sQVe", 1)
            w = [("sQVe", rb)]
            if half == 1:
                w.append(("sCC1", 1))
            a2a = a2a1_in if half == 0 else a2a2_in
            tl2 = tl % (ntt // 2)

            def vd_fn(e, a2a=a2a, tl=tl, tl2=tl2):
                dst = vflat_in(a2a)[128 * tl2:128 * (tl2 + 1), :, :]
                return e.dma_start(
                    out=dst,
                    in_=vstage[:, DM * (tl % 2): DM * (tl % 2 + 1)].rearrange(
                        "p (j d) -> p j d", d=128))
            qvd_rec.append(B.op("s", w, vd_fn, "sQVd", 16))

        def emit_qk_tile(ot, half):
            for nq in range(nchq_h):
                i = iq[0]
                par = i % 2
                c0 = hp * half + nqk_h * nq
                w = [("sQe", ev_rec[ot]),
                     ("sLnEv", LN2EVL[(c0 + nqk_h) // 128 - 1])] + war_ffn1
                if i >= 2:
                    w.append(("sQKe", qke_rec[i - 2]))

                def qk_fn(e, ot=ot, c0=c0, par=par):
                    for cc in range(8):
                        ins = e.matmul(
                            PB[par][:, 0:nqk_h],
                            wattnT[:, 3072 * cc + 128 * ot: 3072 * cc + 128 * (ot + 1)],
                            FMA_A[:, tpc * cc + c0: tpc * cc + c0 + nqk_h],
                            start=(cc == 0), stop=(cc == 7))
                    return ins
                qk_rec.append(B.op("t", w, qk_fn, "sQK", 1))
                w = [("sQK", qk_rec[i])]
                if (ot - 2, half) in qkd_rec and nq == 0:
                    w.append(("sQKd", qkd_rec[(ot - 2, half)]))
                elif half == 1 and ot < 2 and nq == 0 and (14 + ot, 0) in qkd_rec:
                    w.append(("sQKd", qkd_rec[(14 + ot, 0)]))

                def qke_fn(e, ot=ot, half=half, nq=nq, par=par):
                    sl = tpc * (ot % 2) + hp * half + nqk_h * nq
                    return e.activation(qkstage[:, sl: sl + nqk_h],
                                        PB[par][:, 0:nqk_h], AF.Copy)
                qke_rec.append(B.op("c", w, qke_fn, "sQKe", 1))
                iq[0] += 1
            rb = B.op("v", [("sQKe", qke_rec[-1])],
                      lambda e, ot=ot, half=half: e.tensor_reduce(
                          RDBK[:, 0:1],
                          qkstage[:, tpc * (ot % 2) + hp * half:
                                  tpc * (ot % 2) + hp * half + hp],
                          mybir.AxisListType.X, Alu.add), "sQKe", 1)
            w = [("sQKe", rb)]
            if half == 1:
                w.append(("sCC1", 1))
            a2a = a2a1_in if half == 0 else a2a2_in
            j = ot % 8
            r0 = 384 * j + 128 * (ot // 8)
            qkd_rec[(ot, half)] = B.op(
                "s", w, lambda e, a2a=a2a, r0=r0, ot=ot, half=half: e.dma_start(
                    out=a2a[r0:r0 + 128, :],
                    in_=qkstage[:, tpc * (ot % 2) + hp * half:
                                tpc * (ot % 2) + hp * half + hp]),
                "sQKd", 16)

        for ot in range(16):
            emit_qk_tile(ot, 0)
        for tl in range(ntt // 2):
            emit_v_tile(tl, 0)
        B.op("g", [("sQKd", B.get("sQKd")), ("sQVd", B.get("sQVd"))],
             lambda e: _a2a(e, a2a1_in, a2a1_out), "sCC1", 1)
        for ot in range(16):
            emit_qk_tile(ot, 1)
        for tl in range(ntt // 2, ntt):
            emit_v_tile(tl, 1)
        B.op("g", [("sQKd", B.get("sQKd")), ("sQVd", B.get("sQVd")),
                   ("sCC1", 1)],
             lambda e: _a2a(e, a2a2_in, a2a2_out), "sCC2", 1)
        QKE_ALL = B.get("sQKe")
        QVE_ALL = B.get("sQVe")

        _emit(nc, blk, B, S)
        st_ptA.close()
        st_qkv.close()
        st_quant.close()
        st_fma.close()
        st_att = ExitStack()
        nkbh = tb // 128
        sQKV = sbuf("sQKV", [128, 3 * tb], BF16, st_att)
        sVaug = sbuf("sVaug", [128, 130 * nkbh], BF16, st_att)
        sPP = sbuf("sPP", [128, 4096], BF16, st_att)
        rdsb = sbuf("rdsb", [128, 1024], BF16, st_att)
        rnsb = sbuf("rnsb", [128, 1024], F32, st_att)
        SEL = sbuf("SEL", [128, 128], BF16, st_att)
        yh0 = sbuf("yh0", [128, 1024], BF16, st_att)
        yh1 = sbuf("yh1", [128, 1024], BF16, st_att)
        pnorm = st_att.enter_context(nc.psum_tensor("pnorm", [128, 1024], F32))
        sQ = sQKV[:, 0:tb]
        sK = sQKV[:, tb:2 * tb]
        sV = sQKV[:, 2 * tb:3 * tb]

        B.op("g", [("sCC1", 1)], lambda e: e.memset(sVaug[:, :], 1.0), "sVi", 1)
        B.op("g", [], lambda e: e.memset(SEL[:, :], 0.0), "sVi", 1)
        B.op("g", [], lambda e: e.memset(SEL[64:65, :], 1.0), "sVi", 1)
        B.op("g", [], lambda e: e.memset(rdsb[:, :], 0.0), "sVi", 1)
        VI_ALL = B.get("sVi")

        def emit_qkload(u, half, waits):
            first = True
            for Bq in range(4):
                Bg = Bq + 4 * half
                gout = a2a1_out if half == 0 else a2a2_out
                r0 = 384 * (4 * u + Bq)
                w = waits if first else []
                first = False
                B.op("c", w, lambda e, gout=gout, r0=r0, Bg=Bg: e.dma_start(
                    out=sQ[:, hp * Bg: hp * (Bg + 1)],
                    in_=gout[r0:r0 + 128, :]), "sBl", 16)
                B.op("c", [], lambda e, gout=gout, r0=r0, Bg=Bg: e.dma_start(
                    out=sK[:, hp * Bg: hp * (Bg + 1)],
                    in_=gout[r0 + 128:r0 + 256, :]), "sBl", 16)
            return B.get("sBl")

        def emit_vload(u, half, waits):
            engs = ["g", "c", "g", "c"]
            for Bq in range(4):
                Bg = Bq + 4 * half
                gout = a2a1_out if half == 0 else a2a2_out
                r0 = 384 * (4 * u + Bq) + 256

                def fn(e, gout=gout, r0=r0, Bg=Bg):
                    src = (gout[r0:r0 + 128, :]
                           .rearrange("r c -> (r c)")
                           .rearrange("(kb q d) -> q kb d", q=128, d=128))
                    return e.dma_start(
                        out=sV[:, hp * Bg: hp * (Bg + 1)].rearrange(
                            "q (kb d) -> q kb d", d=128),
                        in_=src)
                B.op(engs[Bq], list(waits), fn, "sBv", 16)
            return B.get("sBv")

        nkb_half = nkbh // 2

        def emit_vaug(u, half, bv_snap, av_snap):
            kb0 = nkb_half * half
            c0 = 4 * hp * half

            def src3(e_cols):
                return sV[:, c0:c0 + 4 * hp].rearrange(
                    "p (kb c) -> p kb c", c=128)[:, :, e_cols[0]:e_cols[1]]
            w = [("sBv", bv_snap), ("sVi", VI_ALL)]
            if av_snap:
                w.append(("sAv", av_snap))
            B.op("v", w, lambda e, kb0=kb0: e.tensor_copy(
                sVaug[:, :].rearrange("p (kb c) -> p kb c", c=130)[
                    :, kb0:kb0 + nkb_half, 0:64], src3((0, 64))))
            return B.op("v", [], lambda e, kb0=kb0: e.tensor_copy(
                sVaug[:, :].rearrange("p (kb c) -> p kb c", c=130)[
                    :, kb0:kb0 + nkb_half, 65:129], src3((64, 128))), "sVa", 1)

        ex_rec = {}
        g = 0
        yn_last = 0
        yo_rec = {}
        pend_mm = []
        av_pend = []
        for u in range(2):
            u0_sc = B.get("sSc")
            u0_av = g
            u0_ex = B.get("sEx")
            u0_va = B.get("sVa")
            sbl = {0: emit_qkload(u, 0, [("sCC1", 1), ("sSc", u0_sc),
                                         ("sEx", u0_ex)])}
            sbv = {0: emit_vload(u, 0, [("sCC1", 1), ("sVa", u0_va)])}
            va = {0: emit_vaug(u, 0, sbv[0], u0_av)}
            for qc in range(nqc):
                if qc == nqc // 2:
                    sbl[1] = emit_qkload(u, 1, [("sCC2", 1)])
                    sbv[1] = emit_vload(u, 1, [("sCC2", 1)])
                    va[1] = emit_vaug(u, 1, sbv[1], u0_av)
                hh = 0 if qc < nqc // 2 else 1
                nkb = 4 * qc + 4
                for kb in range(nkb):
                    diag = kb >= 4 * qc
                    qlo = 128 * (kb - 4 * qc) if diag else 0
                    par = g % 2
                    slot = g % 4
                    q0 = 512 * qc + qlo
                    n = 512 - qlo
                    w = [("sBl", sbl[hh])]
                    if g >= 2:
                        w.append(("sEx", ex_rec[g - 2]))
                    if g == 0:
                        w += [("sQKe", QKE_ALL), ("sQVe", QVE_ALL)]

                    def sc_fn(e, kb=kb, par=par, qlo=qlo, q0=q0, n=n):
                        e.matmul(pSS[:, 1024 * par + qlo: 1024 * par + 512],
                                 sK[0:64, 128 * kb:128 * (kb + 1)],
                                 sQ[0:64, q0:q0 + n],
                                 start=True, stop=True, tile_position=(0, 0))
                        return e.matmul(pSS[:, 1024 * par + 512 + qlo:
                                            1024 * par + 1024],
                                        sK[64:128, 128 * kb:128 * (kb + 1)],
                                        sQ[64:128, q0:q0 + n],
                                        start=True, stop=True,
                                        tile_position=(64, 0))
                    sc = B.op("t", w, sc_fn, "sSc", 1)
                    uq = u * nqc + qc
                    if pend_mm and kb == 2:
                        while av_pend and av_pend[0][0] < uq:
                            av_pend.pop(0)[1]()
                        for fn_ in pend_mm:
                            fn_()
                        pend_mm = []
                    while len(av_pend) >= 3:
                        av_pend.pop(0)[1]()
                    w = [("sSc", sc)]
                    if g >= 4:
                        w.append(("sAv", g - 3))
                    ex_rec[g] = B.op("c", w, lambda e, par=par, slot=slot:
                                     e.activation(
                                         sPP[:, 1024 * slot:1024 * (slot + 1)],
                                         pSS[:, 1024 * par:1024 * (par + 1)],
                                         AF.Exp, scale=0.125,
                                         bias=nbias[:, 0:1]), "sEx", 1)
                    mk = None
                    if diag:
                        def mk_fn(e, slot=slot, qlo=qlo):
                            e.tensor_tensor(
                                sPP[:, 1024 * slot + qlo:
                                    1024 * slot + qlo + 128],
                                sPP[:, 1024 * slot + qlo:
                                    1024 * slot + qlo + 128],
                                tri[:, :], Alu.mult)
                            return e.tensor_tensor(
                                sPP[:, 1024 * slot + 512 + qlo:
                                    1024 * slot + 512 + qlo + 128],
                                sPP[:, 1024 * slot + 512 + qlo:
                                    1024 * slot + 512 + qlo + 128],
                                tri[:, :], Alu.mult)
                        mk = B.op("v", [("sEx", ex_rec[g])], mk_fn, "sMk", 1)

                    def emit_av(g=g, kb=kb, slot=slot, qlo=qlo, mk=mk,
                                diag=diag, nkb=nkb, va_snap=va[hh], u=u, qc=qc,
                                yn_snap=yn_last):
                        w = [("sMk", mk)] if diag else [("sEx", ex_rec[g])]
                        w.append(("sVa", va_snap))
                        if kb == 0:
                            w.append(("sYn", yn_snap))
                            if u == 0 and qc == 0:
                                w.append(("sFoV", FFN1_OV))

                        def fn(e, kb=kb, slot=slot, qlo=qlo,
                               st=(kb == 0), sp=(kb == nkb - 1)):
                            p1 = sPP[:, 1024 * slot + qlo: 1024 * slot + 512]
                            p2 = sPP[:, 1024 * slot + 512 + qlo:
                                     1024 * slot + 1024]
                            e.matmul(pb4[0:65, qlo:512],
                                     sVaug[:, 130 * kb:130 * kb + 65], p1,
                                     start=st, stop=sp, tile_position=(0, 0))
                            return e.matmul(pb5[0:65, qlo:512],
                                            sVaug[:, 130 * kb + 65:
                                                  130 * (kb + 1)], p2,
                                            start=st, stop=sp,
                                            tile_position=(0, 0))
                        B.op("t", w, fn, "sAv", 1)
                    av_pend.append((u * nqc + qc, emit_av))
                    g += 1
                qpar = qc % 2

                def rc_fn(e, src, dst_c0):
                    with nc.allow_low_precision(reason="recip as bf16 matmul operand"):
                        return e.reciprocal(rdsb[64:65, dst_c0:dst_c0 + 512], src)
                B.op("v", [("sAv", g), ("sVi", VI_ALL)],
                     lambda e: rc_fn(e, pb4[64:65, 0:512], 0))
                nmv = B.op("v", [], lambda e: rc_fn(
                    e, pb5[64:65, 0:512], 512), "sNmV", 1)
                bc_target = B.get("sBc") + 2

                def mk_mm(nmv=nmv):
                    B.op("t", [("sNmV", nmv), ("sVi", VI_ALL)],
                         lambda e: e.matmul(pnorm[0:64, 0:512],
                                            SEL[64:96, 0:64],
                                            rdsb[64:96, 0:512],
                                            start=True, stop=True,
                                            tile_position=(64, 0)), "sBc", 1)
                    B.op("t", [], lambda e: e.matmul(pnorm[0:64, 512:1024],
                                                     SEL[64:96, 0:64],
                                                     rdsb[64:96, 512:1024],
                                                     start=True, stop=True,
                                                     tile_position=(64, 0)),
                         "sBc", 1)
                pend_mm = [mk_mm]
                B.op("v", [("sBc", bc_target)], lambda e: e.tensor_copy(
                    rnsb[0:64, 0:1024], pnorm[0:64, 0:1024]), "sNmC", 1)
                w = [("sYo", yo_rec.get(qpar, 0))]
                B.op("v", w, lambda e, qpar=qpar: e.tensor_tensor(
                    yh0[0:64, 512 * qpar:512 * qpar + 512],
                    pb4[0:64, 0:512], rnsb[0:64, 0:512], Alu.mult), "sYn", 1)
                yn_last = B.op("v", [], lambda e, qpar=qpar: e.tensor_tensor(
                    yh1[0:64, 512 * qpar:512 * qpar + 512],
                    pb5[0:64, 0:512], rnsb[0:64, 512:1024], Alu.mult),
                    "sYn", 1)
                q0 = 512 * qc
                Bt = q0 // hp
                j = 4 * u + (Bt if Bt < 4 else Bt - 4)
                cdst = (0 if Bt < 4 else hp) + q0 - hp * Bt
                w = [("sYn", yn_last), ("sCC2", 1)]
                B.op("s", w, lambda e, j=j, cdst=cdst, qpar=qpar: e.dma_start(
                    out=a2ac_in[128 * j:128 * j + 64, cdst:cdst + 512],
                    in_=yh0[0:64, 512 * qpar:512 * qpar + 512]), "sYo", 16)
                yo_rec[qpar] = B.op(
                    "s", [], lambda e, j=j, cdst=cdst, qpar=qpar: e.dma_start(
                        out=a2ac_in[128 * j + 64:128 * (j + 1), cdst:cdst + 512],
                        in_=yh1[0:64, 512 * qpar:512 * qpar + 512]), "sYo", 16)
        while av_pend:
            av_pend.pop(0)[1]()
        for fn_ in pend_mm:
            fn_()
        pend_mm = []
        B.op("g", [("sYo", B.get("sYo")), ("sCC2", 1)],
             lambda e: _a2a(e, a2ac_in, a2ac_out), "sCCy", 1)

        ATT_EX = B.get("sEx")
        ATT_NMV = B.get("sNmV")
        ATT_YO = B.get("sYo")
        ATT_YN = B.get("sYn")

        _emit(nc, blk, B, S)
        st_att.close()
        st_c = ExitStack()
        pt[0] = st_c.enter_context(nc.psum_tensor("ptC0", [128, 1024], BF16))
        pt[1] = st_c.enter_context(nc.psum_tensor("ptC1", [128, 1024], BF16))
        FMA_C = sbuf("FMA_C", [128, 8 * tpc], BF16, st_c)
        U1c = sbuf("U1c", [128, 8 * RANK], BF16, st_c)
        V1c = sbuf("V1c", [128, 2 * DFF], BF16, st_c)
        U2c = sbuf("U2c", [128, 32 * RANK], BF16, st_c)
        V2c = sbuf("V2c", [128, 2 * DM], BF16, st_c)
        sHc = sbuf("sHc", [128, 32 * nch2], BF16, st_c)
        sPc = sbuf("sPc", [128, 2 * nch2], BF16, st_c)
        sQ2c = sbuf("sQ2c", [128, 2 * nch2], BF16, st_c)

        for cc in range(8):
            B.op("g", [("sCCy", 1)] if cc == 0 else [],
                 lambda e, cc=cc: e.dma_start(
                     out=FMA_C[:, tpc * cc: tpc * (cc + 1)],
                     in_=a2ac_out[128 * cc:128 * (cc + 1), :]), "sCy", 16)
        CY_ALL = B.get("sCy")

        if DEBUG_DUMP2:
            d_fmc = nc.dram_tensor("d_fmc", [128, 8 * tpc], BF16,
                                   kind="ExternalOutput").ap()
            B.op("s", [("sCy", CY_ALL)],
                 lambda e: e.dma_start(out=d_fmc, in_=FMA_C[:, :]), "sOut", 16)
        gate_w2 = [("sNmV", ATT_NMV), ("sYo", ATT_YO), ("sYn", ATT_YN)]
        for dst, src in ffnw_dmas(U1c, V1c, U2c, V2c):
            B.op("g", gate_w2, lambda e, d=dst, s_=src: e.dma_start(out=d, in_=s_),
                 "sFw2", 16)
            gate_w2 = []
        FW2_ALL = B.get("sFw2")
        scl2 = B.op("v", [("sFw2", FW2_ALL), ("sNmV", ATT_NMV)],
                    lambda e: e.tensor_scalar_mul(V2c[:, :], V2c[:, :], 0.5),
                    "sScl", 1)

        pj_rec, pjv_rec = [], []
        pjv_tile = [None] * ntt
        i = 0
        for tl in range(ntt):
            for half in range(2):
                par = i % 2
                w = [("sCy", CY_ALL), ("sQe", QE_ALL)]
                if i < 2:
                    w += [("sEx", ATT_EX), ("sNmV", ATT_NMV)]
                else:
                    w.append(("sPjV", pjv_rec[i - 2]))

                def fn(e, tl=tl, half=half, par=par):
                    for cc in range(8):
                        ins = e.matmul(
                            PB[par][:, 0:512],
                            FMA_C[:, tpc * cc + 128 * tl: tpc * cc + 128 * (tl + 1)],
                            wprojT[:, 1024 * cc + 512 * half:
                                   1024 * cc + 512 * (half + 1)],
                            start=(cc == 0), stop=(cc == 7))
                    return ins
                pj_rec.append(B.op("t", w, fn, "sPjT", 1))
                pjv_rec.append(B.op("v", [("sPjT", pj_rec[i])],
                                   lambda e, tl=tl, half=half, par=par:
                                   e.scalar_tensor_tensor(
                                       x_sl(tl, 512 * half, 512), PB[par][:, 0:512],
                                       1.0, x_sl(tl, 512 * half, 512),
                                       Alu.mult, Alu.add), "sPjV", 1))
                if half == 1:
                    pjv_tile[tl] = pjv_rec[i]
                i += 1
        PJT_ALL = B.get("sPjT")
        PJV_ALL = B.get("sPjV")

        if DEBUG_STAGE == 2:
            for tt in range(ntt):
                B.op("s", [("sPjV", pjv_tile[tt])],
                     lambda e, tt=tt: e.dma_start(
                         out=out[128 * tt:128 * (tt + 1), :], in_=x_sl(tt)),
                     "sOut", 16)
            finish("s2")
            st_c.close()
            return nc

        gate3 = [[("sPjV", pjv_tile[tt])] for tt in range(ntt)]
        ln3 = emit_ln(gate3, gate3, [("sPjT", PJT_ALL)], FMA_C)
        bufs2 = (U1c, V1c, U2c, V2c, sPc, sHc, sQ2c, FMA_C)
        wwaits2 = [("sFw2", FW2_ALL), ("sScl", scl2), ("sPjV", PJV_ALL),
                   ("sEx", ATT_EX), ("sNmV", ATT_NMV), ("sYn", ATT_YN)]
        emit_ffn(nch2, ln3, bufs2, wwaits2, is_last=True)

        finish("s3")
        st_c.close()
    return nc


def _emit(nc, blk, B, S):
    eng_map = {"s": blk.sync, "g": blk.gpsimd, "t": blk.tensor,
               "v": blk.vector, "c": blk.scalar}
    for ename, deco in eng_map.items():
        ops = [o for o in B.ops if o.eng == ename]
        if not ops:
            continue

        def body(e, ops=ops):
            last = {}
            for o in ops:
                for (s, v) in o.waits:
                    if v > last.get(s, 0):
                        e.wait_ge(S[s], v)
                        last[s] = v
                ins = o.fn(e)
                if o.inc is not None:
                    assert ins is not None, f"inc op returned None ({ename})"
                    ins.then_inc(S[o.inc[0]], o.inc[1])
        deco(body)
    B.ops = []


_CACHE = {}


def _get_nc(tpc):
    if tpc not in _CACHE:
        _CACHE[tpc] = build_program(tpc)
    return _CACHE[tpc]


def make_in_maps(inputs, tpc):
    x = np.ascontiguousarray(np.asarray(inputs["L0_attn_x"], dtype=np.float32))
    Bn, T, C = x.shape
    xf = x.reshape(Bn * T, C)
    tri = np.triu(np.ones((128, 128), dtype=np.float32)).astype(nbf)
    ident = np.eye(128, dtype=np.float32).astype(nbf)
    identf = np.eye(128, dtype=np.float32)
    shared = dict(
        w_attn=np.ascontiguousarray(np.asarray(inputs["L0_attn_w_attn"], np.float32)),
        w_proj=np.ascontiguousarray(np.asarray(inputs["L0_attn_w_proj"], np.float32)),
        U1=np.ascontiguousarray(np.asarray(inputs["L0_ffn_U1"], np.float32)),
        V1=np.ascontiguousarray(np.asarray(inputs["L0_ffn_V1"], np.float32)),
        b1=np.ascontiguousarray(np.asarray(inputs["L0_ffn_b1"], np.float32)),
        U2=np.ascontiguousarray(np.asarray(inputs["L0_ffn_U2"], np.float32)),
        V2=np.ascontiguousarray(np.asarray(inputs["L0_ffn_V2"], np.float32)),
        b2=np.ascontiguousarray(np.asarray(inputs["L0_ffn_b2"], np.float32)),
        tri=tri, ident=ident, identf=identf,
    )
    hp = tpc // 2
    in_maps = []
    for c in range(NCORES):
        b, j = c // 4, c % 4
        base = b * (4 * tpc)
        rows = np.concatenate([
            xf[base + hp * j: base + hp * (j + 1)],
            xf[base + hp * (j + 4): base + hp * (j + 5)],
        ], axis=0)
        m = dict(shared)
        m["x_shard"] = np.ascontiguousarray(rows)
        in_maps.append(m)
    return in_maps


def kernel(**inputs):
    x = np.asarray(inputs["L0_attn_x"])
    Bn, T, C = x.shape
    tpc = (Bn * T) // NCORES
    hp = tpc // 2
    in_maps = make_in_maps(inputs, tpc)
    nc = _get_nc(tpc)
    res = run_bass_kernel_spmd(nc, in_maps, core_ids=list(range(NCORES)))
    out = np.empty((Bn * T, C), dtype=np.float32)
    for c in range(NCORES):
        shard = np.asarray(res.results[c]["out_shard"])
        b, j = c // 4, c % 4
        base = b * (4 * tpc)
        out[base + hp * j: base + hp * (j + 1)] = shard[0:hp]
        out[base + hp * (j + 4): base + hp * (j + 5)] = shard[hp:tpc]
    return out.reshape(Bn, T, C)


# version 30
# speedup vs baseline: 1.0564x; 1.0504x over previous
import numpy as np
import ml_dtypes
import concourse.bass as bass
import concourse.mybir as mybir
from concourse.bass_utils import run_bass_kernel_spmd
from concourse.alu_op_type import AluOpType
from concourse.aot_env import allow_ici_all_to_all
from contextlib import ExitStack

F32 = mybir.dt.float32
BF16 = mybir.dt.bfloat16
nbf = ml_dtypes.bfloat16
AF = mybir.ActivationFunctionType
Alu = AluOpType

NCORES = 8
DM = 1024
DFF = 4096
RANK = 256
MAGIC = 12582912.0
EXP_BIAS = -12.0
DEBUG_STAGE = 3
DEBUG_DUMP = False
DEBUG_DUMP2 = False


class Op:
    __slots__ = ("eng", "waits", "fn", "inc")

    def __init__(self, eng, waits, fn, inc):
        self.eng = eng
        self.waits = waits
        self.fn = fn
        self.inc = inc


SERIAL = False


class Builder:
    def __init__(self):
        self.ops = []
        self.cnt = {}

    def op(self, eng, waits, fn, inc=None, incn=1):
        if SERIAL:
            waits = list(waits) + [(s, v) for (s, v) in self.cnt.items()]
        w = [(s, v) for (s, v) in waits if v and v > 0]
        if inc is None:
            self.ops.append(Op(eng, w, fn, None))
            return None
        self.cnt[inc] = self.cnt.get(inc, 0) + incn
        self.ops.append(Op(eng, w, fn, (inc, incn)))
        return self.cnt[inc]

    def bump(self, name, n):
        self.cnt[name] = self.cnt.get(name, 0) + n
        return self.cnt[name]

    def get(self, s):
        return self.cnt.get(s, 0)


def _a2a(e, tin, tout):
    with allow_ici_all_to_all():
        return e.collective_compute(
            "AllToAll", mybir.AluOpType.bypass,
            replica_groups=[list(range(NCORES))],
            ins=[tin.ap().opt()], outs=[tout.ap().opt()])


def build_program(tpc):
    tb = 4 * tpc
    ntt = tpc // 128
    nqc = tb // 512
    nch1 = min(256, tpc)
    nch2 = min(256, tpc)
    nqk = min(512, tpc)
    assert tpc % 256 == 0 and tb % 512 == 0

    nc = bass.Bass("TRN2", target_bir_lowering=False)

    x_in = nc.dram_tensor("x_shard", [tpc, DM], F32, kind="ExternalInput").ap()
    w_attn = nc.dram_tensor("w_attn", [3 * DM, DM], F32, kind="ExternalInput").ap()
    w_proj = nc.dram_tensor("w_proj", [DM, DM], F32, kind="ExternalInput").ap()
    U1 = nc.dram_tensor("U1", [DM, RANK], F32, kind="ExternalInput").ap()
    V1 = nc.dram_tensor("V1", [RANK, DFF], F32, kind="ExternalInput").ap()
    b1 = nc.dram_tensor("b1", [DFF], F32, kind="ExternalInput").ap()
    U2 = nc.dram_tensor("U2", [DFF, RANK], F32, kind="ExternalInput").ap()
    V2 = nc.dram_tensor("V2", [RANK, DM], F32, kind="ExternalInput").ap()
    b2 = nc.dram_tensor("b2", [DM], F32, kind="ExternalInput").ap()
    tri_in = nc.dram_tensor("tri", [128, 128], BF16, kind="ExternalInput").ap()
    idf_in = nc.dram_tensor("identf", [128, 128], F32, kind="ExternalInput").ap()
    id_in = nc.dram_tensor("ident", [128, 128], BF16, kind="ExternalInput").ap()
    out = nc.dram_tensor("out_shard", [tpc, DM], F32, kind="ExternalOutput").ap()

    hp_ = tpc // 2
    a2a1_in = nc.dram_tensor("a2a1_in", [NCORES * 384, hp_], BF16)
    a2a1_out = nc.dram_tensor("a2a1_out", [NCORES * 384, hp_], BF16)
    a2a2_in = nc.dram_tensor("a2a2_in", [NCORES * 384, hp_], BF16)
    a2a2_out = nc.dram_tensor("a2a2_out", [NCORES * 384, hp_], BF16)
    a2ac_in = nc.dram_tensor("a2ac_in", [NCORES * 128, tpc], BF16)
    a2ac_out = nc.dram_tensor("a2ac_out", [NCORES * 128, tpc], BF16)

    B = Builder()
    SEMS = ["sX", "sW", "sScl", "sLnV", "sLnC", "sLnSt", "sLnAp", "sLnTr",
            "sLnEv", "sFp", "sFpe", "sFh", "sFg", "sFq2", "sFq2e", "sFo",
            "sFoV", "sQl", "sAbs", "sQs", "sQt", "sQv1", "sQg", "sQv3",
            "sQtr", "sQe", "sQK", "sQKe", "sQKd", "sQV", "sQVe", "sQVd",
            "sCC1", "sCC2", "sCCy", "sBl", "sBv", "sSc", "sEx", "sMk", "sAv",
            "sNmC", "sNmV", "sYn", "sYo", "sCy", "sFw2", "sPjT", "sPjV", "sOut",
            "sVa", "sVi", "sBc"]

    with ExitStack() as top:
        def sbuf(name, shape, dt, stack=None):
            return (stack or top).enter_context(nc.sbuf_tensor(name, shape, dt))

        X = sbuf("X", [128, ntt * DM], F32)
        xlntok = sbuf("xlntok", [128, 2 * DM], BF16)
        xdumpV = sbuf("xdumpV", [128, DM], BF16)
        xdumpC = sbuf("xdumpC", [128, DM], BF16)
        SUMX = sbuf("SUMX", [128, ntt], F32)
        SUMQ = sbuf("SUMQ", [128, ntt], F32)
        MEAN = sbuf("MEAN", [128, ntt], F32)
        MSQ = sbuf("MSQ", [128, ntt], F32)
        VAR = sbuf("VAR", [128, ntt], F32)
        SD = sbuf("SD", [128, ntt], F32)
        RST = sbuf("RST", [128, ntt], F32)
        tri = sbuf("tri_s", [128, 128], BF16)
        idb = sbuf("idb", [128, 128], BF16)
        onesb = sbuf("onesb", [128, 128], BF16)
        ones1 = sbuf("ones1", [1, 128], BF16)
        onesf = sbuf("onesf", [128, 128], F32)
        b1sb = sbuf("b1sb", [128, DFF // 128], F32)
        b1T = sbuf("b1T", [32, 128], F32)
        idf = sbuf("idf", [128, 128], F32)
        b2rowh = sbuf("b2rowh", [1, DM], BF16)
        scaleA = sbuf("scaleA", [128, 1], F32)
        iscaleA = sbuf("iscaleA", [128, 1], F32)
        scaleP = sbuf("scaleP", [128, 1], F32)
        iscaleP = sbuf("iscaleP", [128, 1], F32)
        PARTS = sbuf("PARTS", [128, 32], F32)
        PRED = sbuf("PRED", [128, 2], F32)
        epsb = sbuf("epsb", [128, 1], F32)
        wprojT = sbuf("wprojT", [128, 8192], BF16)
        nbias = sbuf("nbias", [128, 1], F32)
        RDBK = sbuf("RDBK", [128, 2], F32)

        pSS = top.enter_context(nc.psum_tensor("pSS", [128, 2048], F32))
        pb45 = top.enter_context(nc.psum_tensor("pb45", [128, 1024], F32))
        pb4 = pb45[:, 0:512]
        pb5 = pb45[:, 512:1024]
        st_ptA = ExitStack()
        pt = [st_ptA.enter_context(nc.psum_tensor(f"ptA{i}", [128, 1024], BF16))
              for i in range(2)]
        PB = [pSS[:, 0:512], pSS[:, 512:1024], pSS[:, 1024:1536],
              pSS[:, 1536:2048], pb4[:, 0:512], pb5[:, 0:512]]

        S = {s: top.enter_context(nc.semaphore(s)) for s in SEMS}
        blk = top.enter_context(nc.Block())

        def x_sl(tt, c0=0, n=DM):
            return X[:, DM * tt + c0: DM * tt + c0 + n]

        def finish(stage_tag):
            B.op("g", [("sOut", B.get("sOut"))], lambda e: None)
            _emit(nc, blk, B, S)

        st_fma = ExitStack()
        FMA_A = sbuf("FMA_A", [128, 8 * tpc], BF16, st_fma)
        st_ffn1 = ExitStack()
        U1a = sbuf("U1a", [128, 8 * RANK], BF16, st_ffn1)
        V1a = sbuf("V1a", [128, 2 * DFF], BF16, st_ffn1)
        U2a = sbuf("U2a", [128, 32 * RANK], BF16, st_ffn1)
        V2a = sbuf("V2a", [128, 2 * DM], BF16, st_ffn1)
        sHa = sbuf("sHa", [128, 32 * nch1], BF16, st_ffn1)
        sPa = sbuf("sPa", [128, 2 * nch1], BF16, st_ffn1)
        sQ2a = sbuf("sQ2a", [128, 2 * nch1], BF16, st_ffn1)
        wv1 = sbuf("wv1", [128, DM], F32, st_ffn1)

        B.op("s", [], lambda e: e.dma_start(
            out=X[:, :].rearrange("p (tt c) -> p tt c", c=DM),
            in_=x_in.rearrange("(tt p) c -> p tt c", p=128)), "sX", 16)

        def wtile_src(j):
            return (w_attn[128 * j:128 * (j + 1), :] if j < 24
                    else w_proj[128 * (j - 24):128 * (j - 23), :])

        for j in range(32):
            w = [] if j < 1 else [("sAbs", j)]
            B.op("s", w, lambda e, j=j: e.dma_start(
                out=wv1[:, :], in_=wtile_src(j)), "sQl", 16)

        def ffnw_dmas(U1d, V1d, U2d, V2d):
            return [
                (U1d[:, :].rearrange("p (cc r) -> p cc r", r=RANK),
                 U1.rearrange("(cc p) r -> p cc r", p=128)),
                (V1d[:, :].rearrange("p (rt f) -> p rt f", f=DFF),
                 V1.rearrange("(rt p) f -> p rt f", p=128)),
                (U2d[:, :].rearrange("p (ft r) -> p ft r", r=RANK),
                 U2.rearrange("(ft p) r -> p ft r", p=128)),
                (V2d[:, :].rearrange("p (rt c) -> p rt c", c=DM),
                 V2.rearrange("(rt p) c -> p rt c", p=128)),
            ]

        for dst, src in [
            (b2rowh[:, :], b2.rearrange("(a c) -> a c", a=1)),
            (tri[:, :], tri_in),
            (idb[:, :], id_in),
            (idf[:, :], idf_in),
        ] + ffnw_dmas(U1a, V1a, U2a, V2a):
            B.op("g", [], lambda e, d=dst, s_=src: e.dma_start(out=d, in_=s_),
                 "sW", 16)
        B.op("g", [], lambda e: e.dma_start(
            out=b1T[:, :], in_=b1.rearrange("(ft p) -> ft p", p=128)), "sW", 16)
        B.op("g", [], lambda e: e.memset(onesb[:, :], 1.0), "sW", 1)
        B.op("g", [], lambda e: e.memset(ones1[:, :], 1.0), "sW", 1)
        B.op("g", [], lambda e: e.memset(onesf[:, :], 1.0), "sW", 1)
        B.op("g", [], lambda e: e.memset(epsb[:, :], 1e-5), "sW", 1)
        B.op("g", [], lambda e: e.memset(nbias[:, :], EXP_BIAS), "sW", 1)
        bt = B.op("t", [("sW", B.get("sW"))],
                  lambda e: e.matmul(pb4[:, 0:128], b1T[:, :], idf[0:32, :],
                                     start=True, stop=True), "sW", 1)
        B.op("v", [("sW", bt)],
             lambda e: e.tensor_copy(b1sb[:, :], pb4[:, 0:32]), "sW", 1)
        W_ALL = B.get("sW")

        B.op("v", [("sW", W_ALL)],
             lambda e: e.tensor_scalar_mul(V2a[:, :], V2a[:, :], 0.5))
        scl1 = B.op("v", [], lambda e: e.tensor_scalar_mul(
            b2rowh[:, :], b2rowh[:, :], 0.5), "sScl", 1)

        def emit_ln(gate_v, gate_c, fma_war, FMAd, tr_extra=()):
            r = {"ev": [], "ap": [], "tr": []}
            for tt in range(ntt):
                B.op("v", [("sX", 16)] + gate_v[tt],
                     lambda e, tt=tt: e.tensor_scalar(
                         xdumpV[:, :], x_sl(tt), 1.0, 0.0, Alu.mult, Alu.add,
                         accum_out=SUMX[:, tt:tt + 1]), "sLnV", 1)
            for tt in range(ntt):
                B.op("c", [("sX", 16)] + gate_c[tt],
                     lambda e, tt=tt: e.activation(
                         xdumpC[:, :], x_sl(tt), AF.Square,
                         accum_out=SUMQ[:, tt:tt + 1]), "sLnC", 1)
            vc = B.get("sLnC")
            va = B.get("sLnV")
            st0 = B.op("c", [("sLnV", va)],
                       lambda e: e.activation(MEAN[:, :], SUMX[:, :], AF.Copy,
                                              scale=1.0 / DM), "sLnSt", 1)
            B.op("v", [("sLnSt", st0)],
                 lambda e: e.tensor_tensor(MSQ[:, :], MEAN[:, :], MEAN[:, :], Alu.mult))
            B.op("v", [], lambda e: e.tensor_copy(xdumpV[:, :], xdumpV[:, :]))
            st1 = B.op("v", [("sLnC", vc)],
                       lambda e: e.scalar_tensor_tensor(
                           VAR[:, :], SUMQ[:, :], 1.0 / DM, MSQ[:, :],
                           Alu.mult, Alu.subtract), "sLnSt", 1)
            st2 = B.op("c", [("sLnSt", st1)],
                       lambda e: e.activation(SD[:, :], VAR[:, :], AF.Sqrt, bias=epsb[:, 0:1]),
                       "sLnSt", 1)
            B.op("v", [("sLnSt", st2)],
                 lambda e: e.reciprocal(RST[:, :], SD[:, :]))
            B.op("v", [], lambda e: e.tensor_copy(xdumpV[:, :], xdumpV[:, :]))
            for tt in range(ntt):
                par = tt % 2
                w = [("sLnTr", r["tr"][tt - 2])] if tt >= 2 else []
                r["ap"].append(B.op("v", w, lambda e, tt=tt, par=par: e.tensor_scalar(
                    xlntok[:, DM * par: DM * par + DM], x_sl(tt),
                    MEAN[:, tt:tt + 1], RST[:, tt:tt + 1],
                    Alu.subtract, Alu.mult), "sLnAp", 1))

                def tr_fn(e, tt=tt, par=par):
                    for cc in range(8):
                        ins = e.transpose(
                            pt[par][:, 128 * cc:128 * (cc + 1)],
                            xlntok[:, DM * par + 128 * cc: DM * par + 128 * (cc + 1)],
                            idb[:, :])
                    return ins
                w = [("sLnAp", r["ap"][tt]), ("sW", W_ALL)]
                if tt < 2:
                    w += list(tr_extra)
                else:
                    w.append(("sLnEv", r["ev"][tt - 2]))
                r["tr"].append(B.op("t", w, tr_fn, "sLnTr", 1))

                def ev_fn(e, tt=tt, par=par, FMAd=FMAd):
                    dst = FMAd[:, :].rearrange("p (cc t) -> p cc t", t=tpc)[
                        :, :, 128 * tt:128 * (tt + 1)]
                    src = pt[par][:, :].rearrange("p (cc r) -> p cc r", r=128)
                    return e.tensor_copy(dst, src)
                w = [("sLnTr", r["tr"][tt])] + (fma_war if tt == 0 else [])
                r["ev"].append(B.op("v", w, ev_fn, "sLnEv", 1))
            return r

        def emit_ffn(nch, ln, bufs, wwaits, is_last, extra_c=None):
            U1w, V1w, U2w, V2w, sPb, sHb, sQ2b, FMAd = bufs
            nchunks = tpc // nch
            ntl = nch // 128
            r = {"ov_tile": [None] * ntt}
            prev_q2e = [0, 0]
            prev_g = {}
            prev_ov = {}
            prev_o_last = 0
            prev_h_last = 0
            for n in range(nchunks):
                t0 = n * nch
                p_rec, h_rec, q2_rec, o_rec, g_list = [], [], [], [], []
                for rt in range(2):
                    w = [("sLnEv", ln["ev"][(t0 + nch) // 128 - 1]),
                         ("sFq2e", prev_q2e[rt])] + wwaits

                    def p_fn(e, rt=rt, t0=t0, U1w=U1w, FMAd=FMAd):
                        for cc in range(8):
                            ins = e.matmul(
                                PB[rt][:, 0:nch],
                                U1w[:, RANK * cc + 128 * rt: RANK * cc + 128 * (rt + 1)],
                                FMAd[:, tpc * cc + t0: tpc * cc + t0 + nch],
                                start=(cc == 0), stop=(cc == 7))
                        return ins
                    p_rec.append(B.op("t", w, p_fn, "sFp", 1))
                pe_rec = []
                for rt in range(2):
                    w = [("sFp", p_rec[rt]), ("sFh", prev_h_last)]
                    pe_rec.append(B.op("c", w, lambda e, rt=rt, sPb=sPb:
                                       e.activation(
                                           sPb[:, nch * rt: nch * rt + nch],
                                           PB[rt][:, 0:nch], AF.Copy), "sFpe", 1))
                for ft in range(32):
                    par = 2 + (ft % 2)
                    w = [("sFpe", pe_rec[1])]
                    if par in prev_g:
                        w.append(("sFg", prev_g[par]))

                    def h_fn(e, ft=ft, par=par, V1w=V1w, sPb=sPb):
                        for rt in range(2):
                            ins = e.matmul(
                                PB[par][:, 0:nch],
                                V1w[:, DFF * rt + 128 * ft: DFF * rt + 128 * (ft + 1)],
                                sPb[:, nch * rt: nch * rt + nch],
                                start=(rt == 0), stop=(rt == 1))
                        return ins
                    h_rec.append(B.op("t", w, h_fn, "sFh", 1))
                    g = B.op("c", [("sFh", h_rec[ft])],
                             lambda e, ft=ft, par=par, sHb=sHb: e.activation(
                                 sHb[:, nch * ft: nch * ft + nch], PB[par][:, 0:nch],
                                 AF.Gelu_apprx_tanh, bias=b1sb[:, ft:ft + 1]),
                             "sFg", 1)
                    prev_g[par] = g
                    g_list.append(g)
                for rt in range(2):
                    def q2_fn(e, rt=rt, g_list=tuple(g_list), U2w=U2w, sHb=sHb):
                        for ft in range(32):
                            e.wait_ge(S["sFg"], g_list[ft])
                            ins = e.matmul(
                                PB[rt][:, 0:nch],
                                U2w[:, RANK * ft + 128 * rt: RANK * ft + 128 * (rt + 1)],
                                sHb[:, nch * ft: nch * ft + nch],
                                start=(ft == 0), stop=(ft == 31))
                        return ins
                    w = [("sFpe", pe_rec[rt])]
                    q2_rec.append(B.op("t", w, q2_fn, "sFq2", 1))
                q2e_rec = []
                for rt in range(2):
                    w = [("sFq2", q2_rec[rt]), ("sFo", prev_o_last)]
                    q2e_rec.append(B.op("v", w, lambda e, rt=rt, sPb=sPb, sQ2b=sQ2b:
                                        e.tensor_copy(
                                            sQ2b[:, nch * rt: nch * rt + nch],
                                            PB[rt][:, 0:nch]), "sFq2e", 1))
                    prev_q2e[rt] = q2e_rec[rt]
                for tl in range(ntl):
                    for half in range(2):
                        idx = tl * 2 + half
                        par = 4 + (idx % 2)
                        gtt = t0 // 128 + tl
                        w = [("sFq2e", q2e_rec[1])] + wwaits
                        if par in prev_ov:
                            w.append(("sFoV", prev_ov[par]))

                        def o_fn(e, tl=tl, half=half, par=par, sQ2b=sQ2b, V2w=V2w):
                            for rt in range(2):
                                e.matmul(
                                    PB[par][:, 0:512],
                                    sQ2b[:, nch * rt + 128 * tl: nch * rt + 128 * (tl + 1)],
                                    V2w[:, DM * rt + 512 * half: DM * rt + 512 * (half + 1)],
                                    start=(rt == 0), stop=False)
                            return e.matmul(
                                PB[par][:, 0:512], ones1[0:1, 0:128],
                                b2rowh[0:1, 512 * half: 512 * (half + 1)],
                                start=False, stop=True)
                        o = B.op("t", w, o_fn, "sFo", 1)
                        o_rec.append(o)
                        prev_o_last = o
                        ov = B.op("v", [("sFo", o)],
                                  lambda e, gtt=gtt, half=half, par=par:
                                  e.scalar_tensor_tensor(
                                      x_sl(gtt, 512 * half, 512), PB[par][:, 0:512],
                                      1.0, x_sl(gtt, 512 * half, 512),
                                      Alu.mult, Alu.add), "sFoV", 1)
                        prev_ov[par] = ov
                        if half == 1:
                            r["ov_tile"][gtt] = ov
                            if is_last:
                                B.op("s", [("sFoV", ov)],
                                     lambda e, gtt=gtt: e.dma_start(
                                         out=out[128 * gtt:128 * (gtt + 1), :],
                                         in_=x_sl(gtt)), "sOut", 16)
                prev_h_last = h_rec[-1]
                if extra_c:
                    take = min(4, len(extra_c))
                    for fn_ in extra_c[:take]:
                        fn_()
                    del extra_c[:take]
            return r

        abs_rec = []

        def _mk_abs(j):
            def emit():
                abs_rec.append(B.op("c", [("sQl", 16 * (j + 1))],
                                    lambda e, j=j: e.activation(
                                        xdumpC[:, :], wv1[:, :], AF.Abs,
                                        accum_out=PARTS[:, j:j + 1]), "sAbs", 1))
            return emit
        abs_work = [_mk_abs(j) for j in range(32)]

        ln1 = emit_ln([[]] * ntt, [[]] * ntt, [], FMA_A)
        if DEBUG_DUMP:
            d_fma = nc.dram_tensor("d_fma", [128, 8 * tpc], BF16,
                                   kind="ExternalOutput").ap()
            d_st = nc.dram_tensor("d_st", [128, 8 * ntt], F32,
                                  kind="ExternalOutput").ap()
            B.op("s", [("sLnEv", B.get("sLnEv"))],
                 lambda e: e.dma_start(out=d_fma, in_=FMA_A[:, :]), "sOut", 16)

            def st_fn(e):
                for k, tns in enumerate([SUMX, SUMQ, MEAN, MSQ, VAR, SD, RST]):
                    e.dma_start(out=d_st[:, ntt * k: ntt * (k + 1)],
                                in_=tns[:, :]).then_inc(S["sOut"], 16)
                return None
            B.op("s", [("sLnAp", B.get("sLnAp"))], st_fn)
            B.bump("sOut", 16 * 7)
        bufs1 = (U1a, V1a, U2a, V2a, sPa, sHa, sQ2a, FMA_A)
        ffn1 = emit_ffn(nch1, ln1, bufs1, [("sW", W_ALL), ("sScl", scl1)],
                        is_last=(DEBUG_STAGE == 1), extra_c=abs_work)
        if DEBUG_DUMP:
            d_p = nc.dram_tensor("d_p", [128, 2 * nch1], BF16,
                                 kind="ExternalOutput").ap()
            d_h = nc.dram_tensor("d_h", [128, 32 * nch1], BF16,
                                 kind="ExternalOutput").ap()
            d_q2 = nc.dram_tensor("d_q2", [128, 2 * nch1], BF16,
                                  kind="ExternalOutput").ap()
            B.op("s", [("sFpe", B.get("sFpe"))],
                 lambda e: e.dma_start(out=d_p, in_=sPa[:, :]), "sOut", 16)
            B.op("s", [("sFg", B.get("sFg"))],
                 lambda e: e.dma_start(out=d_h, in_=sHa[:, :]), "sOut", 16)
            B.op("s", [("sFq2e", B.get("sFq2e"))],
                 lambda e: e.dma_start(out=d_q2, in_=sQ2a[:, :]), "sOut", 16)

        if DEBUG_STAGE == 1:
            finish("s1")
            st_ffn1.close()
            st_fma.close()
            return nc

        FFN1_OV = B.get("sFoV")
        FFN1_Q2E = B.get("sFq2e")
        FFN1_G = B.get("sFg")
        FFN1_FP = B.get("sFp")

        gate = [[("sFoV", ffn1["ov_tile"][tt])] for tt in range(ntt)]
        ln2 = emit_ln(gate, gate, [("sFp", FFN1_FP)], FMA_A)
        LN2_EV_ = B.get("sLnEv")

        _emit(nc, blk, B, S)
        st_ffn1.close()
        st_quant = ExitStack()
        wave = sbuf("wave", [128, 2 * DM], F32, st_quant)
        Q32 = sbuf("Q32", [128, 2 * DM], F32, st_quant)
        WQ = sbuf("WQ", [128, 2 * DM], BF16, st_quant)
        wattnT = sbuf("wattnT", [128, 24576], BF16, st_quant)

        for fn_ in abs_work:
            fn_()
        abs_work.clear()
        B.op("v", [("sAbs", abs_rec[-1])], lambda e: e.tensor_reduce(
            PRED[:, 0:1], PARTS[:, 0:24], mybir.AxisListType.X, Alu.add))
        qs1 = B.op("v", [], lambda e: e.tensor_reduce(
            PRED[:, 1:2], PARTS[:, 24:32], mybir.AxisListType.X, Alu.add), "sQs", 1)
        qt1 = B.op("t", [("sQs", qs1), ("sW", W_ALL), ("sFq2e", FFN1_Q2E),
                         ("sFoV", FFN1_OV)],
                   lambda e: (e.matmul(pSS[:, 0:1], onesf[:, :], PRED[:, 0:1],
                                       start=True, stop=True),
                              e.matmul(pSS[:, 512:513], onesf[:, :], PRED[:, 1:2],
                                       start=True, stop=True))[-1], "sQt", 1)
        B.op("v", [("sQt", qt1)], lambda e: e.tensor_scalar(
            scaleA[:, :], pSS[:, 0:1], 1.0 / (3072 * 1024), 1e-8, Alu.mult, Alu.max))
        B.op("v", [], lambda e: e.tensor_scalar(
            scaleP[:, :], pSS[:, 512:513], 1.0 / (1024 * 1024), 1e-8, Alu.mult, Alu.max))
        B.op("v", [], lambda e: e.tensor_copy(xdumpV[:, :], xdumpV[:, :]))
        B.op("v", [], lambda e: e.reciprocal(iscaleA[:, :], scaleA[:, :]))
        B.op("v", [], lambda e: e.tensor_copy(xdumpV[:, :], xdumpV[:, :]))
        qs2 = B.op("v", [], lambda e: e.reciprocal(iscaleP[:, :], scaleP[:, :]),
                   "sQs", 1)

        c1_rec, c3_rec, tr_rec, ev_rec = [], [], [], []

        def q_c1(j):
            par = j % 2
            isc = iscaleA if j < 24 else iscaleP
            w = [("sQl", 16 * (32 + j // 2 + 1)), ("sQs", qs2)]
            c1_rec.append(B.op("v", w, lambda e, par=par, isc=isc:
                               e.tensor_scalar(
                                   Q32[:, DM * par: DM * (par + 1)],
                                   wave[:, DM * par: DM * (par + 1)],
                                   isc[:, :], 2.25, Alu.mult, Alu.min), "sQv1", 1))

        def q_tail(j):
            par = j % 2
            sc_ = scaleA if j < 24 else scaleP
            B.op("v", [], lambda e, par=par: e.tensor_scalar(
                Q32[:, DM * par: DM * (par + 1)],
                Q32[:, DM * par: DM * (par + 1)],
                -2.25, MAGIC, Alu.max, Alu.add))
            B.op("v", [], lambda e: e.tensor_copy(xdumpV[:, :], xdumpV[:, :]))
            w = [("sQtr", tr_rec[j - 2])] if j >= 2 else []
            c3_rec.append(B.op("v", w, lambda e, par=par, sc_=sc_: e.tensor_scalar(
                WQ[:, DM * par: DM * (par + 1)],
                Q32[:, DM * par: DM * (par + 1)],
                -MAGIC, sc_[:, :], Alu.add, Alu.mult), "sQv3", 1))

            def tr_fn(e, par=par):
                for cc in range(8):
                    ins = e.transpose(
                        pt[par][:, 128 * cc:128 * (cc + 1)],
                        WQ[:, DM * par + 128 * cc: DM * par + 128 * (cc + 1)],
                        idb[:, :])
                return ins
            w = [("sQv3", c3_rec[j]), ("sLnEv", LN2_EV_)]
            if j >= 2:
                w.append(("sQe", ev_rec[j - 2]))
            tr_rec.append(B.op("t", w, tr_fn, "sQtr", 1))

            def ev_fn(e, j=j, par=par):
                if j < 24:
                    dst = wattnT[:, :].rearrange("p (cc o) -> p cc o", o=3072)[
                        :, :, 128 * j:128 * (j + 1)]
                else:
                    dst = wprojT[:, :].rearrange("p (cc o) -> p cc o", o=1024)[
                        :, :, 128 * (j - 24):128 * (j - 23)]
                src_ = pt[par][:, :].rearrange("p (cc r) -> p cc r", r=128)
                return e.activation(dst, src_, AF.Copy)
            ev_rec.append(B.op("c", [("sQtr", tr_rec[j])], ev_fn, "sQe", 1))

        def wpair_src(p):
            j = 2 * p
            if j < 24 and j + 1 < 24:
                return w_attn[128 * j:128 * (j + 2), :]
            return w_proj[128 * (j - 24):128 * (j - 22), :]

        for p in range(16):
            w = [("sQs", qs2)] if p == 0 else [("sQv1", c1_rec[2 * p - 1])]
            B.op("s" if p % 2 == 0 else "g", w, lambda e, p=p: e.dma_start(
                out=wave[:, :].rearrange("q (j c) -> q j c", c=DM),
                in_=wpair_src(p).rearrange("(j q) c -> q j c", q=128)),
                "sQl", 16)
            if p >= 1:
                q_tail(2 * p - 2)
            q_c1(2 * p)
            if p >= 1:
                q_tail(2 * p - 1)
            q_c1(2 * p + 1)
        q_tail(30)
        q_tail(31)
        QE_ALL = B.get("sQe")

        if DEBUG_DUMP2:
            d_wat = nc.dram_tensor("d_wat", [128, 24576], BF16,
                                   kind="ExternalOutput").ap()
            d_wpt = nc.dram_tensor("d_wpt", [128, 8192], BF16,
                                   kind="ExternalOutput").ap()
            B.op("s", [("sQe", B.get("sQe"))],
                 lambda e: e.dma_start(out=d_wat, in_=wattnT[:, :]), "sOut", 16)
            B.op("s", [("sQe", B.get("sQe"))],
                 lambda e: e.dma_start(out=d_wpt, in_=wprojT[:, :]), "sOut", 16)

        st_qkv = ExitStack()
        qkstage = sbuf("qkstage", [128, 2 * tpc], BF16, st_qkv)
        vstage = sbuf("vstage", [128, 2 * DM], BF16, st_qkv)
        hp = tpc // 2
        nqk_h = min(512, hp)
        nchq_h = hp // nqk_h
        war_ffn1 = [("sFq2e", FFN1_Q2E), ("sFg", FFN1_G), ("sFoV", FFN1_OV)]
        LN2_EV = LN2_EV_
        LN2EVL = ln2["ev"]

        def vflat_in(a2a):
            return (a2a.rearrange("(j s) c -> j s c", j=NCORES)[:, 256:384, :]
                    .rearrange("j s c -> j (s c)")
                    .rearrange("j (t d) -> t j d", d=128))

        qv_rec, qve_rec, qvd_rec = [], [], []
        qk_rec, qke_rec = [], []
        qkd_rec = {}
        iv = [0]
        iq = [0]

        def emit_v_tile(tl, half):
            for hf2 in range(2):
                i = iv[0]
                par = 2 + (i % 2)
                w = [("sQe", ev_rec[19 + 4 * hf2]), ("sLnEv", LN2EVL[tl])] \
                    + war_ffn1
                if i >= 2:
                    w.append(("sQVe", qve_rec[i - 2]))

                def qv_fn(e, tl=tl, hf2=hf2, par=par):
                    for cc in range(8):
                        ins = e.matmul(
                            PB[par][:, 0:512],
                            FMA_A[:, tpc * cc + 128 * tl: tpc * cc + 128 * (tl + 1)],
                            wattnT[:, 3072 * cc + 2048 + 512 * hf2:
                                   3072 * cc + 2048 + 512 * (hf2 + 1)],
                            start=(cc == 0), stop=(cc == 7))
                    return ins
                qv_rec.append(B.op("t", w, qv_fn, "sQV", 1))
                w = [("sQV", qv_rec[i])]
                if tl >= 2 and hf2 == 0:
                    w.append(("sQVd", 16 * (tl - 1)))
                qve_rec.append(B.op("c", w, lambda e, tl=tl, hf2=hf2, par=par:
                                    e.activation(
                                        vstage[:, DM * (tl % 2) + 512 * hf2:
                                               DM * (tl % 2) + 512 * (hf2 + 1)],
                                        PB[par][:, 0:512], AF.Copy), "sQVe", 1))
                iv[0] += 1
            rb = B.op("v", [("sQVe", qve_rec[-1])],
                      lambda e, tl=tl: e.tensor_reduce(
                          RDBK[:, 1:2], vstage[:, DM * (tl % 2): DM * (tl % 2 + 1)],
                          mybir.AxisListType.X, Alu.add), "sQVe", 1)
            w = [("sQVe", rb)]
            if half == 1:
                w.append(("sCC1", 1))
            a2a = a2a1_in if half == 0 else a2a2_in
            tl2 = tl % (ntt // 2)

            def vd_fn(e, a2a=a2a, tl=tl, tl2=tl2):
                dst = vflat_in(a2a)[128 * tl2:128 * (tl2 + 1), :, :]
                return e.dma_start(
                    out=dst,
                    in_=vstage[:, DM * (tl % 2): DM * (tl % 2 + 1)].rearrange(
                        "p (j d) -> p j d", d=128))
            qvd_rec.append(B.op("s", w, vd_fn, "sQVd", 16))

        def emit_qk_tile(ot, half):
            for nq in range(nchq_h):
                i = iq[0]
                par = i % 2
                c0 = hp * half + nqk_h * nq
                w = [("sQe", ev_rec[ot]),
                     ("sLnEv", LN2EVL[(c0 + nqk_h) // 128 - 1])] + war_ffn1
                if i >= 2:
                    w.append(("sQKe", qke_rec[i - 2]))

                def qk_fn(e, ot=ot, c0=c0, par=par):
                    for cc in range(8):
                        ins = e.matmul(
                            PB[par][:, 0:nqk_h],
                            wattnT[:, 3072 * cc + 128 * ot: 3072 * cc + 128 * (ot + 1)],
                            FMA_A[:, tpc * cc + c0: tpc * cc + c0 + nqk_h],
                            start=(cc == 0), stop=(cc == 7))
                    return ins
                qk_rec.append(B.op("t", w, qk_fn, "sQK", 1))
                w = [("sQK", qk_rec[i])]
                if (ot - 2, half) in qkd_rec and nq == 0:
                    w.append(("sQKd", qkd_rec[(ot - 2, half)]))
                elif half == 1 and ot < 2 and nq == 0 and (14 + ot, 0) in qkd_rec:
                    w.append(("sQKd", qkd_rec[(14 + ot, 0)]))

                def qke_fn(e, ot=ot, half=half, nq=nq, par=par):
                    sl = tpc * (ot % 2) + hp * half + nqk_h * nq
                    return e.activation(qkstage[:, sl: sl + nqk_h],
                                        PB[par][:, 0:nqk_h], AF.Copy)
                qke_rec.append(B.op("c", w, qke_fn, "sQKe", 1))
                iq[0] += 1
            rb = B.op("v", [("sQKe", qke_rec[-1])],
                      lambda e, ot=ot, half=half: e.tensor_reduce(
                          RDBK[:, 0:1],
                          qkstage[:, tpc * (ot % 2) + hp * half:
                                  tpc * (ot % 2) + hp * half + hp],
                          mybir.AxisListType.X, Alu.add), "sQKe", 1)
            w = [("sQKe", rb)]
            if half == 1:
                w.append(("sCC1", 1))
            a2a = a2a1_in if half == 0 else a2a2_in
            j = ot % 8
            r0 = 384 * j + 128 * (ot // 8)
            qkd_rec[(ot, half)] = B.op(
                "s", w, lambda e, a2a=a2a, r0=r0, ot=ot, half=half: e.dma_start(
                    out=a2a[r0:r0 + 128, :],
                    in_=qkstage[:, tpc * (ot % 2) + hp * half:
                                tpc * (ot % 2) + hp * half + hp]),
                "sQKd", 16)

        for ot in range(16):
            emit_qk_tile(ot, 0)
        for tl in range(ntt // 2):
            emit_v_tile(tl, 0)
        B.op("g", [("sQKd", B.get("sQKd")), ("sQVd", B.get("sQVd"))],
             lambda e: _a2a(e, a2a1_in, a2a1_out), "sCC1", 1)
        for ot in range(16):
            emit_qk_tile(ot, 1)
        for tl in range(ntt // 2, ntt):
            emit_v_tile(tl, 1)
        B.op("g", [("sQKd", B.get("sQKd")), ("sQVd", B.get("sQVd")),
                   ("sCC1", 1)],
             lambda e: _a2a(e, a2a2_in, a2a2_out), "sCC2", 1)
        QKE_ALL = B.get("sQKe")
        QVE_ALL = B.get("sQVe")

        _emit(nc, blk, B, S)
        st_ptA.close()
        st_qkv.close()
        st_quant.close()
        st_fma.close()
        st_att = ExitStack()
        nkbh = tb // 128
        sQKV = sbuf("sQKV", [128, 3 * tb], BF16, st_att)
        sVaug = sbuf("sVaug", [128, 130 * nkbh], BF16, st_att)
        sPP = sbuf("sPP", [128, 4096], BF16, st_att)
        rdsb = sbuf("rdsb", [128, 1024], BF16, st_att)
        tLn = sbuf("tLn", [128, 1024], F32, st_att)
        rnsb = sbuf("rnsb", [128, 1024], BF16, st_att)
        SEL = sbuf("SEL", [128, 128], BF16, st_att)
        yh0 = sbuf("yh0", [128, 1024], BF16, st_att)
        yh1 = sbuf("yh1", [128, 1024], BF16, st_att)
        pnorm = st_att.enter_context(nc.psum_tensor("pnorm", [128, 1024], F32))
        sQ = sQKV[:, 0:tb]
        sK = sQKV[:, tb:2 * tb]
        sV = sQKV[:, 2 * tb:3 * tb]

        B.op("g", [("sCC1", 1)], lambda e: e.memset(sVaug[:, :], 1.0), "sVi", 1)
        B.op("g", [], lambda e: e.memset(SEL[:, :], 0.0), "sVi", 1)
        B.op("g", [], lambda e: e.memset(SEL[64:65, :], 1.0), "sVi", 1)
        B.op("g", [], lambda e: e.memset(rdsb[:, :], 0.0), "sVi", 1)
        VI_ALL = B.get("sVi")

        def emit_qkload(u, half, waits):
            first = True
            for Bq in range(4):
                Bg = Bq + 4 * half
                gout = a2a1_out if half == 0 else a2a2_out
                r0 = 384 * (4 * u + Bq)
                w = waits if first else []
                first = False
                B.op("c", w, lambda e, gout=gout, r0=r0, Bg=Bg: e.dma_start(
                    out=sQ[:, hp * Bg: hp * (Bg + 1)],
                    in_=gout[r0:r0 + 128, :]), "sBl", 16)
                B.op("c", [], lambda e, gout=gout, r0=r0, Bg=Bg: e.dma_start(
                    out=sK[:, hp * Bg: hp * (Bg + 1)],
                    in_=gout[r0 + 128:r0 + 256, :]), "sBl", 16)
            return B.get("sBl")

        def emit_vload(u, half, waits):
            engs = ["g", "c", "g", "c"]
            for Bq in range(4):
                Bg = Bq + 4 * half
                gout = a2a1_out if half == 0 else a2a2_out
                r0 = 384 * (4 * u + Bq) + 256

                def fn(e, gout=gout, r0=r0, Bg=Bg):
                    src = (gout[r0:r0 + 128, :]
                           .rearrange("r c -> (r c)")
                           .rearrange("(kb q d) -> q kb d", q=128, d=128))
                    return e.dma_start(
                        out=sV[:, hp * Bg: hp * (Bg + 1)].rearrange(
                            "q (kb d) -> q kb d", d=128),
                        in_=src)
                B.op(engs[Bq], list(waits), fn, "sBv", 16)
            return B.get("sBv")

        nkb_half = nkbh // 2

        def emit_vaug(u, half, bv_snap, av_snap):
            kb0 = nkb_half * half
            c0 = 4 * hp * half

            def src3(e_cols):
                return sV[:, c0:c0 + 4 * hp].rearrange(
                    "p (kb c) -> p kb c", c=128)[:, :, e_cols[0]:e_cols[1]]
            w = [("sBv", bv_snap), ("sVi", VI_ALL)]
            if av_snap:
                w.append(("sAv", av_snap))
            B.op("v", w, lambda e, kb0=kb0: e.tensor_copy(
                sVaug[:, :].rearrange("p (kb c) -> p kb c", c=130)[
                    :, kb0:kb0 + nkb_half, 0:64], src3((0, 64))))
            return B.op("v", [], lambda e, kb0=kb0: e.tensor_copy(
                sVaug[:, :].rearrange("p (kb c) -> p kb c", c=130)[
                    :, kb0:kb0 + nkb_half, 65:129], src3((64, 128))), "sVa", 1)

        ex_rec = {}
        g = 0
        yn_last = 0
        yo_rec = {}
        pend_mm = []
        av_pend = []
        for u in range(2):
            u0_sc = B.get("sSc")
            u0_av = g
            u0_ex = B.get("sEx")
            u0_va = B.get("sVa")
            sbl = {0: emit_qkload(u, 0, [("sCC1", 1), ("sSc", u0_sc),
                                         ("sEx", u0_ex)])}
            sbv = {0: emit_vload(u, 0, [("sCC1", 1), ("sVa", u0_va)])}
            va = {0: emit_vaug(u, 0, sbv[0], u0_av)}
            for qc in range(nqc):
                if qc == nqc // 2:
                    sbl[1] = emit_qkload(u, 1, [("sCC2", 1)])
                    sbv[1] = emit_vload(u, 1, [("sCC2", 1)])
                    va[1] = emit_vaug(u, 1, sbv[1], u0_av)
                hh = 0 if qc < nqc // 2 else 1
                nkb = 4 * qc + 4
                for kb in range(nkb):
                    diag = kb >= 4 * qc
                    qlo = 128 * (kb - 4 * qc) if diag else 0
                    par = g % 2
                    slot = g % 4
                    q0 = 512 * qc + qlo
                    n = 512 - qlo
                    w = [("sBl", sbl[hh])]
                    if g >= 2:
                        w.append(("sEx", ex_rec[g - 2]))
                    if g == 0:
                        w += [("sQKe", QKE_ALL), ("sQVe", QVE_ALL)]

                    def sc_fn(e, kb=kb, par=par, qlo=qlo, q0=q0, n=n):
                        e.matmul(pSS[:, 1024 * par + qlo: 1024 * par + 512],
                                 sK[0:64, 128 * kb:128 * (kb + 1)],
                                 sQ[0:64, q0:q0 + n],
                                 start=True, stop=True, tile_position=(0, 0))
                        return e.matmul(pSS[:, 1024 * par + 512 + qlo:
                                            1024 * par + 1024],
                                        sK[64:128, 128 * kb:128 * (kb + 1)],
                                        sQ[64:128, q0:q0 + n],
                                        start=True, stop=True,
                                        tile_position=(64, 0))
                    sc = B.op("t", w, sc_fn, "sSc", 1)
                    uq = u * nqc + qc
                    if pend_mm and kb == 2:
                        while av_pend and av_pend[0][0] < uq:
                            av_pend.pop(0)[1]()
                        for fn_ in pend_mm:
                            fn_()
                        pend_mm = []
                    while len(av_pend) >= 3:
                        av_pend.pop(0)[1]()
                    w = [("sSc", sc)]
                    if g >= 4:
                        w.append(("sAv", g - 3))
                    ex_rec[g] = B.op("c", w, lambda e, par=par, slot=slot:
                                     e.activation(
                                         sPP[:, 1024 * slot:1024 * (slot + 1)],
                                         pSS[:, 1024 * par:1024 * (par + 1)],
                                         AF.Exp, scale=0.125,
                                         bias=nbias[:, 0:1]), "sEx", 1)
                    mk = None
                    if diag:
                        def mk_fn(e, slot=slot, qlo=qlo):
                            e.tensor_tensor(
                                sPP[:, 1024 * slot + qlo:
                                    1024 * slot + qlo + 128],
                                sPP[:, 1024 * slot + qlo:
                                    1024 * slot + qlo + 128],
                                tri[:, :], Alu.mult)
                            return e.tensor_tensor(
                                sPP[:, 1024 * slot + 512 + qlo:
                                    1024 * slot + 512 + qlo + 128],
                                sPP[:, 1024 * slot + 512 + qlo:
                                    1024 * slot + 512 + qlo + 128],
                                tri[:, :], Alu.mult)
                        mk = B.op("v", [("sEx", ex_rec[g])], mk_fn, "sMk", 1)

                    def emit_av(g=g, kb=kb, slot=slot, qlo=qlo, mk=mk,
                                diag=diag, nkb=nkb, va_snap=va[hh], u=u, qc=qc,
                                yn_snap=yn_last):
                        w = [("sMk", mk)] if diag else [("sEx", ex_rec[g])]
                        w.append(("sVa", va_snap))
                        if kb == 0:
                            w.append(("sYn", yn_snap))
                            if u == 0 and qc == 0:
                                w.append(("sFoV", FFN1_OV))

                        def fn(e, kb=kb, slot=slot, qlo=qlo,
                               st=(kb == 0), sp=(kb == nkb - 1)):
                            p1 = sPP[:, 1024 * slot + qlo: 1024 * slot + 512]
                            p2 = sPP[:, 1024 * slot + 512 + qlo:
                                     1024 * slot + 1024]
                            e.matmul(pb4[0:65, qlo:512],
                                     sVaug[:, 130 * kb:130 * kb + 65], p1,
                                     start=st, stop=sp, tile_position=(0, 0))
                            return e.matmul(pb5[0:65, qlo:512],
                                            sVaug[:, 130 * kb + 65:
                                                  130 * (kb + 1)], p2,
                                            start=st, stop=sp,
                                            tile_position=(0, 0))
                        B.op("t", w, fn, "sAv", 1)
                    av_pend.append((u * nqc + qc, emit_av))
                    g += 1
                qpar = qc % 2

                bc_target = B.get("sBc") + 2
                nmv_target = B.get("sNmV") + 1

                def mk_norm(g_end=g, nmv=nmv_target):
                    B.op("c", [("sAv", g_end), ("sVi", VI_ALL)],
                         lambda e: e.activation(tLn[64:65, 0:1024],
                                                pb45[64:65, 0:1024], AF.Ln))

                    def rc_fn(e):
                        with nc.allow_low_precision(reason="recip as bf16 mm operand"):
                            return e.activation(rdsb[64:65, 0:1024],
                                                tLn[64:65, 0:1024],
                                                AF.Exp, scale=-1.0)
                    B.op("c", [], rc_fn, "sNmV", 1)
                    B.op("t", [("sNmV", nmv), ("sVi", VI_ALL)],
                         lambda e: e.matmul(pnorm[0:64, 0:512],
                                            SEL[64:96, 0:64],
                                            rdsb[64:96, 0:512],
                                            start=True, stop=True,
                                            tile_position=(64, 0)), "sBc", 1)
                    B.op("t", [], lambda e: e.matmul(pnorm[0:64, 512:1024],
                                                     SEL[64:96, 0:64],
                                                     rdsb[64:96, 512:1024],
                                                     start=True, stop=True,
                                                     tile_position=(64, 0)),
                         "sBc", 1)
                pend_mm = [mk_norm]
                B.op("v", [("sBc", bc_target - 1)], lambda e: e.tensor_copy(
                    rnsb[0:64, 0:512], pnorm[0:64, 0:512]), "sNmC", 1)
                w = [("sYo", yo_rec.get(qpar, 0))]
                B.op("v", w, lambda e, qpar=qpar: e.tensor_tensor(
                    yh0[0:64, 512 * qpar:512 * qpar + 512],
                    pb4[0:64, 0:512], rnsb[0:64, 0:512], Alu.mult), "sYn", 1)
                B.op("v", [("sBc", bc_target)], lambda e: e.tensor_copy(
                    rnsb[0:64, 512:1024], pnorm[0:64, 512:1024]), "sNmC", 1)
                yn_last = B.op("v", [], lambda e, qpar=qpar: e.tensor_tensor(
                    yh1[0:64, 512 * qpar:512 * qpar + 512],
                    pb5[0:64, 0:512], rnsb[0:64, 512:1024], Alu.mult),
                    "sYn", 1)
                q0 = 512 * qc
                Bt = q0 // hp
                j = 4 * u + (Bt if Bt < 4 else Bt - 4)
                cdst = (0 if Bt < 4 else hp) + q0 - hp * Bt
                w = [("sYn", yn_last), ("sCC2", 1)]
                B.op("s", w, lambda e, j=j, cdst=cdst, qpar=qpar: e.dma_start(
                    out=a2ac_in[128 * j:128 * j + 64, cdst:cdst + 512],
                    in_=yh0[0:64, 512 * qpar:512 * qpar + 512]), "sYo", 16)
                yo_rec[qpar] = B.op(
                    "s", [], lambda e, j=j, cdst=cdst, qpar=qpar: e.dma_start(
                        out=a2ac_in[128 * j + 64:128 * (j + 1), cdst:cdst + 512],
                        in_=yh1[0:64, 512 * qpar:512 * qpar + 512]), "sYo", 16)
        while av_pend:
            av_pend.pop(0)[1]()
        for fn_ in pend_mm:
            fn_()
        pend_mm = []
        B.op("g", [("sYo", B.get("sYo")), ("sCC2", 1)],
             lambda e: _a2a(e, a2ac_in, a2ac_out), "sCCy", 1)

        ATT_EX = B.get("sEx")
        ATT_NMV = B.get("sNmV")
        ATT_YO = B.get("sYo")
        ATT_YN = B.get("sYn")

        _emit(nc, blk, B, S)
        st_att.close()
        st_c = ExitStack()
        pt[0] = st_c.enter_context(nc.psum_tensor("ptC0", [128, 1024], BF16))
        pt[1] = st_c.enter_context(nc.psum_tensor("ptC1", [128, 1024], BF16))
        FMA_C = sbuf("FMA_C", [128, 8 * tpc], BF16, st_c)
        U1c = sbuf("U1c", [128, 8 * RANK], BF16, st_c)
        V1c = sbuf("V1c", [128, 2 * DFF], BF16, st_c)
        U2c = sbuf("U2c", [128, 32 * RANK], BF16, st_c)
        V2c = sbuf("V2c", [128, 2 * DM], BF16, st_c)
        sHc = sbuf("sHc", [128, 32 * nch2], BF16, st_c)
        sPc = sbuf("sPc", [128, 2 * nch2], BF16, st_c)
        sQ2c = sbuf("sQ2c", [128, 2 * nch2], BF16, st_c)

        for cc in range(8):
            B.op("g", [("sCCy", 1)] if cc == 0 else [],
                 lambda e, cc=cc: e.dma_start(
                     out=FMA_C[:, tpc * cc: tpc * (cc + 1)],
                     in_=a2ac_out[128 * cc:128 * (cc + 1), :]), "sCy", 16)
        CY_ALL = B.get("sCy")

        if DEBUG_DUMP2:
            d_fmc = nc.dram_tensor("d_fmc", [128, 8 * tpc], BF16,
                                   kind="ExternalOutput").ap()
            B.op("s", [("sCy", CY_ALL)],
                 lambda e: e.dma_start(out=d_fmc, in_=FMA_C[:, :]), "sOut", 16)
        gate_w2 = [("sNmV", ATT_NMV), ("sYo", ATT_YO), ("sYn", ATT_YN)]
        for dst, src in ffnw_dmas(U1c, V1c, U2c, V2c):
            B.op("g", gate_w2, lambda e, d=dst, s_=src: e.dma_start(out=d, in_=s_),
                 "sFw2", 16)
            gate_w2 = []
        FW2_ALL = B.get("sFw2")
        scl2 = B.op("v", [("sFw2", FW2_ALL), ("sNmV", ATT_NMV)],
                    lambda e: e.tensor_scalar_mul(V2c[:, :], V2c[:, :], 0.5),
                    "sScl", 1)

        pj_rec, pjv_rec = [], []
        pjv_tile = [None] * ntt
        i = 0
        for tl in range(ntt):
            for half in range(2):
                par = i % 2
                w = [("sCy", CY_ALL), ("sQe", QE_ALL)]
                if i < 2:
                    w += [("sEx", ATT_EX), ("sNmV", ATT_NMV)]
                else:
                    w.append(("sPjV", pjv_rec[i - 2]))

                def fn(e, tl=tl, half=half, par=par):
                    for cc in range(8):
                        ins = e.matmul(
                            PB[par][:, 0:512],
                            FMA_C[:, tpc * cc + 128 * tl: tpc * cc + 128 * (tl + 1)],
                            wprojT[:, 1024 * cc + 512 * half:
                                   1024 * cc + 512 * (half + 1)],
                            start=(cc == 0), stop=(cc == 7))
                    return ins
                pj_rec.append(B.op("t", w, fn, "sPjT", 1))
                pjv_rec.append(B.op("v", [("sPjT", pj_rec[i])],
                                   lambda e, tl=tl, half=half, par=par:
                                   e.scalar_tensor_tensor(
                                       x_sl(tl, 512 * half, 512), PB[par][:, 0:512],
                                       1.0, x_sl(tl, 512 * half, 512),
                                       Alu.mult, Alu.add), "sPjV", 1))
                if half == 1:
                    pjv_tile[tl] = pjv_rec[i]
                i += 1
        PJT_ALL = B.get("sPjT")
        PJV_ALL = B.get("sPjV")

        if DEBUG_STAGE == 2:
            for tt in range(ntt):
                B.op("s", [("sPjV", pjv_tile[tt])],
                     lambda e, tt=tt: e.dma_start(
                         out=out[128 * tt:128 * (tt + 1), :], in_=x_sl(tt)),
                     "sOut", 16)
            finish("s2")
            st_c.close()
            return nc

        gate3 = [[("sPjV", pjv_tile[tt])] for tt in range(ntt)]
        ln3 = emit_ln(gate3, gate3, [("sPjT", PJT_ALL)], FMA_C)
        bufs2 = (U1c, V1c, U2c, V2c, sPc, sHc, sQ2c, FMA_C)
        wwaits2 = [("sFw2", FW2_ALL), ("sScl", scl2), ("sPjV", PJV_ALL),
                   ("sEx", ATT_EX), ("sNmV", ATT_NMV), ("sYn", ATT_YN)]
        emit_ffn(nch2, ln3, bufs2, wwaits2, is_last=True)

        finish("s3")
        st_c.close()
    return nc


def _emit(nc, blk, B, S):
    eng_map = {"s": blk.sync, "g": blk.gpsimd, "t": blk.tensor,
               "v": blk.vector, "c": blk.scalar}
    for ename, deco in eng_map.items():
        ops = [o for o in B.ops if o.eng == ename]
        if not ops:
            continue

        def body(e, ops=ops):
            last = {}
            for o in ops:
                for (s, v) in o.waits:
                    if v > last.get(s, 0):
                        e.wait_ge(S[s], v)
                        last[s] = v
                ins = o.fn(e)
                if o.inc is not None:
                    assert ins is not None, f"inc op returned None ({ename})"
                    ins.then_inc(S[o.inc[0]], o.inc[1])
        deco(body)
    B.ops = []


_CACHE = {}


def _get_nc(tpc):
    if tpc not in _CACHE:
        _CACHE[tpc] = build_program(tpc)
    return _CACHE[tpc]


def make_in_maps(inputs, tpc):
    x = np.ascontiguousarray(np.asarray(inputs["L0_attn_x"], dtype=np.float32))
    Bn, T, C = x.shape
    xf = x.reshape(Bn * T, C)
    tri = np.triu(np.ones((128, 128), dtype=np.float32)).astype(nbf)
    ident = np.eye(128, dtype=np.float32).astype(nbf)
    identf = np.eye(128, dtype=np.float32)
    shared = dict(
        w_attn=np.ascontiguousarray(np.asarray(inputs["L0_attn_w_attn"], np.float32)),
        w_proj=np.ascontiguousarray(np.asarray(inputs["L0_attn_w_proj"], np.float32)),
        U1=np.ascontiguousarray(np.asarray(inputs["L0_ffn_U1"], np.float32)),
        V1=np.ascontiguousarray(np.asarray(inputs["L0_ffn_V1"], np.float32)),
        b1=np.ascontiguousarray(np.asarray(inputs["L0_ffn_b1"], np.float32)),
        U2=np.ascontiguousarray(np.asarray(inputs["L0_ffn_U2"], np.float32)),
        V2=np.ascontiguousarray(np.asarray(inputs["L0_ffn_V2"], np.float32)),
        b2=np.ascontiguousarray(np.asarray(inputs["L0_ffn_b2"], np.float32)),
        tri=tri, ident=ident, identf=identf,
    )
    hp = tpc // 2
    in_maps = []
    for c in range(NCORES):
        b, j = c // 4, c % 4
        base = b * (4 * tpc)
        rows = np.concatenate([
            xf[base + hp * j: base + hp * (j + 1)],
            xf[base + hp * (j + 4): base + hp * (j + 5)],
        ], axis=0)
        m = dict(shared)
        m["x_shard"] = np.ascontiguousarray(rows)
        in_maps.append(m)
    return in_maps


def kernel(**inputs):
    x = np.asarray(inputs["L0_attn_x"])
    Bn, T, C = x.shape
    tpc = (Bn * T) // NCORES
    hp = tpc // 2
    in_maps = make_in_maps(inputs, tpc)
    nc = _get_nc(tpc)
    res = run_bass_kernel_spmd(nc, in_maps, core_ids=list(range(NCORES)))
    out = np.empty((Bn * T, C), dtype=np.float32)
    for c in range(NCORES):
        shard = np.asarray(res.results[c]["out_shard"])
        b, j = c // 4, c % 4
        base = b * (4 * tpc)
        out[base + hp * j: base + hp * (j + 1)] = shard[0:hp]
        out[base + hp * (j + 4): base + hp * (j + 5)] = shard[hp:tpc]
    return out.reshape(Bn, T, C)
